# version 03
# baseline (speedup 1.0000x reference)
import os
import math
import numpy as np
import ml_dtypes
from contextlib import ExitStack

import concourse.bass as bass
import concourse.mybir as mybir
from concourse.bass_utils import run_bass_kernel_spmd

BFD = ml_dtypes.bfloat16
F32 = mybir.dt.float32
BF16 = mybir.dt.bfloat16
AF = mybir.ActivationFunctionType
ALU = mybir.AluOpType
AX = mybir.AxisListType

B, S, D, H, I = 4, 4096, 2048, 16, 8192
NTO, NTA, DC, IC = 16, 32, 16, 64
TO, TA = NTO * 128, NTA * 128
NCQ, TQW = 4, 512
TT, NTT = 512, 4
EPS = 1e-5

C23 = 12582912.0
C52 = float(1.5 * 2 ** 52)
ISQHD = float(1.0 / math.sqrt(128.0))
NEG_BIG = -1.0e30

_LN4 = math.log(10000.0) / 64.0
LN4_HI = float(np.float32(int(_LN4 * 2 ** 14) / 2 ** 14))
LN4_MID = float(np.float32(_LN4 - LN4_HI))
LN4_LO = float(np.float32(_LN4 - LN4_HI - float(np.float32(LN4_MID))))
_TPI = 2.0 * math.pi
TPI_HI = float(np.float32(int(_TPI * 2 ** 11) / 2 ** 11))
TPI_MID = float(np.float32(_TPI - TPI_HI))
TPI_LO = float(np.float32(_TPI - TPI_HI - float(np.float32(TPI_MID))))
INV2PI = float(1.0 / _TPI)
PI32 = float(np.float32(math.pi))
_HPI = math.pi / 2.0
HPI_HI = float(np.float32(int(_HPI * 2 ** 12) / 2 ** 12))
HPI_MID = float(np.float32(_HPI - HPI_HI))
HPI_LO = float(np.float32(_HPI - HPI_HI - float(np.float32(HPI_MID))))

PHASES = int(os.environ.get("MK_PHASES", "6"))
P2LIM = int(os.environ.get("MK_P2LIM", "99"))
SCOUT = os.environ.get("MK_SCOUT", "") == "1"
QD = int(os.environ.get("MK_QD", "0"))
PROBES = [s for s in os.environ.get("MK_PROBE", "").split(",") if s]

ENGS = ("sync", "gpsimd", "tensor", "vector", "scalar")


class Sem:
    __slots__ = ("h", "n")

    def __init__(self, ctx, nc, name):
        self.h = ctx.enter_context(nc.semaphore(name))
        self.n = 0


class Prog:
    def __init__(self, nc, ctx):
        self.nc = nc
        self.ctx = ctx
        self.ops = {e: [] for e in ENGS}
        self.dS = Sem(ctx, nc, "dS")
        self.dG = Sem(ctx, nc, "dG")
        self.cT = Sem(ctx, nc, "cT")
        self.cV = Sem(ctx, nc, "cV")
        self.cS = Sem(ctx, nc, "cS")
        self.cG = Sem(ctx, nc, "cG")

    def em(self, eng, fn):
        self.ops[eng].append(fn)

    def wait(self, eng, sem, n=None):
        v = sem.n if n is None else n
        if v <= 0:
            return
        self.em(eng, lambda e, s=sem.h, v=v: e.wait_ge(s, v))

    def dma(self, eng, out, in_, **kw):
        sem = self.dS if eng == "sync" else self.dG
        sem.n += 16
        self.em(eng, lambda e, o=out, i=in_, s=sem.h, kw=kw:
                e.dma_start(out=o, in_=i, **kw).then_inc(s, 16))
        return sem.n

    def _op(self, eng, sem, fn, inc):
        if inc:
            sem.n += 1
            self.em(eng, lambda e, f=fn, s=sem.h: f(e).then_inc(s, 1))
        else:
            self.em(eng, lambda e, f=fn: f(e))
        return sem.n

    def top(self, fn, inc=True):
        return self._op("tensor", self.cT, fn, inc)

    def vop(self, fn, inc=True):
        return self._op("vector", self.cV, fn, inc)

    def sop(self, fn, inc=True):
        return self._op("scalar", self.cS, fn, inc)

    def gop(self, fn, inc=True):
        return self._op("gpsimd", self.cG, fn, inc)

    def vchain(self, fns):
        for fn in fns:
            n = self.vop(fn)
            self.wait("vector", self.cV, n)
        return self.cV.n


def _had(n):
    h = np.array([[1.0]])
    while h.shape[0] < n:
        h = np.block([[h, h], [h, -h]])
    return h


def build_program():
    nc = bass.Bass("TRN2", target_bir_lowering=False, debug=False,
                   detect_race_conditions=False)
    ctx = ExitStack()

    t_x = nc.dram_tensor("x", [TA, D], F32, kind="ExternalInput").ap()
    t_pos = nc.dram_tensor("pos", [128, NTA], F32, kind="ExternalInput").ap()
    t_kb = nc.dram_tensor("kbias", [128, NTA], F32, kind="ExternalInput").ap()
    t_wqkv = nc.dram_tensor("wqkv", [D, 3 * D], F32, kind="ExternalInput").ap()
    t_bqkv = nc.dram_tensor("bqkv", [1, 3 * D], F32, kind="ExternalInput").ap()
    t_wo = nc.dram_tensor("wo", [D, D], F32, kind="ExternalInput").ap()
    t_bo = nc.dram_tensor("bo", [1, D], F32, kind="ExternalInput").ap()
    t_wg = nc.dram_tensor("wg", [I, D], F32, kind="ExternalInput").ap()
    t_wu = nc.dram_tensor("wu", [I, D], F32, kind="ExternalInput").ap()
    t_wd = nc.dram_tensor("wd", [D, I], F32, kind="ExternalInput").ap()
    t_id = nc.dram_tensor("ident", [128, 128], BF16, kind="ExternalInput").ap()
    t_h128 = nc.dram_tensor("h128", [128, 128], BF16, kind="ExternalInput").ap()
    t_hic = nc.dram_tensor("hic", [IC, IC], BF16, kind="ExternalInput").ap()
    t_mask = nc.dram_tensor("masks", [128, 4 * TQW], BF16, kind="ExternalInput").ap()
    t_out = nc.dram_tensor("out", [TO, D], F32, kind="ExternalOutput").ap()

    SK = "ExternalOutput" if SCOUT else "Internal"
    t_kT = nc.dram_tensor("kT_sc", [H * 128, TA], BF16, kind=SK).ap()
    t_qT = nc.dram_tensor("qT_sc", [H * 128, TO], BF16, kind=SK).ap()
    t_v = nc.dram_tensor("v_sc", [TA, D], BF16, kind=SK).ap()
    t_x1 = nc.dram_tensor("x1_sc", [TO, D], F32, kind=SK).ap()
    t_a2qT = nc.dram_tensor("a2qT_sc", [D, TO], BF16, kind=SK).ap()
    t_wgq = nc.dram_tensor("wgq_sc", [I, D], BF16, kind="Internal").ap()
    t_wuq = nc.dram_tensor("wuq_sc", [I, D], BF16, kind="Internal").ap()
    t_wdq = nc.dram_tensor("wdq_sc", [D, I], BF16, kind="Internal").ap()
    t_Tsc = nc.dram_tensor("T_sc", [I, TT], BF16, kind="Internal").ap()
    t_a3 = nc.dram_tensor("a3_sc", [TT, I], BF16, kind="Internal").ap()
    t_nrm = nc.dram_tensor("nrm_sc", [128, 32], F32, kind="Internal").ap()
    t_nrm2 = nc.dram_tensor("nrm2_sc", [32, 1], F32, kind="Internal").ap()

    probes = {}

    def mkprobe(name, shape, dtype=F32):
        if name in PROBES:
            probes[name] = nc.dram_tensor("pb_" + name, list(shape), dtype,
                                          kind="ExternalOutput").ap()
            return probes[name]
        return None

    NBF, NF32 = 65536, 12288
    bfA = ctx.enter_context(nc.sbuf_tensor("bfA", [128, NBF], BF16)).ap()
    fA = ctx.enter_context(nc.sbuf_tensor("fA", [128, NF32], F32)).ap()
    sb_id = ctx.enter_context(nc.sbuf_tensor("sb_id", [128, 128], BF16)).ap()
    sb_h128 = ctx.enter_context(nc.sbuf_tensor("sb_h128", [128, 128], BF16)).ap()
    sb_hic = ctx.enter_context(nc.sbuf_tensor("sb_hic", [IC, IC], BF16)).ap()
    sb_mask = ctx.enter_context(nc.sbuf_tensor("sb_mask", [128, 4 * TQW], BF16)).ap()
    sb_S = ctx.enter_context(nc.sbuf_tensor("sb_S", [128, NTA * 64], BF16)).ap()
    sb_C = ctx.enter_context(nc.sbuf_tensor("sb_C", [128, NTA * 64], BF16)).ap()
    sm = ctx.enter_context(nc.sbuf_tensor("sm", [128, 256], F32)).ap()
    sm2 = ctx.enter_context(nc.sbuf_tensor("sm2", [128, 1408], F32)).ap()
    sb_Tr = ctx.enter_context(nc.sbuf_tensor("sb_Tr", [IC, 2048], BF16)).ap()

    PH, NP1 = 0, 1
    S1, SSQ, NEGM, RSTD, TPA, TPB = 8, 9, 10, 11, 12, 13
    MXC, SCL, SINV, NPS = 16, 17, 18, 19
    MG, MU, THC, NTHC = 24, 25, 26, 27
    INVF = 64
    NRMQ, NRMK, BIAS = 0, 256, 768
    HMX, POSC, KBC, MBC = 1280, 1312, 1344, 1376

    p = Prog(nc, ctx)
    dS, dG, cT, cV, cS, cG = p.dS, p.dG, p.cT, p.cV, p.cS, p.cG

    smc = lambda c: sm[:, c:c + 1]

    p.dma("sync", sb_id[:, :], t_id)
    p.dma("sync", sb_h128[:, :], t_h128)
    p.dma("sync", sb_hic[:, :], t_hic)
    p.dma("sync", sb_mask[:, :], t_mask)
    p.dma("sync", sm2[:, POSC:POSC + 32], t_pos)
    n0 = p.dma("sync", sm2[:, KBC:KBC + 32], t_kb)

    iot = fA[:, 0:64]
    p.gop(lambda e: e.iota(iot, pattern=[[1, 64]], base=0, channel_multiplier=0,
                           allow_small_or_imprecise_dtypes=True))
    p.wait("vector", cG, cG.n)
    p.wait("vector", dS, n0)

    arg = fA[:, 64:128]
    invf = sm[:, INVF:INVF + 64]
    p.vop(lambda e: e.tensor_scalar_mul(arg, iot, -LN4_HI), inc=False)
    p.vop(lambda e: e.scalar_tensor_tensor(arg, iot, -LN4_MID, arg, ALU.mult, ALU.add), inc=False)
    nv = p.vop(lambda e: e.scalar_tensor_tensor(arg, iot, -LN4_LO, arg, ALU.mult, ALU.add))
    p.wait("scalar", cV, nv)
    ns = p.sop(lambda e: e.activation(invf, arg, AF.Exp, bias=0.0, scale=1.0))
    p.wait("vector", cS, ns)

    p.vchain([
        lambda e: e.memset(smc(PH), 0.5),
        lambda e: e.tensor_scalar_add(smc(PH), smc(PH), C23),
        lambda e: e.tensor_scalar_sub(smc(PH), smc(PH), C23),
        lambda e: e.tensor_scalar_mul(smc(NP1), smc(PH), -2.0),
    ])

    for j in range(NTA):
        p.vop(lambda e, j=j: e.tensor_scalar_mul(
            fA[:, 2048 + j * 64:2048 + (j + 1) * 64], invf,
            sm2[:, POSC + j:POSC + j + 1]), inc=False)
    nv = p.vop(lambda e: e.nop())
    p.wait("vector", cV, nv)

    def sincos(dst, shift):
        a_ = fA[:, 2048:4096]
        t1 = fA[:, 4096:6144]
        t2 = fA[:, 6144:8192]
        k_ = fA[:, 8192:10240]
        if shift is not None:
            p.vop(lambda e: e.tensor_scalar_add(t1, a_, shift[0]), inc=False)
            p.vop(lambda e: e.tensor_scalar_add(t1, t1, shift[1]), inc=False)
            p.vop(lambda e: e.tensor_scalar_add(t1, t1, shift[2]), inc=False)
            src = t1
        else:
            src = a_
        nps = smc(NPS)
        p.vchain([lambda e: e.tensor_scalar_mul(nps, smc(NP1), INV2PI)])
        p.vop(lambda e, s=src: e.tensor_scalar(k_, s, INV2PI, C23, ALU.mult, ALU.add), inc=False)
        p.vop(lambda e, s=src: e.tensor_scalar(t2, s, INV2PI, C52, ALU.mult, ALU.add), inc=False)
        p.vop(lambda e: e.scalar_tensor_tensor(t2, t2, C52, k_, ALU.subtract, ALU.add), inc=False)
        p.vop(lambda e, s=src: e.scalar_tensor_tensor(k_, s, nps, t2, ALU.mult, ALU.add), inc=False)
        p.vop(lambda e: e.tensor_scalar_sub(k_, k_, C23), inc=False)
        p.vop(lambda e, s=src: e.scalar_tensor_tensor(t2, k_, -TPI_HI, s, ALU.mult, ALU.add), inc=False)
        p.vop(lambda e: e.scalar_tensor_tensor(t2, k_, -TPI_MID, t2, ALU.mult, ALU.add), inc=False)
        p.vop(lambda e: e.scalar_tensor_tensor(t2, k_, -TPI_LO, t2, ALU.mult, ALU.add), inc=False)
        nv_ = p.vop(lambda e: e.tensor_scalar(t2, t2, PI32, -PI32, ALU.min, ALU.max))
        p.wait("scalar", cV, nv_)
        ns_ = p.sop(lambda e, d=dst: e.activation(d, t2, AF.Sin, bias=0.0, scale=1.0))
        p.wait("vector", cS, ns_)

    sincos(fA[:, 10240:12288], None)
    nv = p.vop(lambda e: e.tensor_copy(sb_S[:, :], fA[:, 10240:12288]))
    p.wait("vector", cV, nv)
    sincos(fA[:, 10240:12288], (HPI_HI, HPI_MID, HPI_LO))
    nv = p.vop(lambda e: e.tensor_copy(sb_C[:, :], fA[:, 10240:12288]))
    p.wait("vector", cV, nv)
    nv = p.vop(lambda e: e.nop())
    p.wait("vector", cV, nv)

    pb = mkprobe("tables", (128, 512))
    if pb is not None:
        p.vop(lambda e: e.tensor_copy(fA[:, 0:128], sb_A[:, 0:128]), inc=False)
        p.vop(lambda e: e.tensor_copy(fA[:, 128:256], sb_B[:, 3 * 128:4 * 128]), inc=False)
        p.vop(lambda e: e.tensor_copy(fA[:, 256:320], invf), inc=False)
        nv = p.vop(lambda e: e.nop())
        p.wait("gpsimd", cV, nv)
        p.dma("gpsimd", pb, fA[:, 0:512])

    if PHASES < 2:
        return _finish_stub(nc, ctx, p, t_out, fA, probes)

    A1T, WS, QWK, KST, QST, JNK = 0, 32768, 40960, 43008, 51200, 59392
    a1T = lambda dc, tl: bfA[:, A1T + dc * 2048 + tl * 128:A1T + dc * 2048 + (tl + 1) * 128]
    w_sb = bfA[:, WS:WS + 8192].rearrange("p (c j) -> p c j", j=512)
    qkv_bf = bfA[:, QWK:QWK + 512]
    rp2 = bfA[:, QWK + 512:QWK + 1024]
    t1r = bfA[:, QWK + 1024:QWK + 1536]
    a1bf = bfA[:, JNK:JNK + 2048]
    junkb = bfA[:, JNK + 2048:JNK + 4096]
    sb_Ar = bfA[:, 63488:64000]
    sb_Br = bfA[:, 64000:64512]

    xbuf = [fA[:, 0:2048], fA[:, 2048:4096]]
    a1f = fA[:, 4096:6144]
    b_rep = fA[:, 6144:6656]
    onesr = fA[:, 6656:6784]
    brow = fA[0:1, 6784:7296]

    psP = ExitStack()
    psQ = [psP.enter_context(nc.psum_tensor(f"psQ{i}", [128, 512], F32)).ap() for i in range(2)]
    psTb = [psP.enter_context(nc.psum_tensor(f"psTb{i}", [128, 128], BF16)).ap() for i in range(2)]
    psR = psP.enter_context(nc.psum_tensor("psR", [128, 512], F32)).ap()

    nv = p.vop(lambda e: e.memset(onesr[0:1, :], 1.0))
    p.wait("tensor", cV, nv)

    def ln_stats(xap, nfeat, junk=None):
        jk = junkb if junk is None else junk
        nv_ = p.vop(lambda e, x_=xap: e.tensor_reduce(smc(S1), x_, AX.X, ALU.add))
        p.wait("scalar", cV, nv_)
        ns_ = p.sop(lambda e, x_=xap, n=nfeat, jk=jk: e.activation(
            jk[:, 0:n], x_, AF.Square, bias=0.0, scale=1.0, accum_out=smc(SSQ)))
        p.wait("vector", cS, ns_)
        p.vchain([
            lambda e: e.tensor_scalar_mul(smc(NEGM), smc(S1), -1.0 / nfeat),
            lambda e: e.tensor_tensor(smc(TPA), smc(NEGM), smc(NEGM), ALU.mult),
            lambda e: e.scalar_tensor_tensor(smc(TPB), smc(SSQ), 1.0 / nfeat, smc(TPA), ALU.mult, ALU.subtract),
            lambda e: e.tensor_scalar_add(smc(TPB), smc(TPB), EPS),
            lambda e: e.reciprocal(smc(TPA), smc(TPB)),
        ])
        p.wait("scalar", cV, cV.n)
        ns_ = p.sop(lambda e: e.activation(smc(RSTD), smc(TPA), AF.Sqrt, bias=0.0, scale=1.0))
        p.wait("vector", cS, ns_)

    n_tables_done = cV.n
    xcons = [n_tables_done, n_tables_done]
    evm = [0, 0]
    qevm = [0, 0]
    tevm = [0, 0]

    for chunk in range(2):
        for tl in range(16):
            j = chunk * 16 + tl
            bi = tl % 2
            p.wait("sync", cV, xcons[bi])
            nld = p.dma("sync", xbuf[bi], t_x[j * 128:(j + 1) * 128, :])
            p.wait("vector", dS, nld)
            ln_stats(xbuf[bi], D)
            p.vop(lambda e, b_=bi: e.tensor_scalar(a1f, xbuf[b_], smc(NEGM), smc(RSTD), ALU.add, ALU.mult), inc=False)
            nv_ = p.vop(lambda e: e.tensor_copy(a1bf, a1f))
            xcons[bi] = nv_
            p.wait("tensor", cV, nv_)
            for dc in range(DC):
                pi = dc % 2
                p.wait("tensor", cV, evm[pi])
                nt_ = p.top(lambda e, dc=dc, pi=pi: e.transpose(
                    psTb[pi][:, :], a1bf[:, dc * 128:(dc + 1) * 128], sb_id))
                p.wait("vector", cT, nt_)
                evm[pi] = p.vop(lambda e, dc=dc, tl=tl, pi=pi: e.tensor_copy(
                    a1T(dc, tl), psTb[pi][:, :]))

        pb = mkprobe(f"a1T_c{chunk}", (128, 2048))
        if pb is not None:
            nv = p.vop(lambda e: e.nop())
            p.wait("gpsimd", cV, nv)
            p.dma("gpsimd", pb, bfA[:, A1T:A1T + 2048])

        wqkv_r = t_wqkv.rearrange("(c p) o -> p c o", p=128)
        oblks = list(range(12)) if chunk == 0 else list(range(4, 12))

        wcons = 0
        psr_prev = 0
        for obi, ob in enumerate(oblks):
            p.wait("gpsimd", cT, wcons)
            nwl = p.dma("gpsimd", w_sb[:, :, :], wqkv_r[:, :, ob * 512:(ob + 1) * 512])
            if True:
                p.wait("sync", cT, psr_prev)
                nbl = p.dma("sync", brow, t_bqkv[:, ob * 512:(ob + 1) * 512])
                p.wait("tensor", dS, nbl)
                nt_ = p.top(lambda e: e.matmul(psR[:, :], onesr[0:1, :], brow, start=True, stop=True))
                psr_prev = nt_
                p.wait("vector", cT, nt_)
                nbr = p.vop(lambda e: e.tensor_copy(b_rep, psR[:, :]))
            else:
                nbr = cV.n

            is_q, is_k = ob < 4, 4 <= ob < 8
            for tl in range(16):
                j = chunk * 16 + tl
                pi = tl % 2
                p.wait("tensor", dG, nwl)
                p.wait("tensor", cV, qevm[pi])
                for dc in range(DC):
                    ntm = p.top(lambda e, dc=dc, pi=pi, tl=tl: e.matmul(
                        psQ[pi][:, :], a1T(dc, tl), w_sb[:, dc, :],
                        start=(dc == 0), stop=(dc == DC - 1)), inc=(dc == DC - 1))
                wcons = ntm
                p.wait("vector", cT, ntm)
                p.wait("vector", cV, nbr)
                nv_ = p.vop(lambda e, pi=pi: e.scalar_tensor_tensor(
                    qkv_bf, psQ[pi][:, :], 1.0, b_rep, ALU.mult, ALU.add))
                qevm[pi] = nv_
                if is_q or is_k:
                    p.wait("scalar", cV, nv_)
                    for hh in range(4):
                        if is_q:
                            col = NRMQ + (ob * 4 + hh) * 16 + tl
                        else:
                            col = NRMK + ((ob - 4) * 4 + hh) * 32 + j
                        nsq = p.sop(lambda e, hh=hh, col=col: e.activation(
                            junkb[:, hh * 128:(hh + 1) * 128],
                            qkv_bf[:, hh * 128:(hh + 1) * 128], AF.Square,
                            bias=0.0, scale=1.0, accum_out=sm2[:, col:col + 1]), inc=(hh == 3))
                    for hh in range(4):
                        p.vop(lambda e, hh=hh, j=j: e.tensor_copy(
                            sb_Ar[:, hh * 128:hh * 128 + 64], sb_S[:, j * 64:(j + 1) * 64]), inc=False)
                        p.vop(lambda e, hh=hh, j=j: e.tensor_copy(
                            sb_Ar[:, hh * 128 + 64:(hh + 1) * 128], sb_S[:, j * 64:(j + 1) * 64]), inc=False)
                        p.vop(lambda e, hh=hh, j=j: e.tensor_scalar_mul(
                            sb_Br[:, hh * 128:hh * 128 + 64], sb_C[:, j * 64:(j + 1) * 64], -1.0), inc=False)
                        p.vop(lambda e, hh=hh, j=j: e.tensor_copy(
                            sb_Br[:, hh * 128 + 64:(hh + 1) * 128], sb_C[:, j * 64:(j + 1) * 64]), inc=False)
                    nab = p.vop(lambda e, j=j: e.tensor_copy(sb_Br[:, 64:128], sb_C[:, j * 64:(j + 1) * 64]))
                    p.wait("vector", cV, nab)
                    p.wait("vector", cS, nsq)
                    qv = qkv_bf.rearrange("p (h q) -> p h q", q=128)
                    rv = rp2.rearrange("p (h q) -> p h q", q=128)
                    bv = sb_Br.rearrange("p (h q) -> p h q", q=128)
                    p.vop(lambda e: e.tensor_tensor(t1r, qkv_bf, sb_Ar, ALU.mult), inc=False)
                    p.vop(lambda e: e.tensor_tensor(rv[:, :, 0:64], qv[:, :, 64:128], bv[:, :, 0:64], ALU.mult), inc=False)
                    p.vop(lambda e: e.tensor_tensor(rv[:, :, 64:128], qv[:, :, 0:64], bv[:, :, 64:128], ALU.mult), inc=False)
                    nv_ = p.vop(lambda e: e.tensor_tensor(qkv_bf, t1r, rp2, ALU.add))
                    qevm[pi] = nv_
                    p.wait("tensor", cV, nv_)
                    stg = QST if is_q else KST
                    for hh in range(4):
                        ti = hh % 2
                        p.wait("tensor", cV, tevm[ti])
                        nt2 = p.top(lambda e, hh=hh, ti=ti: e.transpose(
                            psTb[ti][:, :], qkv_bf[:, hh * 128:(hh + 1) * 128], sb_id))
                        p.wait("vector", cT, nt2)
                        tevm[ti] = p.vop(lambda e, hh=hh, ti=ti, stg=stg, tl=tl: e.tensor_copy(
                            bfA[:, stg + hh * 2048 + tl * 128:stg + hh * 2048 + (tl + 1) * 128],
                            psTb[ti][:, :]))
                else:
                    p.wait("gpsimd", cV, nv_)
                    nst = p.dma("gpsimd", t_v[j * 128:(j + 1) * 128, (ob - 8) * 512:(ob - 7) * 512], qkv_bf)
                    p.wait("vector", dG, nst)
            if is_q or is_k:
                nvf = p.vop(lambda e: e.nop())
                p.wait("gpsimd", cV, nvf)
                stg = QST if is_q else KST
                tsc = t_qT if is_q else t_kT
                hb = (ob - (0 if is_q else 4)) * 4
                cb = chunk * 2048
                width = 2048
                for hh in range(4):
                    hr = (hb + hh) * 128
                    p.dma("gpsimd", tsc[hr:hr + 128, cb:cb + width],
                          bfA[:, stg + hh * 2048:stg + hh * 2048 + width])
                p.wait("vector", dG, dG.n)

    psP.close()

    if PHASES < 3:
        return _finish_stub(nc, ctx, p, t_out, fA, probes)

    KTH, QTH, VP, PT, AOUT = 0, 8192, 12288, 20608, 32768
    aout = lambda jj: bfA[:, AOUT + jj * 2048:AOUT + (jj + 1) * 2048]

    psP3 = ExitStack()
    psS = [psP3.enter_context(nc.psum_tensor(f"psS{i}", [128, 512], F32)).ap() for i in range(2)]
    psPV = [psP3.enter_context(nc.psum_tensor(f"psPV{i}", [128, 129], F32)).ap() for i in range(4)]
    psBC = psP3.enter_context(nc.psum_tensor("psBC", [128, 16], F32)).ap()

    p.wait("vector", cS, cS.n)
    for h in range(H):
        p.vop(lambda e, h=h: e.tensor_reduce(
            sm2[:, HMX + h:HMX + h + 1], sm2[:, NRMQ + h * 16:NRMQ + (h + 1) * 16], AX.X, ALU.max), inc=False)
        p.vop(lambda e, h=h: e.tensor_reduce(
            sm2[:, HMX + 16 + h:HMX + 17 + h], sm2[:, NRMK + h * 32:NRMK + (h + 1) * 32], AX.X, ALU.max), inc=False)
    nv = p.vop(lambda e: e.nop())
    p.wait("gpsimd", cV, nv)
    n1 = p.dma("gpsimd", t_nrm, sm2[:, HMX:HMX + 32])
    p.wait("sync", dG, n1)
    nrm_rows = fA[0:32, 0:128]
    n2 = p.dma("sync", nrm_rows, t_nrm.rearrange("p c -> c p"), allow_slow_non_contiguous=True)
    p.wait("vector", dS, n2)
    mx32 = fA[0:32, 128:129]
    p.vchain([lambda e: e.tensor_reduce(mx32, nrm_rows, AX.X, ALU.max)])
    nv = p.vop(lambda e: e.nop())
    p.wait("gpsimd", cV, nv)
    n3 = p.dma("gpsimd", t_nrm2, mx32)
    p.wait("sync", dG, n3)
    mrow = fA[0:1, 192:224]
    n4 = p.dma("sync", mrow, t_nrm2.rearrange("a b -> b a"), allow_slow_non_contiguous=True)
    p.wait("vector", dS, n4)
    m2row = fA[0:1, 224:240]
    p.vchain([lambda e: e.tensor_tensor(m2row, fA[0:1, 192:208], fA[0:1, 208:224], ALU.mult)])
    p.wait("scalar", cV, cV.n)
    mrow2 = fA[0:1, 240:256]
    ns = p.sop(lambda e: e.activation(mrow2, m2row, AF.Sqrt, bias=0.0, scale=1.0 / 128.0))
    p.wait("tensor", cS, ns)
    nt = p.top(lambda e: e.matmul(psBC[:, :], onesr[0:1, :], mrow2, start=True, stop=True))
    p.wait("vector", cT, nt)
    nv = p.vop(lambda e: e.tensor_copy(sm2[:, MBC:MBC + 16], psBC[:, :]))
    p.wait("vector", cV, nv)
    for h in range(H):
        p.vop(lambda e, h=h: e.tensor_scalar(
            sm2[:, BIAS + h * 32:BIAS + (h + 1) * 32], sm2[:, KBC:KBC + 32],
            sm2[:, MBC + h:MBC + h + 1], None, ALU.subtract), inc=False)
    nbias = p.vop(lambda e: e.nop())
    p.wait("scalar", cV, nbias)
    p.wait("tensor", cV, nbias)

    pb = mkprobe("Mbc", (128, 16))
    if pb is not None:
        p.wait("gpsimd", cV, nbias)
        p.dma("gpsimd", pb, sm2[:, MBC:MBC + 16])

    p.wait("sync", dG, dG.n)

    ldm = [0, 0]
    vmem = [0, 0]
    sexp = [0, 0]
    pvm = [0, 0]
    evq = 0
    vplus = lambda hb: bfA[:, VP + hb * 4128:VP + (hb + 1) * 4128].rearrange("p (k c) -> p k c", c=129)
    for h in range(H):
        hb = h % 2
        p.wait("sync", cT, ldm[hb])
        nk = p.dma("sync", bfA[:, KTH + hb * 4096:KTH + (hb + 1) * 4096], t_kT[h * 128:(h + 1) * 128, :])
        nq = p.dma("sync", bfA[:, QTH + hb * 2048:QTH + (hb + 1) * 2048], t_qT[h * 128:(h + 1) * 128, :])
        p.wait("vector", cT, vmem[hb])
        nms = p.vop(lambda e, hb=hb: e.memset(vplus(hb)[:, :, 128:129], 1.0))
        p.wait("sync", cV, nms)
        nvl = p.dma("sync", vplus(hb)[:, :, 0:128],
                    t_v.rearrange("(k p) d -> p k d", p=128)[:, :, h * 128:(h + 1) * 128])
        kT_h = bfA[:, KTH + hb * 4096:KTH + (hb + 1) * 4096]
        qT_h = bfA[:, QTH + hb * 2048:QTH + (hb + 1) * 2048]
        first_pe = True
        for cq in range(NCQ):
            klist = list(range(0, 4 * cq + 4)) + list(range(16, 32))
            nk_t = len(klist)
            for ki, kt in enumerate(klist):
                sb_ = ki % 2
                if first_pe:
                    p.wait("tensor", dS, nvl)
                    first_pe = False
                p.wait("tensor", cS, sexp[sb_])
                nmt = p.top(lambda e, kt=kt, cq=cq, sb_=sb_, kT_h=kT_h, qT_h=qT_h: e.matmul(
                    psS[sb_][:, :], kT_h[:, kt * 128:(kt + 1) * 128],
                    qT_h[:, cq * 512:(cq + 1) * 512], start=True, stop=True))
                diag = 4 * cq <= kt <= 4 * cq + 3
                expw = nmt
                if diag:
                    p.wait("vector", cT, nmt)
                    nmk = p.vop(lambda e, z=kt - 4 * cq, sb_=sb_: e.tensor_tensor(
                        psS[sb_][:, :], psS[sb_][:, :], sb_mask[:, z * 512:(z + 1) * 512], ALU.add))
                    p.wait("scalar", cV, nmk)
                p.wait("scalar", cT, nmt)
                p.wait("scalar", cT, pvm[sb_])
                nse = p.sop(lambda e, sb_=sb_, h=h, kt=kt: e.activation(
                    bfA[:, PT + sb_ * 512:PT + (sb_ + 1) * 512], psS[sb_][:, :], AF.Exp,
                    bias=sm2[:, BIAS + h * 32 + kt:BIAS + h * 32 + kt + 1], scale=ISQHD))
                sexp[sb_] = nse
                p.wait("tensor", cS, nse)
                if ki == 0:
                    p.wait("tensor", cV, evq)
                for sub in range(4):
                    pvm[sb_] = p.top(lambda e, sub=sub, sb_=sb_, kt=kt, hb=hb, ki=ki, last=(ki == nk_t - 1): e.matmul(
                        psPV[sub][:, :], bfA[:, PT + sb_ * 512 + sub * 128:PT + sb_ * 512 + (sub + 1) * 128],
                        vplus(hb)[:, kt, :], start=(ki == 0), stop=last,
                        skip_group_check=True), inc=(sub == 3))
            ldm[hb] = pvm[0] if pvm[0] > pvm[1] else pvm[1]
            vmem[hb] = ldm[hb]
            p.wait("vector", cT, max(pvm))
            for sub in range(4):
                jj = cq * 4 + sub
                p.vchain([lambda e, sub=sub: e.reciprocal(smc(TPA), psPV[sub][:, 128:129])])
                evq = p.vop(lambda e, sub=sub, jj=jj, h=h: e.tensor_scalar(
                    aout(jj)[:, h * 128:(h + 1) * 128], psPV[sub][:, 0:128],
                    smc(TPA), None, ALU.mult))

    psP3.close()

    pb = mkprobe("aout", (128, 2048))
    if pb is not None:
        nv = p.vop(lambda e: e.tensor_copy(fA[:, 0:2048], aout(0)))
        p.wait("gpsimd", cV, nv)
        p.dma("gpsimd", pb, fA[:, 0:2048])

    if PHASES < 4:
        return _finish_stub(nc, ctx, p, t_out, fA, probes)

    ATT = 0
    attT = lambda dc, jj: bfA[:, ATT + dc * 2048 + jj * 128:ATT + dc * 2048 + (jj + 1) * 128]

    psP4 = ExitStack()
    psTb4 = [psP4.enter_context(nc.psum_tensor(f"psT4{i}", [128, 128], BF16)).ap() for i in range(2)]
    psO = [psP4.enter_context(nc.psum_tensor(f"psO{i}", [128, 512], F32)).ap() for i in range(2)]
    psR4 = psP4.enter_context(nc.psum_tensor("psR4", [128, 512], F32)).ap()

    evm4 = [0, 0]
    for jj in range(NTO):
        for dc in range(DC):
            ti = dc % 2
            p.wait("tensor", cV, evm4[ti])
            nt_ = p.top(lambda e, jj=jj, dc=dc, ti=ti: e.transpose(
                psTb4[ti][:, :], aout(jj)[:, dc * 128:(dc + 1) * 128], sb_id))
            p.wait("vector", cT, nt_)
            evm4[ti] = p.vop(lambda e, jj=jj, dc=dc, ti=ti: e.tensor_copy(
                attT(dc, jj), psTb4[ti][:, :]))
    p.wait("gpsimd", cV, max(evm4))
    WO = AOUT
    wo_sb = bfA[:, WO:WO + 32768].rearrange("p (c o) -> p c o", o=2048)
    nwo = p.dma("gpsimd", wo_sb[:, :, :], t_wo.rearrange("(c p) o -> p c o", p=128))
    bo_rep = fA[:, 4096:6144]
    psr_prev = 0
    for obx in range(4):
        p.wait("sync", cT, psr_prev)
        nbl = p.dma("sync", brow, t_bo[:, obx * 512:(obx + 1) * 512])
        p.wait("tensor", dS, nbl)
        nt_ = p.top(lambda e: e.matmul(psR4[:, :], onesr[0:1, :], brow, start=True, stop=True))
        psr_prev = nt_
        p.wait("vector", cT, nt_)
        nv = p.vop(lambda e, obx=obx: e.tensor_copy(bo_rep[:, obx * 512:(obx + 1) * 512], psR4[:, :]))
        p.wait("tensor", cV, nv)
    p.wait("tensor", dG, nwo)

    proj = fA[:, 0:2048]
    x1t = fA[:, 2048:4096]
    a2t = fA[:, 0:2048]
    qc1 = fA[:, 8192:10240]
    qc2 = fA[:, 10240:12288]

    oev = 0
    oevs = [0, 0]
    qc1_free = 0
    for jj in range(NTO):
        p.wait("tensor", cV, oev)
        for obx in range(4):
            pi = obx % 2
            p.wait("tensor", cV, oevs[pi])
            for dc in range(DC):
                ntm = p.top(lambda e, dc=dc, jj=jj, obx=obx, pi=pi: e.matmul(
                    psO[pi][:, :], attT(dc, jj), wo_sb[:, dc, obx * 512:(obx + 1) * 512],
                    start=(dc == 0), stop=(dc == DC - 1)), inc=(dc == DC - 1))
            p.wait("vector", cT, ntm)
            nv = p.vop(lambda e, obx=obx, pi=pi: e.scalar_tensor_tensor(
                proj[:, obx * 512:(obx + 1) * 512], psO[pi][:, :], 1.0,
                bo_rep[:, obx * 512:(obx + 1) * 512], ALU.mult, ALU.add))
            oevs[pi] = nv
        oev = nv
        ln_stats(proj, D, junk=qc2)
        p.wait("sync", cV, qc1_free)
        nxl = p.dma("sync", qc1, t_x[jj * 128:(jj + 1) * 128, :])
        p.vop(lambda e: e.tensor_scalar(qc2, proj, smc(NEGM), smc(RSTD), ALU.add, ALU.mult), inc=False)
        p.wait("vector", dS, nxl)
        nv = p.vop(lambda e: e.tensor_tensor(x1t, qc1, qc2, ALU.add))
        p.wait("gpsimd", cV, nv)
        nst = p.dma("gpsimd", t_x1[jj * 128:(jj + 1) * 128, :], x1t)
        ln_stats(x1t, D, junk=qc2)
        nv = p.vop(lambda e: e.tensor_scalar(a2t, x1t, smc(NEGM), smc(RSTD), ALU.add, ALU.mult))
        p.wait("vector", cV, nv)
        p.vchain([
            lambda e: e.tensor_reduce(smc(MXC), a2t, AX.X, ALU.max, apply_absolute_value=True),
            lambda e: e.tensor_scalar_max(smc(MXC), smc(MXC), EPS),
            lambda e: e.reciprocal(smc(TPA), smc(MXC)),
            lambda e: e.tensor_scalar_mul(smc(SCL), smc(TPA), 128.0),
            lambda e: e.tensor_scalar_mul(smc(SINV), smc(MXC), 1.0 / 128.0),
            lambda e: e.tensor_scalar(smc(NPS), smc(SCL), smc(NP1), None, ALU.mult),
        ])
        p.vop(lambda e: e.tensor_scalar(qc1, a2t, smc(SCL), C23, ALU.mult, ALU.add), inc=False)
        p.vop(lambda e: e.tensor_scalar(qc2, a2t, smc(SCL), C52, ALU.mult, ALU.add), inc=False)
        p.vop(lambda e: e.scalar_tensor_tensor(qc2, qc2, C52, qc1, ALU.subtract, ALU.add), inc=False)
        nqf = p.vop(lambda e: e.scalar_tensor_tensor(qc1, a2t, smc(NPS), qc2, ALU.mult, ALU.add))
        qc1_free = nqf
        p.vop(lambda e: e.tensor_scalar(qc1, qc1, C23, 127.0, ALU.subtract, ALU.min), inc=False)
        dst = bfA[:, ATT:ATT + 32768].rearrange("p (c t) -> p c t", t=2048)[:, :, jj * 128:(jj + 1) * 128]
        src3 = qc1.rearrange("p (c t) -> p c t", t=128)
        nv = p.vop(lambda e, dst=dst, src3=src3: e.tensor_scalar(dst, src3, smc(SINV), None, ALU.mult))
        p.wait("tensor", cV, nv)
        for dc in range(DC):
            ti = dc % 2
            p.wait("tensor", cV, evm4[ti])
            nt_ = p.top(lambda e, jj=jj, dc=dc, ti=ti: e.transpose(
                psTb4[ti][:, :], attT(dc, jj), sb_id))
            p.wait("vector", cT, nt_)
            evm4[ti] = p.vop(lambda e, jj=jj, dc=dc, ti=ti: e.tensor_copy(
                attT(dc, jj), psTb4[ti][:, :]))
        nv = p.vop(lambda e: e.nop())
        p.wait("sync", cV, nv)
        nst = p.dma("sync",
                    t_a2qT.rearrange("(c p) t -> p c t", p=128)[:, :, jj * 128:(jj + 1) * 128],
                    bfA[:, ATT:ATT + 32768].rearrange("p (c t) -> p c t", t=2048)[:, :, jj * 128:(jj + 1) * 128])
        p.wait("vector", dS, nst)

    psP4.close()

    pb = mkprobe("x1", (128, 2048))
    if pb is not None:
        p.wait("sync", dG, dG.n)
        n_ = p.dma("sync", qc1, t_x1[0:128, :])
        p.wait("gpsimd", dS, n_)
        p.dma("gpsimd", pb, qc1)

    if PHASES < 5:
        return _finish_stub(nc, ctx, p, t_out, fA, probes)

    psP5 = ExitStack()
    ps11 = psP5.enter_context(nc.psum_tensor("ps11", [1, 1], F32)).ap()
    psBC5 = psP5.enter_context(nc.psum_tensor("psBC5", [128, 1], F32)).ap()

    p.wait("sync", cV, cV.n)
    p.wait("sync", cT, cT.n)
    p.wait("scalar", cV, cV.n)
    p.wait("vector", cT, cT.n)
    onescol = fA[:, 6144:6145]
    nv = p.vop(lambda e: e.memset(onescol, 1.0))
    p.wait("tensor", cV, nv)
    wtiles = [fA[:, 0:2048], fA[:, 2048:4096]]
    absout = fA[:, 4096:6144]
    accall = sm2[:, 0:64]
    acc1 = sm[:, 32:33]
    val1 = fA[0:1, 6148:6149]
    q1b = bfA[:, 0:2048]
    q2b = bfA[:, 2048:4096]
    q3b = bfA[:, 4096:6144]

    for mat, (t_w, t_wq) in enumerate(((t_wg, t_wgq), (t_wu, t_wuq), (t_wd, t_wdq))):
        nrows = I if mat < 2 else D
        ncols = D if mat < 2 else I
        ncc = ncols // 2048
        slabs = [(rt, cc) for rt in range(nrows // 128) for cc in range(ncc)]
        p.wait("sync", cV, cV.n)
        scons = [0, 0]
        for si, (rt, cc) in enumerate(slabs):
            bi = si % 2
            p.wait("sync", cS, scons[bi])
            nld = p.dma("sync", wtiles[bi],
                        t_w[rt * 128:(rt + 1) * 128, cc * 2048:(cc + 1) * 2048])
            p.wait("scalar", dS, nld)
            scons[bi] = p.sop(lambda e, bi=bi, si=si: e.activation(
                absout, wtiles[bi], AF.Abs, bias=0.0, scale=1.0,
                accum_out=sm2[:, si:si + 1]))
        p.wait("vector", cS, max(scons))
        p.vchain([lambda e, n=len(slabs): e.tensor_reduce(acc1, sm2[:, 0:n], AX.X, ALU.add)])
        p.vchain([lambda e, mat=mat: e.tensor_copy(sm[:, 40 + mat:41 + mat], acc1)])
        if mat == 0:
            p.vchain([lambda e: e.tensor_copy(sm2[:, 64:128], sm2[:, 0:64])])
        p.wait("tensor", cV, cV.n)
        nt_ = p.top(lambda e: e.matmul(ps11[:, :], acc1, onescol, start=True, stop=True))
        p.wait("vector", cT, nt_)
        p.vchain([
            lambda e: e.tensor_copy(val1, ps11[:, :]),
            lambda e, nr=nrows, nc_=ncols: e.tensor_scalar_mul(val1, val1, 1.0 / (nr * nc_)),
            lambda e: e.tensor_scalar_max(val1, val1, EPS),
        ])
        p.wait("tensor", cV, cV.n)
        nt_ = p.top(lambda e: e.matmul(psBC5[:, :], onesr[0:1, :], val1, start=True, stop=True))
        p.wait("vector", cT, nt_)
        mc = MG if mat == 0 else (MU if mat == 1 else THC)
        p.vchain([
            lambda e, mc=mc: e.tensor_copy(sm[:, mc:mc + 1], psBC5[:, :]),
            lambda e, mc=mc: e.tensor_scalar_mul(smc(TPA), sm[:, mc:mc + 1], 0.5),
            lambda e: e.tensor_scalar_mul(smc(TPB), smc(TPA), -1.0),
        ])
        vcons = [0, 0]
        for si, (rt, cc) in enumerate(slabs):
            bi = si % 2
            p.wait("sync", cV, vcons[bi])
            nld = p.dma("sync", wtiles[bi],
                        t_w[rt * 128:(rt + 1) * 128, cc * 2048:(cc + 1) * 2048])
            p.wait("vector", dS, nld)
            p.vop(lambda e, bi=bi: e.tensor_scalar(
                q1b, wtiles[bi], smc(TPA), None, ALU.is_gt), inc=False)
            vcons[bi] = p.vop(lambda e, bi=bi: e.tensor_scalar(
                q2b, wtiles[bi], smc(TPB), None, ALU.is_lt))
            nv = p.vop(lambda e: e.tensor_tensor(q3b, q1b, q2b, ALU.subtract))
            p.wait("gpsimd", cV, nv)
            nst = p.dma("gpsimd",
                        t_wq[rt * 128:(rt + 1) * 128, cc * 2048:(cc + 1) * 2048], q3b)
            p.wait("vector", dG, nst)

    psP5.close()

    if PHASES < 6:
        return _finish_stub(nc, ctx, p, t_out, fA, probes)

    A2Q, WGB, WUB, TROT, DSTR, INTER, SG, YB = 0, 8192, 16384, 8192, 10240, 24576, 57344, 57344
    a2q_sl = bfA[:, A2Q:A2Q + 8192].rearrange("p (c t) -> p c t", t=512)
    interc = lambda c: bfA[:, INTER + c * 512:INTER + (c + 1) * 512]

    x1sl = fA[:, 0:2048]
    chn = fA[:, 2048:4096]
    rows = lambda ts_: fA[:, 4096 + ts_ * 2048:4096 + (ts_ + 1) * 2048]

    p.wait("sync", dG, dG.n)

    tt_guard = 0
    ps_guard = 0
    for tt in range(NTT):
        psA_ = ExitStack()
        psG = psA_.enter_context(nc.psum_tensor(f"psG{tt}", [128, 512], F32)).ap()
        psU = psA_.enter_context(nc.psum_tensor(f"psU{tt}", [128, 512], F32)).ap()
        p.wait("sync", cT, tt_guard)
        nld = p.dma("sync", a2q_sl[:, :, :],
                    t_a2qT.rearrange("(c p) t -> p c t", p=128)[:, :, tt * 512:(tt + 1) * 512])
        p.wait("tensor", dS, nld)
        silu_sg = bfA[:, SG:SG + 512]
        guev = ps_guard
        nsg_prev = 0
        for ib in range(16):
            for dc in range(DC):
                p.dma("sync", bfA[:, WGB + dc * 512:WGB + (dc + 1) * 512],
                      t_wgq[ib * 512:(ib + 1) * 512, dc * 128:(dc + 1) * 128], transpose=True)
                nlu = p.dma("sync", bfA[:, WUB + dc * 512:WUB + (dc + 1) * 512],
                            t_wuq[ib * 512:(ib + 1) * 512, dc * 128:(dc + 1) * 128], transpose=True)
            p.wait("tensor", dS, nlu)
            for it in range(4):
                c = ib * 4 + it
                p.wait("tensor", cV, guev)
                p.wait("tensor", cS, nsg_prev)
                for dc in range(DC):
                    ng = p.top(lambda e, dc=dc, it=it: e.matmul(
                        psG[:, :], bfA[:, WGB + dc * 512 + it * 128:WGB + dc * 512 + (it + 1) * 128],
                        a2q_sl[:, dc, :], start=(dc == 0), stop=(dc == DC - 1)), inc=(dc == DC - 1))
                p.wait("scalar", cT, ng)
                nsg = p.sop(lambda e: e.activation(silu_sg, psG[:, :], AF.Silu,
                                                   bias=0.0, scale=sm[:, MG:MG + 1]))
                nsg_prev = nsg
                for dc in range(DC):
                    nu = p.top(lambda e, dc=dc, it=it: e.matmul(
                        psU[:, :], bfA[:, WUB + dc * 512 + it * 128:WUB + dc * 512 + (it + 1) * 128],
                        a2q_sl[:, dc, :], start=(dc == 0), stop=(dc == DC - 1)), inc=(dc == DC - 1))
                p.wait("vector", cT, nu)
                p.wait("vector", cS, nsg)
                guev = p.vop(lambda e, c=c: e.scalar_tensor_tensor(
                    interc(c), psU[:, :], sm[:, MU:MU + 1], silu_sg, ALU.mult, ALU.mult))
            p.wait("sync", cT, nu)
        psA_.close()
        pb = mkprobe(f"inter{tt}", (128, 2048))
        if pb is not None and tt == 0:
            p.wait("gpsimd", cV, guev)
            nv = p.vop(lambda e: e.tensor_copy(fA[:, 2048:4096], bfA[:, INTER:INTER + 2048]))
            p.wait("gpsimd", cV, nv)
            p.dma("gpsimd", pb, fA[:, 2048:4096])

        psB_ = ExitStack()
        psF = [psB_.enter_context(nc.psum_tensor(f"psF{tt}_{i}", [128, 512], F32)).ap() for i in range(2)]
        fev = [guev, guev]
        fst = [0, 0, 0, 0]
        for c in range(IC):
            fi = c % 2
            ri = c % 4
            p.wait("tensor", cV, fev[fi])
            nt_ = p.top(lambda e, c=c, fi=fi: e.matmul(
                psF[fi][:, :], sb_h128, interc(c), start=True, stop=True))
            p.wait("vector", cT, nt_)
            p.wait("vector", dG, fst[ri])
            fev[fi] = p.vop(lambda e, c=c, fi=fi, ri=ri: e.tensor_copy(
                bfA[:, TROT + ri * 512:TROT + (ri + 1) * 512], psF[fi][:, :]))
            p.wait("gpsimd", cV, fev[fi])
            fst[ri] = p.dma("gpsimd", t_Tsc[c * 128:(c + 1) * 128, :],
                            bfA[:, TROT + ri * 512:TROT + (ri + 1) * 512])
        psB_.close()

        psC_ = ExitStack()
        psY = psC_.enter_context(nc.psum_tensor(f"psY{tt}", [128, 512], F32)).ap()
        p.wait("sync", dG, dG.n)
        yv = bfA[:, YB:YB + 8192]
        y3 = yv.rearrange("p (a b) -> p a b", b=128)
        Tsc_r = t_Tsc.rearrange("(c s) t -> c s t", s=128)
        trcons = [0, 0]
        yev = max(fev)
        for tb in range(4):
            for g in range(16):
                gi = g % 2
                trh = sb_Tr[0:IC, gi * 1024:(gi + 1) * 1024]
                p.wait("sync", cT, trcons[gi])
                nld = p.dma("sync", trh.rearrange("c (s t) -> c s t", t=128),
                            Tsc_r[:, g * 8:(g + 1) * 8, tb * 128:(tb + 1) * 128])
                p.wait("tensor", dS, nld)
                p.wait("tensor", cV, yev)
                for s_ in range(8):
                    ntm = p.top(lambda e, s_=s_, gi=gi: e.matmul(
                        psY[:, s_ * 64:(s_ + 1) * 64],
                        sb_Tr[0:IC, gi * 1024 + s_ * 128:gi * 1024 + (s_ + 1) * 128],
                        sb_hic, start=True, stop=True), inc=(s_ == 7))
                trcons[gi] = ntm
                p.wait("vector", cT, ntm)
                yev = p.vop(lambda e, g=g: e.tensor_copy(
                    y3[:, :, g * 8:(g + 1) * 8],
                    psY.rearrange("p (s c) -> p c s", c=64)))
            p.wait("vector", cV, yev)
            p.vchain([
                lambda e: e.tensor_reduce(smc(MXC), yv, AX.X, ALU.max, apply_absolute_value=True),
                lambda e: e.tensor_scalar_max(smc(MXC), smc(MXC), EPS),
                lambda e: e.reciprocal(smc(TPA), smc(MXC)),
                lambda e: e.tensor_scalar_mul(smc(SCL), smc(TPA), 128.0),
                lambda e: e.tensor_scalar_mul(smc(SINV), smc(MXC), 1.0 / 128.0),
                lambda e: e.tensor_scalar(smc(NPS), smc(SCL), smc(NP1), None, ALU.mult),
            ])
            for qk in range(4):
                ysl = yv[:, qk * 2048:(qk + 1) * 2048]
                p.vop(lambda e, ysl=ysl: e.tensor_scalar(x1sl, ysl, smc(SCL), C23, ALU.mult, ALU.add), inc=False)
                p.vop(lambda e, ysl=ysl: e.tensor_scalar(chn, ysl, smc(SCL), C52, ALU.mult, ALU.add), inc=False)
                p.vop(lambda e: e.scalar_tensor_tensor(chn, chn, C52, x1sl, ALU.subtract, ALU.add), inc=False)
                p.vop(lambda e, ysl=ysl: e.scalar_tensor_tensor(x1sl, ysl, smc(NPS), chn, ALU.mult, ALU.add), inc=False)
                p.vop(lambda e: e.tensor_scalar(x1sl, x1sl, C23, 127.0, ALU.subtract, ALU.min), inc=False)
                nv = p.vop(lambda e, ysl=ysl: e.tensor_scalar(ysl, x1sl, smc(SINV), None, ALU.mult))
            p.wait("gpsimd", cV, nv)
            nst = p.dma("gpsimd", t_a3[tb * 128:(tb + 1) * 128, :], yv)
            p.wait("vector", dG, nst)
        psC_.close()

        psD_ = ExitStack()
        psDn = [psD_.enter_context(nc.psum_tensor(f"psD{tt}_{i}", [128, 512], F32)).ap() for i in range(4)]
        p.wait("sync", dG, dG.n)
        a3b = [bfA[:, DSTR:DSTR + 512], bfA[:, DSTR + 512:DSTR + 1024]]
        wdb = [bfA[:, DSTR + 1024:DSTR + 1536], bfA[:, DSTR + 1536:DSTR + 2048]]
        dcons = [0, 0]
        dnev = cV.n
        for ob in range(4):
            p.wait("tensor", cV, dnev)
            for ic in range(IC):
                bi = ic % 2
                p.wait("sync", cT, dcons[bi])
                p.dma("sync", a3b[bi], t_a3[:, ic * 128:(ic + 1) * 128], transpose=True)
                nld = p.dma("sync", wdb[bi],
                            t_wdq[ob * 512:(ob + 1) * 512, ic * 128:(ic + 1) * 128], transpose=True)
                p.wait("tensor", dS, nld)
                if ic < 2:
                    p.wait("tensor", cV, dnev)
                for ts_ in range(4):
                    dcons[bi] = p.top(lambda e, ts_=ts_, bi=bi, ic=ic: e.matmul(
                        psDn[ts_][:, :], a3b[bi][:, ts_ * 128:(ts_ + 1) * 128], wdb[bi],
                        start=(ic == 0), stop=(ic == IC - 1), skip_group_check=True), inc=(ts_ == 3))
            p.wait("vector", cT, max(dcons))
            for ts_ in range(4):
                dnev = p.vop(lambda e, ts_=ts_, ob=ob: e.tensor_copy(
                    rows(ts_)[:, ob * 512:(ob + 1) * 512], psDn[ts_][:, :]))
        p.wait("tensor", cV, dnev)
        tt_guard = max(dcons)
        ps_guard = dnev
        psD_.close()

        for ts_ in range(4):
            jj = tt * 4 + ts_
            ln_stats(rows(ts_), D)
            p.wait("sync", dG, dG.n)
            nxl = p.dma("sync", x1sl, t_x1[jj * 128:(jj + 1) * 128, :])
            nv = p.vop(lambda e, ts_=ts_: e.tensor_scalar(
                chn, rows(ts_), smc(NEGM), smc(RSTD), ALU.add, ALU.mult))
            pbc = mkprobe(f"chn{jj}", (128, 2048))
            if pbc is not None:
                p.wait("gpsimd", cV, nv)
                p.dma("gpsimd", pbc, chn)
            p.wait("vector", dS, nxl)
            nv = p.vop(lambda e: e.tensor_tensor(x1sl, x1sl, chn, ALU.add))
            p.wait("gpsimd", cV, nv)
            p.dma("gpsimd", t_out[jj * 128:(jj + 1) * 128, :], x1sl)
            p.wait("vector", dG, dG.n)

    return _finish(nc, ctx, p, probes)


def _finish_stub(nc, ctx, p, t_out, fA, probes):
    nv = p.vop(lambda e: e.memset(fA[:, 0:2048], 0.0))
    p.wait("gpsimd", p.cV, nv)
    for j in range(NTO):
        p.dma("gpsimd", t_out[j * 128:(j + 1) * 128, :], fA[:, 0:2048])
    return _finish(nc, ctx, p, probes)


def _finish(nc, ctx, p, probes):
    p.wait("gpsimd", p.dG, p.dG.n)
    p.wait("gpsimd", p.dS, p.dS.n)
    blk = ctx.enter_context(nc.Block())
    for eng in ENGS:
        ops = p.ops[eng]
        getattr(blk, eng)(lambda e, ops=ops: [f(e) for f in ops])
    ctx.close()
    return nc, probes


def _stage(inputs):
    x = np.ascontiguousarray(np.asarray(inputs["L0_attn_x"], dtype=np.float32))
    perm = _qkv_perm_cached()
    wqkv = np.ascontiguousarray(np.asarray(inputs["L0_attn_w_qkv"], np.float32)[:, perm])
    bqkv = np.ascontiguousarray(np.asarray(inputs["L0_attn_b_qkv"], np.float32)[perm][None, :])
    wo = np.ascontiguousarray(np.asarray(inputs["L0_attn_w_o"], np.float32))
    bo = np.ascontiguousarray(np.asarray(inputs["L0_attn_b_o"], np.float32)[None, :])
    wg = np.ascontiguousarray(np.asarray(inputs["L0_ffn_w_gate"], np.float32))
    wu = np.ascontiguousarray(np.asarray(inputs["L0_ffn_w_up"], np.float32))
    wd = np.ascontiguousarray(np.asarray(inputs["L0_ffn_w_down"], np.float32))

    ident = np.eye(128, dtype=BFD)
    h128 = _had(128).astype(BFD)
    hic = (_had(IC) * float(np.float32(1.0 / np.sqrt(I)))).astype(BFD)
    masks = np.zeros((128, 4 * TQW), dtype=np.float32)
    pp = np.arange(128)[:, None]
    fl = np.arange(128)[None, :]
    for z in range(4):
        for s_ in range(4):
            blk = masks[:, z * TQW + s_ * 128:z * TQW + (s_ + 1) * 128]
            if s_ < z:
                blk[:] = NEG_BIG
            elif s_ == z:
                blk[:] = np.where(pp > fl, NEG_BIG, 0.0)

    maps = []
    for c in range(8):
        b, half = divmod(c, 2)
        if half == 0:
            xp = x[b]
            pos = np.arange(S, dtype=np.float32)
            kb = np.concatenate([np.zeros(TO, np.float32), np.full(TA - TO, NEG_BIG, np.float32)])
        else:
            xp = np.concatenate([x[b, TO:], x[b, :TO]], axis=0)
            pos = np.concatenate([np.arange(TO, TA), np.arange(0, TO)]).astype(np.float32)
            kb = np.zeros(TA, np.float32)
        maps.append({
            "x": np.ascontiguousarray(xp),
            "pos": np.ascontiguousarray(pos.reshape(NTA, 128).T),
            "kbias": np.ascontiguousarray(kb.reshape(NTA, 128).T),
            "wqkv": wqkv, "bqkv": bqkv, "wo": wo, "bo": bo,
            "wg": wg, "wu": wu, "wd": wd,
            "ident": ident, "h128": h128, "hic": hic, "masks": masks.astype(BFD),
        })
    return maps


_QKV_PERM_C = None


def _qkv_perm_cached():
    global _QKV_PERM_C
    if _QKV_PERM_C is None:
        perm = np.zeros(3 * D, dtype=np.int64)
        j = np.arange(128)
        for h in range(H):
            perm[h * 128 + j] = h * 384 + j
            perm[D + h * 128 + j] = h * 384 + 128 + j
            perm[2 * D + h * 128 + j] = h * 384 + 256 + j
        _QKV_PERM_C = perm
    return _QKV_PERM_C


_BUILT = None


def kernel(**inputs):
    global _BUILT
    if _BUILT is None:
        _BUILT = build_program()
    nc, probes = _BUILT
    maps = _stage(inputs)
    res = run_bass_kernel_spmd(nc, maps, core_ids=list(range(8)))
    if PROBES:
        kernel.last_results = res.results
    out = np.zeros((B, S, D), np.float32)
    for c in range(8):
        b, half = divmod(c, 2)
        out[b, half * TO:(half + 1) * TO] = res.results[c]["out"]
    return out


# version 10
# speedup vs baseline: 1.2012x; 1.2012x over previous
import os
import math
import numpy as np
import ml_dtypes
from contextlib import ExitStack

import concourse.bass as bass
import concourse.mybir as mybir
from concourse.bass_utils import run_bass_kernel_spmd

BFD = ml_dtypes.bfloat16
F32 = mybir.dt.float32
BF16 = mybir.dt.bfloat16
AF = mybir.ActivationFunctionType
ALU = mybir.AluOpType
AX = mybir.AxisListType

B, S, D, H, I = 4, 4096, 2048, 16, 8192
NTO, NTA, DC, IC = 16, 32, 16, 64
TO, TA = NTO * 128, NTA * 128
NCQ, TQW = 4, 512
TT, NTT = 512, 4
EPS = 1e-5

C23 = 12582912.0
C52 = float(1.5 * 2 ** 52)
ISQHD = float(1.0 / math.sqrt(128.0))
FWHT_NORM = float(np.float32(1.0 / np.sqrt(8192.0)))
NEG_BIG = -1.0e30

_LN4 = math.log(10000.0) / 64.0
LN4_HI = float(np.float32(int(_LN4 * 2 ** 14) / 2 ** 14))
LN4_MID = float(np.float32(_LN4 - LN4_HI))
LN4_LO = float(np.float32(_LN4 - LN4_HI - float(np.float32(LN4_MID))))
_TPI = 2.0 * math.pi
TPI_HI = float(np.float32(int(_TPI * 2 ** 11) / 2 ** 11))
TPI_MID = float(np.float32(_TPI - TPI_HI))
TPI_LO = float(np.float32(_TPI - TPI_HI - float(np.float32(TPI_MID))))
INV2PI = float(1.0 / _TPI)
PI32 = float(np.float32(math.pi))
_HPI = math.pi / 2.0
HPI_HI = float(np.float32(int(_HPI * 2 ** 12) / 2 ** 12))
HPI_MID = float(np.float32(_HPI - HPI_HI))
HPI_LO = float(np.float32(_HPI - HPI_HI - float(np.float32(HPI_MID))))

PHASES = int(os.environ.get("MK_PHASES", "6"))
P2LIM = int(os.environ.get("MK_P2LIM", "99"))
SCOUT = os.environ.get("MK_SCOUT", "") == "1"
QD = int(os.environ.get("MK_QD", "0"))
PROBES = [s for s in os.environ.get("MK_PROBE", "").split(",") if s]

ENGS = ("sync", "gpsimd", "tensor", "vector", "scalar")


class Sem:
    __slots__ = ("h", "n")

    def __init__(self, ctx, nc, name):
        self.h = ctx.enter_context(nc.semaphore(name))
        self.n = 0


class Prog:
    def __init__(self, nc, ctx):
        self.nc = nc
        self.ctx = ctx
        self.ops = {e: [] for e in ENGS}
        self.dS = Sem(ctx, nc, "dS")
        self.dG = Sem(ctx, nc, "dG")
        self.cT = Sem(ctx, nc, "cT")
        self.cV = Sem(ctx, nc, "cV")
        self.cS = Sem(ctx, nc, "cS")
        self.cG = Sem(ctx, nc, "cG")
        self.dA = Sem(ctx, nc, "dA")

    def em(self, eng, fn):
        self.ops[eng].append(fn)

    def wait(self, eng, sem, n=None):
        v = sem.n if n is None else n
        if v <= 0:
            return
        self.em(eng, lambda e, s=sem.h, v=v: e.wait_ge(s, v))

    def dma(self, eng, out, in_, **kw):
        sem = {"sync": self.dS, "gpsimd": self.dG, "scalar": self.dA}[eng]
        sem.n += 16
        self.em(eng, lambda e, o=out, i=in_, s=sem.h, kw=kw:
                e.dma_start(out=o, in_=i, **kw).then_inc(s, 16))
        return sem.n

    def _op(self, eng, sem, fn, inc):
        if inc:
            sem.n += 1
            self.em(eng, lambda e, f=fn, s=sem.h: f(e).then_inc(s, 1))
        else:
            self.em(eng, lambda e, f=fn: f(e))
        return sem.n

    def top(self, fn, inc=True):
        return self._op("tensor", self.cT, fn, inc)

    def vop(self, fn, inc=True):
        return self._op("vector", self.cV, fn, inc)

    def sop(self, fn, inc=True):
        return self._op("scalar", self.cS, fn, inc)

    def gop(self, fn, inc=True):
        return self._op("gpsimd", self.cG, fn, inc)

    def vchain(self, fns):
        for fn in fns:
            n = self.vop(fn)
            self.wait("vector", self.cV, n)
        return self.cV.n


def _had(n):
    h = np.array([[1.0]])
    while h.shape[0] < n:
        h = np.block([[h, h], [h, -h]])
    return h


def build_program():
    nc = bass.Bass("TRN2", target_bir_lowering=False, debug=False,
                   detect_race_conditions=False)
    ctx = ExitStack()

    t_x = nc.dram_tensor("x", [TA, D], F32, kind="ExternalInput").ap()
    t_pos = nc.dram_tensor("pos", [128, NTA], F32, kind="ExternalInput").ap()
    t_kb = nc.dram_tensor("kbias", [128, NTA], F32, kind="ExternalInput").ap()
    t_wqkv = nc.dram_tensor("wqkv", [D, 3 * D], F32, kind="ExternalInput").ap()
    t_bqkv = nc.dram_tensor("bqkv", [1, 3 * D], F32, kind="ExternalInput").ap()
    t_wo = nc.dram_tensor("wo", [D, D], F32, kind="ExternalInput").ap()
    t_bo = nc.dram_tensor("bo", [1, D], F32, kind="ExternalInput").ap()
    t_wg = nc.dram_tensor("wg", [I, D], F32, kind="ExternalInput").ap()
    t_wu = nc.dram_tensor("wu", [I, D], F32, kind="ExternalInput").ap()
    t_wd = nc.dram_tensor("wd", [D, I], F32, kind="ExternalInput").ap()
    t_id = nc.dram_tensor("ident", [128, 128], BF16, kind="ExternalInput").ap()
    t_h128 = nc.dram_tensor("h128", [128, 128], BF16, kind="ExternalInput").ap()
    t_hic = nc.dram_tensor("hic", [IC, IC], BF16, kind="ExternalInput").ap()
    t_tri = nc.dram_tensor("tri", [128, 128], BF16, kind="ExternalInput").ap()
    t_sin = nc.dram_tensor("sint", [128, NTA * 64], F32, kind="ExternalInput").ap()
    t_cos = nc.dram_tensor("cost", [128, NTA * 64], F32, kind="ExternalInput").ap()
    t_out = nc.dram_tensor("out", [TO, D], F32, kind="ExternalOutput").ap()

    SK = "ExternalOutput" if SCOUT else "Internal"
    t_kT = nc.dram_tensor("kT_sc", [H * 128, TA], BF16, kind=SK).ap()
    t_qT = nc.dram_tensor("qT_sc", [H * 128, TO], BF16, kind=SK).ap()
    t_v = nc.dram_tensor("v_sc", [TA, D], BF16, kind=SK).ap()
    t_x1 = nc.dram_tensor("x1_sc", [TO, D], F32, kind=SK).ap()
    t_a2qT = nc.dram_tensor("a2qT_sc", [D, TO], BF16, kind=SK).ap()
    t_wgqT = nc.dram_tensor("wgqT_sc", [D, I], BF16, kind="Internal").ap()
    t_wuqT = nc.dram_tensor("wuqT_sc", [D, I], BF16, kind="Internal").ap()
    t_wdqT = nc.dram_tensor("wdqT_sc", [I, D], BF16, kind="Internal").ap()
    t_Tsc = nc.dram_tensor("T_sc", [I, TT], BF16, kind="Internal").ap()
    t_a3 = nc.dram_tensor("a3_sc", [TT, I], BF16, kind="Internal").ap()
    t_nrm = nc.dram_tensor("nrm_sc", [128, 32], F32, kind="Internal").ap()
    t_nrm2 = nc.dram_tensor("nrm2_sc", [32, 1], F32, kind="Internal").ap()

    probes = {}

    def mkprobe(name, shape, dtype=F32):
        if name in PROBES:
            probes[name] = nc.dram_tensor("pb_" + name, list(shape), dtype,
                                          kind="ExternalOutput").ap()
            return probes[name]
        return None

    NBF, NF32 = 65536, 12288
    bfA = ctx.enter_context(nc.sbuf_tensor("bfA", [128, NBF], BF16)).ap()
    fA = ctx.enter_context(nc.sbuf_tensor("fA", [128, NF32], F32)).ap()
    sb_id = ctx.enter_context(nc.sbuf_tensor("sb_id", [128, 128], BF16)).ap()
    sb_h128 = ctx.enter_context(nc.sbuf_tensor("sb_h128", [128, 128], BF16)).ap()
    sb_hic = ctx.enter_context(nc.sbuf_tensor("sb_hic", [IC, IC], BF16)).ap()
    sb_tri = ctx.enter_context(nc.sbuf_tensor("sb_tri", [128, 128], BF16)).ap()
    sb_S = ctx.enter_context(nc.sbuf_tensor("sb_S", [128, NTA * 64], F32)).ap()
    sb_C = ctx.enter_context(nc.sbuf_tensor("sb_C", [128, NTA * 64], F32)).ap()
    sb_Ar = ctx.enter_context(nc.sbuf_tensor("sb_Ar", [128, 512], F32)).ap()
    sb_Br = ctx.enter_context(nc.sbuf_tensor("sb_Br", [128, 512], F32)).ap()
    sm = ctx.enter_context(nc.sbuf_tensor("sm", [128, 256], F32)).ap()
    sm2 = ctx.enter_context(nc.sbuf_tensor("sm2", [128, 1408], F32)).ap()
    sb_Tr = ctx.enter_context(nc.sbuf_tensor("sb_Tr", [IC, 2048], BF16)).ap()

    PH, NP1 = 0, 1
    S1, SSQ, NEGM, RSTD, TPA, TPB = 8, 9, 10, 11, 12, 13
    MXC, SCL, SINV, NPS = 16, 17, 18, 19
    MG, MU, THC, NTHC = 24, 25, 26, 27
    INVF = 64
    NRMQ, NRMK, BIAS = 0, 256, 768
    HMX, POSC, KBC, MBC = 1280, 1312, 1344, 1376

    p = Prog(nc, ctx)
    dS, dG, cT, cV, cS, cG = p.dS, p.dG, p.cT, p.cV, p.cS, p.cG

    smc = lambda c: sm[:, c:c + 1]

    p.dma("sync", sb_id[:, :], t_id)
    p.dma("sync", sb_h128[:, :], t_h128)
    p.dma("sync", sb_hic[:, :], t_hic)
    p.dma("sync", sb_tri[:, :], t_tri)
    p.dma("sync", sb_S[:, :], t_sin)
    p.dma("sync", sb_C[:, :], t_cos)
    p.dma("sync", sm2[:, POSC:POSC + 32], t_pos)
    n0 = p.dma("sync", sm2[:, KBC:KBC + 32], t_kb)
    p.wait("vector", dS, n0)

    p.vchain([
        lambda e: e.memset(smc(PH), 0.5),
        lambda e: e.tensor_scalar_add(smc(PH), smc(PH), C23),
        lambda e: e.tensor_scalar_sub(smc(PH), smc(PH), C23),
        lambda e: e.tensor_scalar_mul(smc(NP1), smc(PH), -2.0),
    ])
    nv = p.vop(lambda e: e.nop())
    p.wait("vector", cV, nv)

    A1T, WS, QWK, KST, QST, JNK = 0, 32768, 40960, 43008, 51200, 59392
    a1T = lambda dc, tl: bfA[:, A1T + dc * 2048 + tl * 128:A1T + dc * 2048 + (tl + 1) * 128]
    w_sb = bfA[:, WS:WS + 8192].rearrange("p (c j) -> p c j", j=512)
    qkv_bf = bfA[:, QWK:QWK + 512]
    rp2 = bfA[:, QWK + 512:QWK + 1024]
    t1r = bfA[:, QWK + 1024:QWK + 1536]
    a1bf = bfA[:, JNK:JNK + 2048]
    junkb = bfA[:, JNK + 2048:JNK + 4096]

    xbuf = [fA[:, 0:2048], fA[:, 2048:4096]]
    a1f = fA[:, 4096:6144]
    b_rep = fA[:, 6144:6656]
    onesr = fA[:, 6656:6784]
    brow = fA[0:1, 6784:7296]

    psP = ExitStack()
    psQ = [psP.enter_context(nc.psum_tensor(f"psQ{i}", [128, 512], F32)).ap() for i in range(2)]
    psTb = [psP.enter_context(nc.psum_tensor(f"psTb{i}", [128, 128], BF16)).ap() for i in range(2)]
    psR = psP.enter_context(nc.psum_tensor("psR", [128, 512], F32)).ap()

    nv = p.vop(lambda e: e.memset(onesr[0:1, :], 1.0))
    p.wait("tensor", cV, nv)

    def ln_stats(xap, nfeat, junk=None):
        jk = junkb if junk is None else junk
        nv_ = p.vop(lambda e, x_=xap: e.tensor_reduce(smc(S1), x_, AX.X, ALU.add))
        p.wait("scalar", cV, nv_)
        ns_ = p.sop(lambda e, x_=xap, n=nfeat, jk=jk: e.activation(
            jk[:, 0:n], x_, AF.Square, bias=0.0, scale=1.0, accum_out=smc(SSQ)))
        p.wait("vector", cS, ns_)
        p.vchain([
            lambda e: e.tensor_scalar_mul(smc(NEGM), smc(S1), -1.0 / nfeat),
            lambda e: e.tensor_tensor(smc(TPA), smc(NEGM), smc(NEGM), ALU.mult),
            lambda e: e.scalar_tensor_tensor(smc(TPB), smc(SSQ), 1.0 / nfeat, smc(TPA), ALU.mult, ALU.subtract),
            lambda e: e.tensor_scalar_add(smc(TPB), smc(TPB), EPS),
            lambda e: e.reciprocal(smc(TPA), smc(TPB)),
        ])
        p.wait("scalar", cV, cV.n)
        ns_ = p.sop(lambda e: e.activation(smc(RSTD), smc(TPA), AF.Sqrt, bias=0.0, scale=1.0))
        p.wait("vector", cS, ns_)

    n_tables_done = cV.n
    xcons = [n_tables_done, n_tables_done]
    evm = [0, 0]
    qevm = [0, 0]
    tevm = [0, 0]

    for chunk in range(2):
        for tl in range(16):
            j = chunk * 16 + tl
            bi = tl % 2
            p.wait("sync", cV, xcons[bi])
            nld = p.dma("sync", xbuf[bi], t_x[j * 128:(j + 1) * 128, :])
            p.wait("vector", dS, nld)
            ln_stats(xbuf[bi], D)
            p.vop(lambda e, b_=bi: e.tensor_scalar(a1f, xbuf[b_], smc(NEGM), smc(RSTD), ALU.add, ALU.mult), inc=False)
            nv_ = p.vop(lambda e: e.tensor_copy(a1bf, a1f))
            xcons[bi] = nv_
            p.wait("tensor", cV, nv_)
            for dc in range(DC):
                pi = dc % 2
                p.wait("tensor", cV, evm[pi])
                nt_ = p.top(lambda e, dc=dc, pi=pi: e.transpose(
                    psTb[pi][:, :], a1bf[:, dc * 128:(dc + 1) * 128], sb_id))
                p.wait("vector", cT, nt_)
                evm[pi] = p.vop(lambda e, dc=dc, tl=tl, pi=pi: e.tensor_copy(
                    a1T(dc, tl), psTb[pi][:, :]))

        pb = mkprobe(f"a1T_c{chunk}", (128, 2048))
        if pb is not None:
            nv = p.vop(lambda e: e.nop())
            p.wait("gpsimd", cV, nv)
            p.dma("gpsimd", pb, bfA[:, A1T:A1T + 2048])

        wqkv_r = t_wqkv.rearrange("(c p) o -> p c o", p=128)
        oblks = list(range(12)) if chunk == 0 else list(range(4, 12))

        wcons = 0
        psr_prev = 0
        for obi, ob in enumerate(oblks):
            p.wait("gpsimd", cT, wcons)
            nwl = p.dma("gpsimd", w_sb[:, :, :], wqkv_r[:, :, ob * 512:(ob + 1) * 512])
            if True:
                p.wait("sync", cT, psr_prev)
                nbl = p.dma("sync", brow, t_bqkv[:, ob * 512:(ob + 1) * 512])
                p.wait("tensor", dS, nbl)
                nt_ = p.top(lambda e: e.matmul(psR[:, :], onesr[0:1, :], brow, start=True, stop=True))
                psr_prev = nt_
                p.wait("vector", cT, nt_)
                nbr = p.vop(lambda e: e.tensor_copy(b_rep, psR[:, :]))
            else:
                nbr = cV.n

            is_q, is_k = ob < 4, 4 <= ob < 8
            for tl in range(16):
                j = chunk * 16 + tl
                pi = tl % 2
                p.wait("tensor", dG, nwl)
                p.wait("tensor", cV, qevm[pi])
                for dc in range(DC):
                    ntm = p.top(lambda e, dc=dc, pi=pi, tl=tl: e.matmul(
                        psQ[pi][:, :], a1T(dc, tl), w_sb[:, dc, :],
                        start=(dc == 0), stop=(dc == DC - 1)), inc=(dc == DC - 1))
                wcons = ntm
                p.wait("vector", cT, ntm)
                p.wait("vector", cV, nbr)
                nv_ = p.vop(lambda e, pi=pi: e.scalar_tensor_tensor(
                    qkv_bf, psQ[pi][:, :], 1.0, b_rep, ALU.mult, ALU.add))
                qevm[pi] = nv_
                if is_q or is_k:
                    p.wait("scalar", cV, nv_)
                    for hh in range(4):
                        if is_q:
                            col = NRMQ + (ob * 4 + hh) * 16 + tl
                        else:
                            col = NRMK + ((ob - 4) * 4 + hh) * 32 + j
                        nsq = p.sop(lambda e, hh=hh, col=col: e.activation(
                            junkb[:, hh * 128:(hh + 1) * 128],
                            qkv_bf[:, hh * 128:(hh + 1) * 128], AF.Square,
                            bias=0.0, scale=1.0, accum_out=sm2[:, col:col + 1]), inc=(hh == 3))
                    for hh in range(4):
                        p.vop(lambda e, hh=hh, j=j: e.tensor_copy(
                            sb_Ar[:, hh * 128:hh * 128 + 64], sb_S[:, j * 64:(j + 1) * 64]), inc=False)
                        p.vop(lambda e, hh=hh, j=j: e.tensor_copy(
                            sb_Ar[:, hh * 128 + 64:(hh + 1) * 128], sb_S[:, j * 64:(j + 1) * 64]), inc=False)
                        p.vop(lambda e, hh=hh, j=j: e.tensor_scalar_mul(
                            sb_Br[:, hh * 128:hh * 128 + 64], sb_C[:, j * 64:(j + 1) * 64], -1.0), inc=False)
                        p.vop(lambda e, hh=hh, j=j: e.tensor_copy(
                            sb_Br[:, hh * 128 + 64:(hh + 1) * 128], sb_C[:, j * 64:(j + 1) * 64]), inc=False)
                    nab = p.vop(lambda e, j=j: e.tensor_copy(sb_Br[:, 64:128], sb_C[:, j * 64:(j + 1) * 64]))
                    p.wait("vector", cV, nab)
                    p.wait("vector", cS, nsq)
                    qv = qkv_bf.rearrange("p (h q) -> p h q", q=128)
                    rv = rp2.rearrange("p (h q) -> p h q", q=128)
                    bv = sb_Br.rearrange("p (h q) -> p h q", q=128)
                    p.vop(lambda e: e.tensor_tensor(t1r, qkv_bf, sb_Ar, ALU.mult), inc=False)
                    p.vop(lambda e: e.tensor_tensor(rv[:, :, 0:64], qv[:, :, 64:128], bv[:, :, 0:64], ALU.mult), inc=False)
                    p.vop(lambda e: e.tensor_tensor(rv[:, :, 64:128], qv[:, :, 0:64], bv[:, :, 64:128], ALU.mult), inc=False)
                    nv_ = p.vop(lambda e: e.tensor_tensor(qkv_bf, t1r, rp2, ALU.add))
                    qevm[pi] = nv_
                    p.wait("tensor", cV, nv_)
                    stg = QST if is_q else KST
                    for hh in range(4):
                        ti = hh % 2
                        p.wait("tensor", cV, tevm[ti])
                        nt2 = p.top(lambda e, hh=hh, ti=ti: e.transpose(
                            psTb[ti][:, :], qkv_bf[:, hh * 128:(hh + 1) * 128], sb_id))
                        p.wait("vector", cT, nt2)
                        tevm[ti] = p.vop(lambda e, hh=hh, ti=ti, stg=stg, tl=tl: e.tensor_copy(
                            bfA[:, stg + hh * 2048 + tl * 128:stg + hh * 2048 + (tl + 1) * 128],
                            psTb[ti][:, :]))
                else:
                    p.wait("gpsimd", cV, nv_)
                    nst = p.dma("gpsimd", t_v[j * 128:(j + 1) * 128, (ob - 8) * 512:(ob - 7) * 512], qkv_bf)
                    p.wait("vector", dG, nst)
            if is_q or is_k:
                nvf = p.vop(lambda e: e.nop())
                p.wait("gpsimd", cV, nvf)
                stg = QST if is_q else KST
                tsc = t_qT if is_q else t_kT
                hb = (ob - (0 if is_q else 4)) * 4
                cb = chunk * 2048
                width = 2048
                for hh in range(4):
                    hr = (hb + hh) * 128
                    p.dma("gpsimd", tsc[hr:hr + 128, cb:cb + width],
                          bfA[:, stg + hh * 2048:stg + hh * 2048 + width])
                p.wait("vector", dG, dG.n)

    psP.close()

    if PHASES < 3:
        return _finish_stub(nc, ctx, p, t_out, fA, probes)

    KTH, QTH, VP, PT, AOUT = 0, 8192, 12288, 20608, 32768
    aout = lambda jj: bfA[:, AOUT + jj * 2048:AOUT + (jj + 1) * 2048]

    psP3 = ExitStack()
    psS = [psP3.enter_context(nc.psum_tensor(f"psS{i}", [128, 512], F32)).ap() for i in range(2)]
    psPV = [psP3.enter_context(nc.psum_tensor(f"psPV{i}", [128, 129], F32)).ap() for i in range(4)]
    psBC = psP3.enter_context(nc.psum_tensor("psBC", [128, 16], F32)).ap()

    p.wait("vector", cS, cS.n)
    for h in range(H):
        p.vop(lambda e, h=h: e.tensor_reduce(
            sm2[:, HMX + h:HMX + h + 1], sm2[:, NRMQ + h * 16:NRMQ + (h + 1) * 16], AX.X, ALU.max), inc=False)
        p.vop(lambda e, h=h: e.tensor_reduce(
            sm2[:, HMX + 16 + h:HMX + 17 + h], sm2[:, NRMK + h * 32:NRMK + (h + 1) * 32], AX.X, ALU.max), inc=False)
    nv = p.vop(lambda e: e.nop())
    p.wait("gpsimd", cV, nv)
    n1 = p.dma("gpsimd", t_nrm, sm2[:, HMX:HMX + 32])
    p.wait("sync", dG, n1)
    nrm_rows = fA[0:32, 0:128]
    n2 = p.dma("sync", nrm_rows, t_nrm.rearrange("p c -> c p"), allow_slow_non_contiguous=True)
    p.wait("vector", dS, n2)
    mx32 = fA[0:32, 128:129]
    p.vchain([lambda e: e.tensor_reduce(mx32, nrm_rows, AX.X, ALU.max)])
    nv = p.vop(lambda e: e.nop())
    p.wait("gpsimd", cV, nv)
    n3 = p.dma("gpsimd", t_nrm2, mx32)
    p.wait("sync", dG, n3)
    mrow = fA[0:1, 192:224]
    n4 = p.dma("sync", mrow, t_nrm2.rearrange("a b -> b a"), allow_slow_non_contiguous=True)
    p.wait("vector", dS, n4)
    m2row = fA[0:1, 224:240]
    p.vchain([lambda e: e.tensor_tensor(m2row, fA[0:1, 192:208], fA[0:1, 208:224], ALU.mult)])
    p.wait("scalar", cV, cV.n)
    mrow2 = fA[0:1, 240:256]
    ns = p.sop(lambda e: e.activation(mrow2, m2row, AF.Sqrt, bias=0.0, scale=1.0 / 128.0))
    p.wait("tensor", cS, ns)
    nt = p.top(lambda e: e.matmul(psBC[:, :], onesr[0:1, :], mrow2, start=True, stop=True))
    p.wait("vector", cT, nt)
    nv = p.vop(lambda e: e.tensor_copy(sm2[:, MBC:MBC + 16], psBC[:, :]))
    p.wait("vector", cV, nv)
    for h in range(H):
        p.vop(lambda e, h=h: e.tensor_scalar(
            sm2[:, BIAS + h * 32:BIAS + (h + 1) * 32], sm2[:, KBC:KBC + 32],
            sm2[:, MBC + h:MBC + h + 1], None, ALU.subtract), inc=False)
    nbias = p.vop(lambda e: e.nop())
    p.wait("scalar", cV, nbias)
    p.wait("tensor", cV, nbias)

    pb = mkprobe("Mbc", (128, 16))
    if pb is not None:
        p.wait("gpsimd", cV, nbias)
        p.dma("gpsimd", pb, sm2[:, MBC:MBC + 16])

    p.wait("sync", dG, dG.n)

    ldm = [0, 0]
    vmem = [0, 0]
    sexp = [0, 0]
    pvm = [0, 0]
    evq = 0
    vplus = lambda hb: bfA[:, VP + hb * 4128:VP + (hb + 1) * 4128].rearrange("p (k c) -> p k c", c=129)
    for h in range(H):
        hb = h % 2
        p.wait("sync", cT, ldm[hb])
        nk = p.dma("sync", bfA[:, KTH + hb * 4096:KTH + (hb + 1) * 4096], t_kT[h * 128:(h + 1) * 128, :])
        nq = p.dma("sync", bfA[:, QTH + hb * 2048:QTH + (hb + 1) * 2048], t_qT[h * 128:(h + 1) * 128, :])
        p.wait("vector", cT, vmem[hb])
        nms = p.vop(lambda e, hb=hb: e.memset(vplus(hb)[:, :, 128:129], 1.0))
        p.wait("sync", cV, nms)
        nvl = p.dma("sync", vplus(hb)[:, :, 0:128],
                    t_v.rearrange("(k p) d -> p k d", p=128)[:, :, h * 128:(h + 1) * 128])
        kT_h = bfA[:, KTH + hb * 4096:KTH + (hb + 1) * 4096]
        qT_h = bfA[:, QTH + hb * 2048:QTH + (hb + 1) * 2048]
        first_pe = True
        for cq in range(NCQ):
            klist = list(range(0, 4 * cq + 4)) + list(range(16, 32))
            nk_t = len(klist)
            for ki, kt in enumerate(klist):
                sb_ = ki % 2
                if first_pe:
                    p.wait("tensor", dS, nvl)
                    first_pe = False
                p.wait("tensor", cS, sexp[sb_])
                nmt = p.top(lambda e, kt=kt, cq=cq, sb_=sb_, kT_h=kT_h, qT_h=qT_h: e.matmul(
                    psS[sb_][:, :], kT_h[:, kt * 128:(kt + 1) * 128],
                    qT_h[:, cq * 512:(cq + 1) * 512], start=True, stop=True))
                diag = 4 * cq <= kt <= 4 * cq + 3
                if diag:
                    z = kt - 4 * cq
                    p.wait("vector", cT, nmt)
                    if z > 0:
                        p.vop(lambda e, z=z, sb_=sb_: e.tensor_scalar_add(
                            psS[sb_][:, 0:z * 128], psS[sb_][:, 0:z * 128], NEG_BIG), inc=False)
                    nmk = p.vop(lambda e, z=z, sb_=sb_: e.tensor_tensor(
                        psS[sb_][:, z * 128:(z + 1) * 128], psS[sb_][:, z * 128:(z + 1) * 128],
                        sb_tri, ALU.add))
                    p.wait("scalar", cV, nmk)
                p.wait("scalar", cT, nmt)
                p.wait("scalar", cT, pvm[sb_])
                nse = p.sop(lambda e, sb_=sb_, h=h, kt=kt: e.activation(
                    bfA[:, PT + sb_ * 512:PT + (sb_ + 1) * 512], psS[sb_][:, :], AF.Exp,
                    bias=sm2[:, BIAS + h * 32 + kt:BIAS + h * 32 + kt + 1], scale=ISQHD))
                sexp[sb_] = nse
                p.wait("tensor", cS, nse)
                if ki == 0:
                    p.wait("tensor", cV, evq)
                for sub in range(4):
                    pvm[sb_] = p.top(lambda e, sub=sub, sb_=sb_, kt=kt, hb=hb, ki=ki, last=(ki == nk_t - 1): e.matmul(
                        psPV[sub][:, :], bfA[:, PT + sb_ * 512 + sub * 128:PT + sb_ * 512 + (sub + 1) * 128],
                        vplus(hb)[:, kt, :], start=(ki == 0), stop=last,
                        skip_group_check=True), inc=(sub == 3))
            ldm[hb] = pvm[0] if pvm[0] > pvm[1] else pvm[1]
            vmem[hb] = ldm[hb]
            p.wait("vector", cT, max(pvm))
            for sub in range(4):
                jj = cq * 4 + sub
                p.vchain([lambda e, sub=sub: e.reciprocal(smc(TPA), psPV[sub][:, 128:129])])
                evq = p.vop(lambda e, sub=sub, jj=jj, h=h: e.tensor_scalar(
                    aout(jj)[:, h * 128:(h + 1) * 128], psPV[sub][:, 0:128],
                    smc(TPA), None, ALU.mult))

    psP3.close()

    pb = mkprobe("aout", (128, 2048))
    if pb is not None:
        nv = p.vop(lambda e: e.tensor_copy(fA[:, 0:2048], aout(0)))
        p.wait("gpsimd", cV, nv)
        p.dma("gpsimd", pb, fA[:, 0:2048])

    if PHASES < 4:
        return _finish_stub(nc, ctx, p, t_out, fA, probes)

    ATT = 0
    attT = lambda dc, jj: bfA[:, ATT + dc * 2048 + jj * 128:ATT + dc * 2048 + (jj + 1) * 128]

    psP4 = ExitStack()
    psTb4 = [psP4.enter_context(nc.psum_tensor(f"psT4{i}", [128, 128], BF16)).ap() for i in range(2)]
    psO = [psP4.enter_context(nc.psum_tensor(f"psO{i}", [128, 512], F32)).ap() for i in range(2)]
    psR4 = psP4.enter_context(nc.psum_tensor("psR4", [128, 512], F32)).ap()

    evm4 = [0, 0]
    for jj in range(NTO):
        for dc in range(DC):
            ti = dc % 2
            p.wait("tensor", cV, evm4[ti])
            nt_ = p.top(lambda e, jj=jj, dc=dc, ti=ti: e.transpose(
                psTb4[ti][:, :], aout(jj)[:, dc * 128:(dc + 1) * 128], sb_id))
            p.wait("vector", cT, nt_)
            evm4[ti] = p.vop(lambda e, jj=jj, dc=dc, ti=ti: e.tensor_copy(
                attT(dc, jj), psTb4[ti][:, :]))
    p.wait("gpsimd", cV, max(evm4))
    WO = AOUT
    wo_sb = bfA[:, WO:WO + 32768].rearrange("p (c o) -> p c o", o=2048)
    nwo = p.dma("gpsimd", wo_sb[:, :, :], t_wo.rearrange("(c p) o -> p c o", p=128))
    bo_rep = fA[:, 4096:6144]
    psr_prev = 0
    for obx in range(4):
        p.wait("sync", cT, psr_prev)
        nbl = p.dma("sync", brow, t_bo[:, obx * 512:(obx + 1) * 512])
        p.wait("tensor", dS, nbl)
        nt_ = p.top(lambda e: e.matmul(psR4[:, :], onesr[0:1, :], brow, start=True, stop=True))
        psr_prev = nt_
        p.wait("vector", cT, nt_)
        nv = p.vop(lambda e, obx=obx: e.tensor_copy(bo_rep[:, obx * 512:(obx + 1) * 512], psR4[:, :]))
        p.wait("tensor", cV, nv)
    p.wait("tensor", dG, nwo)

    proj = fA[:, 0:2048]
    x1t = fA[:, 2048:4096]
    a2t = fA[:, 0:2048]
    qc1 = fA[:, 8192:10240]
    qc2 = fA[:, 10240:12288]

    oev = 0
    oevs = [0, 0]
    qc1_free = 0
    for jj in range(NTO):
        p.wait("tensor", cV, oev)
        for obx in range(4):
            pi = obx % 2
            p.wait("tensor", cV, oevs[pi])
            for dc in range(DC):
                ntm = p.top(lambda e, dc=dc, jj=jj, obx=obx, pi=pi: e.matmul(
                    psO[pi][:, :], attT(dc, jj), wo_sb[:, dc, obx * 512:(obx + 1) * 512],
                    start=(dc == 0), stop=(dc == DC - 1)), inc=(dc == DC - 1))
            p.wait("vector", cT, ntm)
            nv = p.vop(lambda e, obx=obx, pi=pi: e.scalar_tensor_tensor(
                proj[:, obx * 512:(obx + 1) * 512], psO[pi][:, :], 1.0,
                bo_rep[:, obx * 512:(obx + 1) * 512], ALU.mult, ALU.add))
            oevs[pi] = nv
        oev = nv
        ln_stats(proj, D, junk=qc2)
        p.wait("sync", cV, qc1_free)
        nxl = p.dma("sync", qc1, t_x[jj * 128:(jj + 1) * 128, :])
        p.vop(lambda e: e.tensor_scalar(qc2, proj, smc(NEGM), smc(RSTD), ALU.add, ALU.mult), inc=False)
        p.wait("vector", dS, nxl)
        nv = p.vop(lambda e: e.tensor_tensor(x1t, qc1, qc2, ALU.add))
        p.wait("gpsimd", cV, nv)
        nst = p.dma("gpsimd", t_x1[jj * 128:(jj + 1) * 128, :], x1t)
        ln_stats(x1t, D, junk=qc2)
        nv = p.vop(lambda e: e.tensor_scalar(a2t, x1t, smc(NEGM), smc(RSTD), ALU.add, ALU.mult))
        p.wait("vector", cV, nv)
        p.vchain([
            lambda e: e.tensor_reduce(smc(MXC), a2t, AX.X, ALU.max, apply_absolute_value=True),
            lambda e: e.tensor_scalar_max(smc(MXC), smc(MXC), EPS),
            lambda e: e.reciprocal(smc(TPA), smc(MXC)),
            lambda e: e.tensor_scalar_mul(smc(SCL), smc(TPA), 128.0),
            lambda e: e.tensor_scalar_mul(smc(SINV), smc(MXC), 1.0 / 128.0),
            lambda e: e.tensor_scalar(smc(NPS), smc(SCL), smc(NP1), None, ALU.mult),
        ])
        p.vop(lambda e: e.tensor_scalar(qc1, a2t, smc(SCL), C23, ALU.mult, ALU.add), inc=False)
        p.vop(lambda e: e.tensor_scalar(qc2, a2t, smc(SCL), C52, ALU.mult, ALU.add), inc=False)
        p.vop(lambda e: e.scalar_tensor_tensor(qc2, qc2, C52, qc1, ALU.subtract, ALU.add), inc=False)
        nqf = p.vop(lambda e: e.scalar_tensor_tensor(qc1, a2t, smc(NPS), qc2, ALU.mult, ALU.add))
        qc1_free = nqf
        p.vop(lambda e: e.tensor_scalar(qc1, qc1, C23, 127.0, ALU.subtract, ALU.min), inc=False)
        dst = bfA[:, ATT:ATT + 32768].rearrange("p (c t) -> p c t", t=2048)[:, :, jj * 128:(jj + 1) * 128]
        src3 = qc1.rearrange("p (c t) -> p c t", t=128)
        nv = p.vop(lambda e, dst=dst, src3=src3: e.tensor_scalar(dst, src3, smc(SINV), None, ALU.mult))
        p.wait("tensor", cV, nv)
        for dc in range(DC):
            ti = dc % 2
            p.wait("tensor", cV, evm4[ti])
            nt_ = p.top(lambda e, jj=jj, dc=dc, ti=ti: e.transpose(
                psTb4[ti][:, :], attT(dc, jj), sb_id))
            p.wait("vector", cT, nt_)
            evm4[ti] = p.vop(lambda e, jj=jj, dc=dc, ti=ti: e.tensor_copy(
                attT(dc, jj), psTb4[ti][:, :]))
        nv = p.vop(lambda e: e.nop())
        p.wait("sync", cV, nv)
        nst = p.dma("sync",
                    t_a2qT.rearrange("(c p) t -> p c t", p=128)[:, :, jj * 128:(jj + 1) * 128],
                    bfA[:, ATT:ATT + 32768].rearrange("p (c t) -> p c t", t=2048)[:, :, jj * 128:(jj + 1) * 128])
        p.wait("vector", dS, nst)

    psP4.close()

    pb = mkprobe("x1", (128, 2048))
    if pb is not None:
        p.wait("sync", dG, dG.n)
        n_ = p.dma("sync", qc1, t_x1[0:128, :])
        p.wait("gpsimd", dS, n_)
        p.dma("gpsimd", pb, qc1)

    if PHASES < 5:
        return _finish_stub(nc, ctx, p, t_out, fA, probes)

    psP5 = ExitStack()
    ps11 = psP5.enter_context(nc.psum_tensor("ps11", [1, 1], F32)).ap()
    psBC5 = psP5.enter_context(nc.psum_tensor("psBC5", [128, 1], F32)).ap()
    psT5 = [psP5.enter_context(nc.psum_tensor(f"psT5{i}", [128, 128], BF16)).ap() for i in range(2)]

    p.wait("sync", cV, cV.n)
    p.wait("sync", cT, cT.n)
    p.wait("scalar", cV, cV.n)
    p.wait("vector", cT, cT.n)
    onescol = fA[:, 6144:6145]
    nv = p.vop(lambda e: e.memset(onescol, 1.0))
    p.wait("tensor", cV, nv)
    wtiles = [fA[:, 0:2048], fA[:, 2048:4096]]
    absout = fA[:, 4096:6144]
    accall = sm2[:, 0:64]
    acc1 = sm[:, 32:33]
    val1 = fA[0:1, 6148:6149]
    q1b = bfA[:, 0:2048]
    q2b = bfA[:, 2048:4096]
    q3b = bfA[:, 4096:6144]
    accb = [bfA[:, 8192:16384], bfA[:, 16384:24576]]
    mtiles = [fA[:, 6784:8832], fA[:, 8832:10880]]
    tev5 = [0, 0]
    accst = [0, 0]
    scons = [0, 0]
    vcons = [0, 0]
    p.wait("scalar", dS, dS.n)
    p.wait("gpsimd", cV, cV.n)
    p.wait("gpsimd", cT, cT.n)
    p.wait("gpsimd", dS, dS.n)

    for mat, (t_w, t_wqT) in enumerate(((t_wg, t_wgqT), (t_wu, t_wuqT), (t_wd, t_wdqT))):
        nrows = I if mat < 2 else D
        ncols = D if mat < 2 else I
        ncc = ncols // 2048
        if mat < 2:
            slabs = [(rt, cc) for rt in range(nrows // 128) for cc in range(ncc)]
        else:
            slabs = [(rt, cc) for cc in range(ncc) for rt in range(nrows // 128)]
        if mat > 0:
            p.wait("scalar", cV, red_guard)
        for si, (rt, cc) in enumerate(slabs):
            bi = si % 2
            p.wait("gpsimd", cS, scons[bi])
            nld = p.dma("gpsimd", mtiles[bi],
                        t_w[rt * 128:(rt + 1) * 128, cc * 2048:(cc + 1) * 2048])
            p.wait("scalar", dG, nld)
            scons[bi] = p.sop(lambda e, bi=bi, si=si: e.activation(
                absout, mtiles[bi], AF.Abs, bias=0.0, scale=1.0,
                accum_out=sm2[:, si:si + 1]))
        p.wait("vector", cS, max(scons))
        p.vchain([lambda e, n=len(slabs): e.tensor_reduce(acc1, sm2[:, 0:n], AX.X, ALU.add)])
        p.vchain([lambda e, mat=mat: e.tensor_copy(sm[:, 40 + mat:41 + mat], acc1)])
        if mat == 0:
            p.vchain([lambda e: e.tensor_copy(sm2[:, 64:128], sm2[:, 0:64])])
        red_guard = cV.n
        p.wait("tensor", cV, cV.n)
        nt_ = p.top(lambda e: e.matmul(ps11[:, :], acc1, onescol, start=True, stop=True))
        p.wait("vector", cT, nt_)
        p.vchain([
            lambda e: e.tensor_copy(val1, ps11[:, :]),
            lambda e, nr=nrows, nc_=ncols: e.tensor_scalar_mul(val1, val1, 1.0 / (nr * nc_)),
            lambda e: e.tensor_scalar_max(val1, val1, EPS),
        ])
        p.wait("tensor", cV, cV.n)
        nt_ = p.top(lambda e: e.matmul(psBC5[:, :], onesr[0:1, :], val1, start=True, stop=True))
        p.wait("vector", cT, nt_)
        mc = MG if mat == 0 else (MU if mat == 1 else THC)
        p.vchain([
            lambda e, mc=mc: e.tensor_copy(sm[:, mc:mc + 1], psBC5[:, :]),
            lambda e, mc=mc: e.tensor_scalar_mul(smc(TPA), sm[:, mc:mc + 1], 0.5),
            lambda e: e.tensor_scalar_mul(smc(TPB), smc(TPA), -1.0),
        ])
        for si, (rt, cc) in enumerate(slabs):
            bi = si % 2
            gi = (si // 4) % 2
            rtl = si % 4
            p.wait("sync", cV, vcons[bi])
            nld = p.dma("sync", wtiles[bi],
                        t_w[rt * 128:(rt + 1) * 128, cc * 2048:(cc + 1) * 2048])
            p.wait("vector", dS, nld)
            p.vop(lambda e, bi=bi: e.tensor_scalar(
                q2b, wtiles[bi], smc(TPB), None, ALU.is_lt), inc=False)
            nv = p.vop(lambda e, bi=bi: e.scalar_tensor_tensor(
                q3b, wtiles[bi], smc(TPA), q2b, ALU.is_gt, ALU.subtract))
            vcons[bi] = nv
            p.wait("tensor", cV, nv)
            if rtl == 0:
                p.wait("scalar", dG, accst[gi])
            for dc in range(DC):
                ti = dc % 2
                p.wait("tensor", cS, tev5[ti])
                ntt = p.top(lambda e, ti=ti, dc=dc: e.transpose(
                    psT5[ti][:, :], q3b[:, dc * 128:(dc + 1) * 128], sb_id))
                p.wait("scalar", cT, ntt)
                tev5[ti] = p.sop(lambda e, ti=ti, dc=dc, gi=gi, rtl=rtl: e.activation(
                    accb[gi][:, dc * 512 + rtl * 128:dc * 512 + (rtl + 1) * 128],
                    psT5[ti][:, :], AF.Copy, bias=0.0, scale=1.0))
            p.wait("vector", cT, ntt)
            if rtl == 3:
                p.wait("gpsimd", cS, max(tev5))
                rt0 = rt - 3
                if mat < 2:
                    dstp = t_wqT.rearrange("(g p) i -> p g i", p=128)[:, :, rt0 * 128:rt0 * 128 + 512]
                else:
                    dstp = t_wqT.rearrange("(g p) d -> p g d", p=128)[:, cc * 16:(cc + 1) * 16, rt0 * 128:rt0 * 128 + 512]
                accst[gi] = p.dma("gpsimd", dstp,
                                  accb[gi].rearrange("p (g k) -> p g k", k=512))

    psP5.close()

    if PHASES < 6:
        return _finish_stub(nc, ctx, p, t_out, fA, probes)

    A2Q, WGB, WUB, TROT, DSTR, INTER, SG, YB = 0, 8192, 16384, 8192, 10240, 24576, 57344, 57344
    a2q_sl = bfA[:, A2Q:A2Q + 8192].rearrange("p (c t) -> p c t", t=512)
    interc = lambda c: bfA[:, INTER + c * 512:INTER + (c + 1) * 512]

    x1sl = fA[:, 0:2048]
    chn = fA[:, 2048:4096]
    rows = lambda ts_: fA[:, 4096 + ts_ * 2048:4096 + (ts_ + 1) * 2048]

    p.wait("sync", dG, dG.n)
    p.wait("scalar", dG, dG.n)

    tt_guard = 0
    ps_guard = 0
    for tt in range(NTT):
        psA_ = ExitStack()
        psGs = [psA_.enter_context(nc.psum_tensor(f"psG{tt}_{i}", [128, 512], F32)).ap() for i in range(2)]
        psUs = [psA_.enter_context(nc.psum_tensor(f"psU{tt}_{i}", [128, 512], F32)).ap() for i in range(2)]
        p.wait("sync", cT, tt_guard)
        nld = p.dma("sync", a2q_sl[:, :, :],
                    t_a2qT.rearrange("(c p) t -> p c t", p=128)[:, :, tt * 512:(tt + 1) * 512])
        p.wait("tensor", dS, nld)
        p.wait("vector", cT, tt_guard)
        silu_sg = bfA[:, SG:SG + 512]
        guevs = [ps_guard, ps_guard]
        nsgs = [0, 0]
        cpar = 0
        for ib in range(16):
            p.dma("sync", bfA[:, WGB:WGB + 8192].rearrange("p (c k) -> p c k", k=512),
                  t_wgqT.rearrange("(c p) i -> p c i", p=128)[:, :, ib * 512:(ib + 1) * 512])
            nlu = p.dma("sync", bfA[:, WUB:WUB + 8192].rearrange("p (c k) -> p c k", k=512),
                        t_wuqT.rearrange("(c p) i -> p c i", p=128)[:, :, ib * 512:(ib + 1) * 512])
            p.wait("tensor", dS, nlu)
            for it in range(4):
                c = ib * 4 + it
                pr = cpar % 2
                cpar += 1
                psG, psU = psGs[pr], psUs[pr]
                sgb = bfA[:, SG + pr * 512:SG + (pr + 1) * 512]
                p.wait("tensor", cV, guevs[pr])
                p.wait("tensor", cS, nsgs[pr])
                for dc in range(DC):
                    ng = p.top(lambda e, dc=dc, it=it, psG=psG: e.matmul(
                        psG[:, :], bfA[:, WGB + dc * 512 + it * 128:WGB + dc * 512 + (it + 1) * 128],
                        a2q_sl[:, dc, :], start=(dc == 0), stop=(dc == DC - 1)), inc=(dc == DC - 1))
                p.wait("scalar", cT, ng)
                nsg = p.sop(lambda e, psG=psG, sgb=sgb: e.activation(
                    sgb, psG[:, :], AF.Silu, bias=0.0, scale=sm[:, MG:MG + 1]))
                nsgs[pr] = nsg
                for dc in range(DC):
                    nu = p.top(lambda e, dc=dc, it=it, psU=psU: e.matmul(
                        psU[:, :], bfA[:, WUB + dc * 512 + it * 128:WUB + dc * 512 + (it + 1) * 128],
                        a2q_sl[:, dc, :], start=(dc == 0), stop=(dc == DC - 1)), inc=(dc == DC - 1))
                p.wait("vector", cT, nu)
                p.wait("vector", cS, nsg)
                guevs[pr] = p.vop(lambda e, c=c, psU=psU, sgb=sgb: e.scalar_tensor_tensor(
                    interc(c), psU[:, :], sm[:, MU:MU + 1], sgb, ALU.mult, ALU.mult))
            p.wait("sync", cT, nu)
        guev = max(guevs)
        psA_.close()

        psB_ = ExitStack()
        psF = [psB_.enter_context(nc.psum_tensor(f"psF{tt}_{i}", [128, 512], F32)).ap() for i in range(2)]
        fev = [guev, guev]
        fst = [0, 0, 0, 0]
        for c in range(IC):
            fi = c % 2
            ri = c % 4
            p.wait("tensor", cV, fev[fi])
            nt_ = p.top(lambda e, c=c, fi=fi: e.matmul(
                psF[fi][:, :], sb_h128, interc(c), start=True, stop=True))
            p.wait("vector", cT, nt_)
            p.wait("vector", dG, fst[ri])
            fev[fi] = p.vop(lambda e, c=c, fi=fi, ri=ri: e.tensor_copy(
                bfA[:, TROT + ri * 512:TROT + (ri + 1) * 512], psF[fi][:, :]))
            p.wait("gpsimd", cV, fev[fi])
            fst[ri] = p.dma("gpsimd", t_Tsc[c * 128:(c + 1) * 128, :],
                            bfA[:, TROT + ri * 512:TROT + (ri + 1) * 512])
        psB_.close()

        psC_ = ExitStack()
        psY = psC_.enter_context(nc.psum_tensor(f"psY{tt}", [128, 512], F32)).ap()
        p.wait("sync", dG, dG.n)
        yv = bfA[:, YB:YB + 8192]
        y3 = yv.rearrange("p (a b) -> p a b", b=128)
        Tsc_r = t_Tsc.rearrange("(c s) t -> c s t", s=128)
        trcons = [0, 0]
        yev = max(fev)
        for tb in range(4):
            for g in range(16):
                gi = g % 2
                trh = sb_Tr[0:IC, gi * 1024:(gi + 1) * 1024]
                p.wait("sync", cT, trcons[gi])
                nld = p.dma("sync", trh.rearrange("c (s t) -> c s t", t=128),
                            Tsc_r[:, g * 8:(g + 1) * 8, tb * 128:(tb + 1) * 128])
                p.wait("tensor", dS, nld)
                p.wait("tensor", cV, yev)
                for s_ in range(8):
                    ntm = p.top(lambda e, s_=s_, gi=gi: e.matmul(
                        psY[:, s_ * 64:(s_ + 1) * 64],
                        sb_Tr[0:IC, gi * 1024 + s_ * 128:gi * 1024 + (s_ + 1) * 128],
                        sb_hic, start=True, stop=True), inc=(s_ == 7))
                trcons[gi] = ntm
                p.wait("vector", cT, ntm)
                yev = p.vop(lambda e, g=g: e.tensor_scalar_mul(
                    y3[:, :, g * 8:(g + 1) * 8],
                    psY.rearrange("p (s c) -> p c s", c=64), FWHT_NORM))
            p.wait("vector", cV, yev)
            p.vchain([
                lambda e: e.tensor_reduce(smc(MXC), yv, AX.X, ALU.max, apply_absolute_value=True),
                lambda e: e.tensor_scalar_max(smc(MXC), smc(MXC), EPS),
                lambda e: e.reciprocal(smc(TPA), smc(MXC)),
                lambda e: e.tensor_scalar_mul(smc(SCL), smc(TPA), 128.0),
                lambda e: e.tensor_scalar_mul(smc(SINV), smc(MXC), 1.0 / 128.0),
                lambda e: e.tensor_scalar(smc(NPS), smc(SCL), smc(NP1), None, ALU.mult),
            ])
            for qk in range(4):
                ysl = yv[:, qk * 2048:(qk + 1) * 2048]
                p.vop(lambda e, ysl=ysl: e.tensor_scalar(x1sl, ysl, smc(SCL), C23, ALU.mult, ALU.add), inc=False)
                p.vop(lambda e, ysl=ysl: e.tensor_scalar(chn, ysl, smc(SCL), C52, ALU.mult, ALU.add), inc=False)
                p.vop(lambda e: e.scalar_tensor_tensor(chn, chn, C52, x1sl, ALU.subtract, ALU.add), inc=False)
                p.vop(lambda e, ysl=ysl: e.scalar_tensor_tensor(x1sl, ysl, smc(NPS), chn, ALU.mult, ALU.add), inc=False)
                p.vop(lambda e: e.tensor_scalar(x1sl, x1sl, C23, 127.0, ALU.subtract, ALU.min), inc=False)
                nv = p.vop(lambda e, ysl=ysl: e.tensor_scalar(ysl, x1sl, smc(SINV), None, ALU.mult))
            p.wait("gpsimd", cV, nv)
            nst = p.dma("gpsimd", t_a3[tb * 128:(tb + 1) * 128, :], yv)
            p.wait("vector", dG, nst)
        psC_.close()

        psD_ = ExitStack()
        psDn = [psD_.enter_context(nc.psum_tensor(f"psD{tt}_{i}", [128, 512], F32)).ap() for i in range(4)]
        p.wait("sync", dG, dG.n)
        p.wait("sync", cT, cT.n)
        a3sl = lambda c: bfA[:, INTER + c * 512:INTER + (c + 1) * 512]
        for c in range(IC):
            p.dma("sync", a3sl(c), t_a3[:, c * 128:(c + 1) * 128], transpose=True)
        wdbB = [bfA[:, 12288:16384], bfA[:, 16384:20480]]
        dcons = [0, 0]
        dnev = cV.n
        for ob in range(4):
            p.wait("tensor", cV, dnev)
            for g8 in range(8):
                bi = g8 % 2
                p.wait("sync", cT, dcons[bi])
                nld = p.dma("sync", wdbB[bi].rearrange("p (c k) -> p c k", k=512),
                            t_wdqT.rearrange("(c p) d -> p c d", p=128)[:, g8 * 8:(g8 + 1) * 8, ob * 512:(ob + 1) * 512])
                p.wait("tensor", dS, nld)
                if g8 == 0:
                    p.wait("tensor", cV, dnev)
                for i8 in range(8):
                    ic = g8 * 8 + i8
                    for ts_ in range(4):
                        dcons[bi] = p.top(lambda e, ts_=ts_, bi=bi, ic=ic, i8=i8: e.matmul(
                            psDn[ts_][:, :], a3sl(ic)[:, ts_ * 128:(ts_ + 1) * 128],
                            wdbB[bi][:, i8 * 512:(i8 + 1) * 512],
                            start=(ic == 0), stop=(ic == IC - 1), skip_group_check=True),
                            inc=(ts_ == 3 and i8 == 7))
            p.wait("vector", cT, max(dcons))
            for ts_ in range(4):
                dnev = p.vop(lambda e, ts_=ts_, ob=ob: e.tensor_scalar(
                    rows(ts_)[:, ob * 512:(ob + 1) * 512], psDn[ts_][:, :],
                    sm[:, THC:THC + 1], None, ALU.mult))
        p.wait("tensor", cV, dnev)
        tt_guard = max(dcons)
        ps_guard = dnev
        psD_.close()

        for ts_ in range(4):
            jj = tt * 4 + ts_
            ln_stats(rows(ts_), D)
            p.wait("sync", dG, dG.n)
            nxl = p.dma("sync", x1sl, t_x1[jj * 128:(jj + 1) * 128, :])
            nv = p.vop(lambda e, ts_=ts_: e.tensor_scalar(
                chn, rows(ts_), smc(NEGM), smc(RSTD), ALU.add, ALU.mult))
            pbc = mkprobe(f"chn{jj}", (128, 2048))
            if pbc is not None:
                p.wait("gpsimd", cV, nv)
                p.dma("gpsimd", pbc, chn)
            p.wait("vector", dS, nxl)
            nv = p.vop(lambda e: e.tensor_tensor(x1sl, x1sl, chn, ALU.add))
            p.wait("gpsimd", cV, nv)
            p.dma("gpsimd", t_out[jj * 128:(jj + 1) * 128, :], x1sl)
            p.wait("vector", dG, dG.n)

    return _finish(nc, ctx, p, probes)


def _finish_stub(nc, ctx, p, t_out, fA, probes):
    nv = p.vop(lambda e: e.memset(fA[:, 0:2048], 0.0))
    p.wait("gpsimd", p.cV, nv)
    for j in range(NTO):
        p.dma("gpsimd", t_out[j * 128:(j + 1) * 128, :], fA[:, 0:2048])
    return _finish(nc, ctx, p, probes)


def _finish(nc, ctx, p, probes):
    p.wait("gpsimd", p.dG, p.dG.n)
    p.wait("gpsimd", p.dS, p.dS.n)
    blk = ctx.enter_context(nc.Block())
    for eng in ENGS:
        ops = p.ops[eng]
        getattr(blk, eng)(lambda e, ops=ops: [f(e) for f in ops])
    ctx.close()
    return nc, probes


def _stage(inputs):
    x = np.ascontiguousarray(np.asarray(inputs["L0_attn_x"], dtype=np.float32))
    perm = _qkv_perm_cached()
    wqkv = np.ascontiguousarray(np.asarray(inputs["L0_attn_w_qkv"], np.float32)[:, perm])
    bqkv = np.ascontiguousarray(np.asarray(inputs["L0_attn_b_qkv"], np.float32)[perm][None, :])
    wo = np.ascontiguousarray(np.asarray(inputs["L0_attn_w_o"], np.float32))
    bo = np.ascontiguousarray(np.asarray(inputs["L0_attn_b_o"], np.float32)[None, :])
    wg = np.ascontiguousarray(np.asarray(inputs["L0_ffn_w_gate"], np.float32))
    wu = np.ascontiguousarray(np.asarray(inputs["L0_ffn_w_up"], np.float32))
    wd = np.ascontiguousarray(np.asarray(inputs["L0_ffn_w_down"], np.float32))

    ident = np.eye(128, dtype=BFD)
    h128 = _had(128).astype(BFD)
    hic = _had(IC).astype(BFD)
    pp = np.arange(128)[:, None]
    fl = np.arange(128)[None, :]
    tri = np.where(pp > fl, np.float32(NEG_BIG), np.float32(0.0)).astype(BFD)
    invf32 = (1.0 / (10000.0 ** (np.arange(0, 128, 2).astype(np.float32) / 128))).astype(np.float32)

    maps = []
    for c in range(8):
        b, half = divmod(c, 2)
        if half == 0:
            xp = x[b]
            pos = np.arange(S, dtype=np.float32)
            kb = np.concatenate([np.zeros(TO, np.float32), np.full(TA - TO, NEG_BIG, np.float32)])
        else:
            xp = np.concatenate([x[b, TO:], x[b, :TO]], axis=0)
            pos = np.concatenate([np.arange(TO, TA), np.arange(0, TO)]).astype(np.float32)
            kb = np.zeros(TA, np.float32)
        ang = (pos.astype(np.float32)[:, None] * invf32[None, :]).astype(np.float32)
        sint = np.sin(ang).astype(np.float32)
        cost = np.cos(ang).astype(np.float32)
        sint_d = np.ascontiguousarray(sint.reshape(NTA, 128, 64).transpose(1, 0, 2).reshape(128, NTA * 64))
        cost_d = np.ascontiguousarray(cost.reshape(NTA, 128, 64).transpose(1, 0, 2).reshape(128, NTA * 64))
        maps.append({
            "x": np.ascontiguousarray(xp),
            "pos": np.ascontiguousarray(pos.reshape(NTA, 128).T),
            "kbias": np.ascontiguousarray(kb.reshape(NTA, 128).T),
            "wqkv": wqkv, "bqkv": bqkv, "wo": wo, "bo": bo,
            "wg": wg, "wu": wu, "wd": wd,
            "ident": ident, "h128": h128, "hic": hic, "tri": tri,
            "sint": sint_d, "cost": cost_d,
        })
    return maps


_QKV_PERM_C = None


def _qkv_perm_cached():
    global _QKV_PERM_C
    if _QKV_PERM_C is None:
        perm = np.zeros(3 * D, dtype=np.int64)
        j = np.arange(128)
        for h in range(H):
            perm[h * 128 + j] = h * 384 + j
            perm[D + h * 128 + j] = h * 384 + 128 + j
            perm[2 * D + h * 128 + j] = h * 384 + 256 + j
        _QKV_PERM_C = perm
    return _QKV_PERM_C


_BUILT = None


def kernel(**inputs):
    global _BUILT
    if _BUILT is None:
        _BUILT = build_program()
    nc, probes = _BUILT
    maps = _stage(inputs)
    res = run_bass_kernel_spmd(nc, maps, core_ids=list(range(8)))
    if PROBES:
        kernel.last_results = res.results
    out = np.zeros((B, S, D), np.float32)
    for c in range(8):
        b, half = divmod(c, 2)
        out[b, half * TO:(half + 1) * TO] = res.results[c]["out"]
    return out


# version 11
# speedup vs baseline: 1.2507x; 1.0411x over previous
import os
import math
import numpy as np
import ml_dtypes
from contextlib import ExitStack

import concourse.bass as bass
import concourse.mybir as mybir
from concourse.bass_utils import run_bass_kernel_spmd

BFD = ml_dtypes.bfloat16
F32 = mybir.dt.float32
BF16 = mybir.dt.bfloat16
AF = mybir.ActivationFunctionType
ALU = mybir.AluOpType
AX = mybir.AxisListType

B, S, D, H, I = 4, 4096, 2048, 16, 8192
NTO, NTA, DC, IC = 16, 32, 16, 64
TO, TA = NTO * 128, NTA * 128
NCQ, TQW = 4, 512
TT, NTT = 512, 4
EPS = 1e-5

C23 = 12582912.0
C52 = float(1.5 * 2 ** 52)
ISQHD = float(1.0 / math.sqrt(128.0))
FWHT_NORM = float(np.float32(1.0 / np.sqrt(8192.0)))
NEG_BIG = -1.0e30

_LN4 = math.log(10000.0) / 64.0
LN4_HI = float(np.float32(int(_LN4 * 2 ** 14) / 2 ** 14))
LN4_MID = float(np.float32(_LN4 - LN4_HI))
LN4_LO = float(np.float32(_LN4 - LN4_HI - float(np.float32(LN4_MID))))
_TPI = 2.0 * math.pi
TPI_HI = float(np.float32(int(_TPI * 2 ** 11) / 2 ** 11))
TPI_MID = float(np.float32(_TPI - TPI_HI))
TPI_LO = float(np.float32(_TPI - TPI_HI - float(np.float32(TPI_MID))))
INV2PI = float(1.0 / _TPI)
PI32 = float(np.float32(math.pi))
_HPI = math.pi / 2.0
HPI_HI = float(np.float32(int(_HPI * 2 ** 12) / 2 ** 12))
HPI_MID = float(np.float32(_HPI - HPI_HI))
HPI_LO = float(np.float32(_HPI - HPI_HI - float(np.float32(HPI_MID))))

PHASES = int(os.environ.get("MK_PHASES", "6"))
P2LIM = int(os.environ.get("MK_P2LIM", "99"))
SCOUT = os.environ.get("MK_SCOUT", "") == "1"
QD = int(os.environ.get("MK_QD", "0"))
PROBES = [s for s in os.environ.get("MK_PROBE", "").split(",") if s]

ENGS = ("sync", "gpsimd", "tensor", "vector", "scalar")


class Sem:
    __slots__ = ("h", "n")

    def __init__(self, ctx, nc, name):
        self.h = ctx.enter_context(nc.semaphore(name))
        self.n = 0


class Prog:
    def __init__(self, nc, ctx):
        self.nc = nc
        self.ctx = ctx
        self.ops = {e: [] for e in ENGS}
        self.dS = Sem(ctx, nc, "dS")
        self.dG = Sem(ctx, nc, "dG")
        self.cT = Sem(ctx, nc, "cT")
        self.cV = Sem(ctx, nc, "cV")
        self.cS = Sem(ctx, nc, "cS")
        self.cG = Sem(ctx, nc, "cG")
        self.dA = Sem(ctx, nc, "dA")

    def em(self, eng, fn):
        self.ops[eng].append(fn)

    def wait(self, eng, sem, n=None):
        v = sem.n if n is None else n
        if v <= 0:
            return
        self.em(eng, lambda e, s=sem.h, v=v: e.wait_ge(s, v))

    def dma(self, eng, out, in_, **kw):
        sem = {"sync": self.dS, "gpsimd": self.dG, "scalar": self.dA}[eng]
        sem.n += 16
        self.em(eng, lambda e, o=out, i=in_, s=sem.h, kw=kw:
                e.dma_start(out=o, in_=i, **kw).then_inc(s, 16))
        return sem.n

    def _op(self, eng, sem, fn, inc):
        if inc:
            sem.n += 1
            self.em(eng, lambda e, f=fn, s=sem.h: f(e).then_inc(s, 1))
        else:
            self.em(eng, lambda e, f=fn: f(e))
        return sem.n

    def top(self, fn, inc=True):
        return self._op("tensor", self.cT, fn, inc)

    def vop(self, fn, inc=True):
        return self._op("vector", self.cV, fn, inc)

    def sop(self, fn, inc=True):
        return self._op("scalar", self.cS, fn, inc)

    def gop(self, fn, inc=True):
        return self._op("gpsimd", self.cG, fn, inc)

    def vchain(self, fns):
        for fn in fns:
            n = self.vop(fn)
            self.wait("vector", self.cV, n)
        return self.cV.n


def _had(n):
    h = np.array([[1.0]])
    while h.shape[0] < n:
        h = np.block([[h, h], [h, -h]])
    return h


def build_program():
    nc = bass.Bass("TRN2", target_bir_lowering=False, debug=False,
                   detect_race_conditions=False)
    ctx = ExitStack()

    t_x = nc.dram_tensor("x", [TA, D], F32, kind="ExternalInput").ap()
    t_pos = nc.dram_tensor("pos", [128, NTA], F32, kind="ExternalInput").ap()
    t_kb = nc.dram_tensor("kbias", [128, NTA], F32, kind="ExternalInput").ap()
    t_wqkv = nc.dram_tensor("wqkv", [D, 3 * D], F32, kind="ExternalInput").ap()
    t_bqkv = nc.dram_tensor("bqkv", [1, 3 * D], F32, kind="ExternalInput").ap()
    t_wo = nc.dram_tensor("wo", [D, D], F32, kind="ExternalInput").ap()
    t_bo = nc.dram_tensor("bo", [1, D], F32, kind="ExternalInput").ap()
    t_wg = nc.dram_tensor("wg", [I, D], F32, kind="ExternalInput").ap()
    t_wu = nc.dram_tensor("wu", [I, D], F32, kind="ExternalInput").ap()
    t_wd = nc.dram_tensor("wd", [D, I], F32, kind="ExternalInput").ap()
    t_id = nc.dram_tensor("ident", [128, 128], BF16, kind="ExternalInput").ap()
    t_h128 = nc.dram_tensor("h128", [128, 128], BF16, kind="ExternalInput").ap()
    t_hic = nc.dram_tensor("hic", [IC, IC], BF16, kind="ExternalInput").ap()
    t_tri = nc.dram_tensor("tri", [128, 128], BF16, kind="ExternalInput").ap()
    t_sin = nc.dram_tensor("sint", [128, NTA * 64], F32, kind="ExternalInput").ap()
    t_cos = nc.dram_tensor("cost", [128, NTA * 64], F32, kind="ExternalInput").ap()
    t_out = nc.dram_tensor("out", [TO, D], F32, kind="ExternalOutput").ap()

    SK = "ExternalOutput" if SCOUT else "Internal"
    t_kT = nc.dram_tensor("kT_sc", [H * 128, TA], BF16, kind=SK).ap()
    t_qT = nc.dram_tensor("qT_sc", [H * 128, TO], BF16, kind=SK).ap()
    t_v = nc.dram_tensor("v_sc", [TA, D], BF16, kind=SK).ap()
    t_x1 = nc.dram_tensor("x1_sc", [TO, D], F32, kind=SK).ap()
    t_a2qT = nc.dram_tensor("a2qT_sc", [D, TO], BF16, kind=SK).ap()
    t_wgqT = nc.dram_tensor("wgqT_sc", [D, I], BF16, kind="Internal").ap()
    t_wuqT = nc.dram_tensor("wuqT_sc", [D, I], BF16, kind="Internal").ap()
    t_wdqT = nc.dram_tensor("wdqT_sc", [I, D], BF16, kind="Internal").ap()
    t_Tsc = nc.dram_tensor("T_sc", [I, NTT * TT], BF16, kind="Internal").ap()
    t_a3 = nc.dram_tensor("a3_sc", [TT, I], BF16, kind="Internal").ap()
    t_nrm = nc.dram_tensor("nrm_sc", [128, 32], F32, kind="Internal").ap()
    t_nrm2 = nc.dram_tensor("nrm2_sc", [32, 1], F32, kind="Internal").ap()

    probes = {}

    def mkprobe(name, shape, dtype=F32):
        if name in PROBES:
            probes[name] = nc.dram_tensor("pb_" + name, list(shape), dtype,
                                          kind="ExternalOutput").ap()
            return probes[name]
        return None

    NBF, NF32 = 65536, 12288
    bfA = ctx.enter_context(nc.sbuf_tensor("bfA", [128, NBF], BF16)).ap()
    fA = ctx.enter_context(nc.sbuf_tensor("fA", [128, NF32], F32)).ap()
    sb_id = ctx.enter_context(nc.sbuf_tensor("sb_id", [128, 128], BF16)).ap()
    sb_h128 = ctx.enter_context(nc.sbuf_tensor("sb_h128", [128, 128], BF16)).ap()
    sb_hic = ctx.enter_context(nc.sbuf_tensor("sb_hic", [IC, IC], BF16)).ap()
    sb_tri = ctx.enter_context(nc.sbuf_tensor("sb_tri", [128, 128], BF16)).ap()
    sb_S = ctx.enter_context(nc.sbuf_tensor("sb_S", [128, NTA * 64], F32)).ap()
    sb_C = ctx.enter_context(nc.sbuf_tensor("sb_C", [128, NTA * 64], F32)).ap()
    sb_Ar = ctx.enter_context(nc.sbuf_tensor("sb_Ar", [128, 512], F32)).ap()
    sb_Br = ctx.enter_context(nc.sbuf_tensor("sb_Br", [128, 512], F32)).ap()
    sm = ctx.enter_context(nc.sbuf_tensor("sm", [128, 256], F32)).ap()
    sm2 = ctx.enter_context(nc.sbuf_tensor("sm2", [128, 1408], F32)).ap()
    sb_Tr = ctx.enter_context(nc.sbuf_tensor("sb_Tr", [IC, 2048], BF16)).ap()

    PH, NP1 = 0, 1
    S1, SSQ, NEGM, RSTD, TPA, TPB = 8, 9, 10, 11, 12, 13
    MXC, SCL, SINV, NPS = 16, 17, 18, 19
    MG, MU, THC, NTHC = 24, 25, 26, 27
    INVF = 64
    NRMQ, NRMK, BIAS = 0, 256, 768
    HMX, POSC, KBC, MBC = 1280, 1312, 1344, 1376

    p = Prog(nc, ctx)
    dS, dG, cT, cV, cS, cG = p.dS, p.dG, p.cT, p.cV, p.cS, p.cG

    smc = lambda c: sm[:, c:c + 1]

    p.dma("sync", sb_id[:, :], t_id)
    p.dma("sync", sb_h128[:, :], t_h128)
    p.dma("sync", sb_hic[:, :], t_hic)
    p.dma("sync", sb_tri[:, :], t_tri)
    p.dma("sync", sb_S[:, :], t_sin)
    p.dma("sync", sb_C[:, :], t_cos)
    p.dma("sync", sm2[:, POSC:POSC + 32], t_pos)
    n0 = p.dma("sync", sm2[:, KBC:KBC + 32], t_kb)
    p.wait("vector", dS, n0)

    p.vchain([
        lambda e: e.memset(smc(PH), 0.5),
        lambda e: e.tensor_scalar_add(smc(PH), smc(PH), C23),
        lambda e: e.tensor_scalar_sub(smc(PH), smc(PH), C23),
        lambda e: e.tensor_scalar_mul(smc(NP1), smc(PH), -2.0),
    ])
    nv = p.vop(lambda e: e.nop())
    p.wait("vector", cV, nv)

    A1T, WS, QWK, KST, QST, JNK = 0, 32768, 40960, 43008, 51200, 59392
    a1T = lambda dc, tl: bfA[:, A1T + dc * 2048 + tl * 128:A1T + dc * 2048 + (tl + 1) * 128]
    w_sb = bfA[:, WS:WS + 8192].rearrange("p (c j) -> p c j", j=512)
    qkv_bf = bfA[:, QWK:QWK + 512]
    rp2 = bfA[:, QWK + 512:QWK + 1024]
    t1r = bfA[:, QWK + 1024:QWK + 1536]
    a1bf = bfA[:, JNK:JNK + 2048]
    junkb = bfA[:, JNK + 2048:JNK + 4096]

    xbuf = [fA[:, 0:2048], fA[:, 2048:4096]]
    a1f = fA[:, 4096:6144]
    b_rep = fA[:, 6144:6656]
    onesr = fA[:, 6656:6784]
    brow = fA[0:1, 6784:7296]

    psP = ExitStack()
    psQ = [psP.enter_context(nc.psum_tensor(f"psQ{i}", [128, 512], F32)).ap() for i in range(2)]
    psTb = [psP.enter_context(nc.psum_tensor(f"psTb{i}", [128, 128], BF16)).ap() for i in range(2)]
    psR = psP.enter_context(nc.psum_tensor("psR", [128, 512], F32)).ap()

    nv = p.vop(lambda e: e.memset(onesr[0:1, :], 1.0))
    p.wait("tensor", cV, nv)

    def ln_stats(xap, nfeat, junk=None):
        jk = junkb if junk is None else junk
        nv_ = p.vop(lambda e, x_=xap: e.tensor_reduce(smc(S1), x_, AX.X, ALU.add))
        p.wait("scalar", cV, nv_)
        ns_ = p.sop(lambda e, x_=xap, n=nfeat, jk=jk: e.activation(
            jk[:, 0:n], x_, AF.Square, bias=0.0, scale=1.0, accum_out=smc(SSQ)))
        p.wait("vector", cS, ns_)
        p.vchain([
            lambda e: e.tensor_scalar_mul(smc(NEGM), smc(S1), -1.0 / nfeat),
            lambda e: e.tensor_tensor(smc(TPA), smc(NEGM), smc(NEGM), ALU.mult),
            lambda e: e.scalar_tensor_tensor(smc(TPB), smc(SSQ), 1.0 / nfeat, smc(TPA), ALU.mult, ALU.subtract),
            lambda e: e.tensor_scalar_add(smc(TPB), smc(TPB), EPS),
            lambda e: e.reciprocal(smc(TPA), smc(TPB)),
        ])
        p.wait("scalar", cV, cV.n)
        ns_ = p.sop(lambda e: e.activation(smc(RSTD), smc(TPA), AF.Sqrt, bias=0.0, scale=1.0))
        p.wait("vector", cS, ns_)

    n_tables_done = cV.n
    xcons = [n_tables_done, n_tables_done]
    evm = [0, 0]
    qevm = [0, 0]
    tevm = [0, 0]

    for chunk in range(2):
        for tl in range(16):
            j = chunk * 16 + tl
            bi = tl % 2
            p.wait("sync", cV, xcons[bi])
            nld = p.dma("sync", xbuf[bi], t_x[j * 128:(j + 1) * 128, :])
            p.wait("vector", dS, nld)
            ln_stats(xbuf[bi], D)
            p.vop(lambda e, b_=bi: e.tensor_scalar(a1f, xbuf[b_], smc(NEGM), smc(RSTD), ALU.add, ALU.mult), inc=False)
            nv_ = p.vop(lambda e: e.tensor_copy(a1bf, a1f))
            xcons[bi] = nv_
            p.wait("tensor", cV, nv_)
            for dc in range(DC):
                pi = dc % 2
                p.wait("tensor", cV, evm[pi])
                nt_ = p.top(lambda e, dc=dc, pi=pi: e.transpose(
                    psTb[pi][:, :], a1bf[:, dc * 128:(dc + 1) * 128], sb_id))
                p.wait("vector", cT, nt_)
                evm[pi] = p.vop(lambda e, dc=dc, tl=tl, pi=pi: e.tensor_copy(
                    a1T(dc, tl), psTb[pi][:, :]))

        pb = mkprobe(f"a1T_c{chunk}", (128, 2048))
        if pb is not None:
            nv = p.vop(lambda e: e.nop())
            p.wait("gpsimd", cV, nv)
            p.dma("gpsimd", pb, bfA[:, A1T:A1T + 2048])

        wqkv_r = t_wqkv.rearrange("(c p) o -> p c o", p=128)
        oblks = list(range(12)) if chunk == 0 else list(range(4, 12))

        wcons = 0
        psr_prev = 0
        for obi, ob in enumerate(oblks):
            p.wait("gpsimd", cT, wcons)
            nwl = p.dma("gpsimd", w_sb[:, :, :], wqkv_r[:, :, ob * 512:(ob + 1) * 512])
            if True:
                p.wait("sync", cT, psr_prev)
                nbl = p.dma("sync", brow, t_bqkv[:, ob * 512:(ob + 1) * 512])
                p.wait("tensor", dS, nbl)
                nt_ = p.top(lambda e: e.matmul(psR[:, :], onesr[0:1, :], brow, start=True, stop=True))
                psr_prev = nt_
                p.wait("vector", cT, nt_)
                nbr = p.vop(lambda e: e.tensor_copy(b_rep, psR[:, :]))
            else:
                nbr = cV.n

            is_q, is_k = ob < 4, 4 <= ob < 8
            for tl in range(16):
                j = chunk * 16 + tl
                pi = tl % 2
                p.wait("tensor", dG, nwl)
                p.wait("tensor", cV, qevm[pi])
                for dc in range(DC):
                    ntm = p.top(lambda e, dc=dc, pi=pi, tl=tl: e.matmul(
                        psQ[pi][:, :], a1T(dc, tl), w_sb[:, dc, :],
                        start=(dc == 0), stop=(dc == DC - 1)), inc=(dc == DC - 1))
                wcons = ntm
                p.wait("vector", cT, ntm)
                p.wait("vector", cV, nbr)
                nv_ = p.vop(lambda e, pi=pi: e.scalar_tensor_tensor(
                    qkv_bf, psQ[pi][:, :], 1.0, b_rep, ALU.mult, ALU.add))
                qevm[pi] = nv_
                if is_q or is_k:
                    p.wait("scalar", cV, nv_)
                    for hh in range(4):
                        if is_q:
                            col = NRMQ + (ob * 4 + hh) * 16 + tl
                        else:
                            col = NRMK + ((ob - 4) * 4 + hh) * 32 + j
                        nsq = p.sop(lambda e, hh=hh, col=col: e.activation(
                            junkb[:, hh * 128:(hh + 1) * 128],
                            qkv_bf[:, hh * 128:(hh + 1) * 128], AF.Square,
                            bias=0.0, scale=1.0, accum_out=sm2[:, col:col + 1]), inc=(hh == 3))
                    for hh in range(4):
                        p.vop(lambda e, hh=hh, j=j: e.tensor_copy(
                            sb_Ar[:, hh * 128:hh * 128 + 64], sb_S[:, j * 64:(j + 1) * 64]), inc=False)
                        p.vop(lambda e, hh=hh, j=j: e.tensor_copy(
                            sb_Ar[:, hh * 128 + 64:(hh + 1) * 128], sb_S[:, j * 64:(j + 1) * 64]), inc=False)
                        p.vop(lambda e, hh=hh, j=j: e.tensor_scalar_mul(
                            sb_Br[:, hh * 128:hh * 128 + 64], sb_C[:, j * 64:(j + 1) * 64], -1.0), inc=False)
                        p.vop(lambda e, hh=hh, j=j: e.tensor_copy(
                            sb_Br[:, hh * 128 + 64:(hh + 1) * 128], sb_C[:, j * 64:(j + 1) * 64]), inc=False)
                    nab = p.vop(lambda e, j=j: e.tensor_copy(sb_Br[:, 64:128], sb_C[:, j * 64:(j + 1) * 64]))
                    p.wait("vector", cV, nab)
                    p.wait("vector", cS, nsq)
                    qv = qkv_bf.rearrange("p (h q) -> p h q", q=128)
                    rv = rp2.rearrange("p (h q) -> p h q", q=128)
                    bv = sb_Br.rearrange("p (h q) -> p h q", q=128)
                    p.vop(lambda e: e.tensor_tensor(t1r, qkv_bf, sb_Ar, ALU.mult), inc=False)
                    p.vop(lambda e: e.tensor_tensor(rv[:, :, 0:64], qv[:, :, 64:128], bv[:, :, 0:64], ALU.mult), inc=False)
                    p.vop(lambda e: e.tensor_tensor(rv[:, :, 64:128], qv[:, :, 0:64], bv[:, :, 64:128], ALU.mult), inc=False)
                    nv_ = p.vop(lambda e: e.tensor_tensor(qkv_bf, t1r, rp2, ALU.add))
                    qevm[pi] = nv_
                    p.wait("tensor", cV, nv_)
                    stg = QST if is_q else KST
                    for hh in range(4):
                        ti = hh % 2
                        p.wait("tensor", cV, tevm[ti])
                        nt2 = p.top(lambda e, hh=hh, ti=ti: e.transpose(
                            psTb[ti][:, :], qkv_bf[:, hh * 128:(hh + 1) * 128], sb_id))
                        p.wait("vector", cT, nt2)
                        tevm[ti] = p.vop(lambda e, hh=hh, ti=ti, stg=stg, tl=tl: e.tensor_copy(
                            bfA[:, stg + hh * 2048 + tl * 128:stg + hh * 2048 + (tl + 1) * 128],
                            psTb[ti][:, :]))
                else:
                    p.wait("gpsimd", cV, nv_)
                    nst = p.dma("gpsimd", t_v[j * 128:(j + 1) * 128, (ob - 8) * 512:(ob - 7) * 512], qkv_bf)
                    p.wait("vector", dG, nst)
            if is_q or is_k:
                nvf = p.vop(lambda e: e.nop())
                p.wait("gpsimd", cV, nvf)
                stg = QST if is_q else KST
                tsc = t_qT if is_q else t_kT
                hb = (ob - (0 if is_q else 4)) * 4
                cb = chunk * 2048
                width = 2048
                for hh in range(4):
                    hr = (hb + hh) * 128
                    p.dma("gpsimd", tsc[hr:hr + 128, cb:cb + width],
                          bfA[:, stg + hh * 2048:stg + hh * 2048 + width])
                p.wait("vector", dG, dG.n)

    psP.close()

    if PHASES < 3:
        return _finish_stub(nc, ctx, p, t_out, fA, probes)

    KTH, QTH, VP, PT, AOUT = 0, 8192, 12288, 20608, 32768
    aout = lambda jj: bfA[:, AOUT + jj * 2048:AOUT + (jj + 1) * 2048]

    psP3 = ExitStack()
    psS = [psP3.enter_context(nc.psum_tensor(f"psS{i}", [128, 512], F32)).ap() for i in range(2)]
    psPV = [psP3.enter_context(nc.psum_tensor(f"psPV{i}", [128, 129], F32)).ap() for i in range(4)]
    psBC = psP3.enter_context(nc.psum_tensor("psBC", [128, 16], F32)).ap()

    p.wait("vector", cS, cS.n)
    for h in range(H):
        p.vop(lambda e, h=h: e.tensor_reduce(
            sm2[:, HMX + h:HMX + h + 1], sm2[:, NRMQ + h * 16:NRMQ + (h + 1) * 16], AX.X, ALU.max), inc=False)
        p.vop(lambda e, h=h: e.tensor_reduce(
            sm2[:, HMX + 16 + h:HMX + 17 + h], sm2[:, NRMK + h * 32:NRMK + (h + 1) * 32], AX.X, ALU.max), inc=False)
    nv = p.vop(lambda e: e.nop())
    p.wait("gpsimd", cV, nv)
    n1 = p.dma("gpsimd", t_nrm, sm2[:, HMX:HMX + 32])
    p.wait("sync", dG, n1)
    nrm_rows = fA[0:32, 0:128]
    n2 = p.dma("sync", nrm_rows, t_nrm.rearrange("p c -> c p"), allow_slow_non_contiguous=True)
    p.wait("vector", dS, n2)
    mx32 = fA[0:32, 128:129]
    p.vchain([lambda e: e.tensor_reduce(mx32, nrm_rows, AX.X, ALU.max)])
    nv = p.vop(lambda e: e.nop())
    p.wait("gpsimd", cV, nv)
    n3 = p.dma("gpsimd", t_nrm2, mx32)
    p.wait("sync", dG, n3)
    mrow = fA[0:1, 192:224]
    n4 = p.dma("sync", mrow, t_nrm2.rearrange("a b -> b a"), allow_slow_non_contiguous=True)
    p.wait("vector", dS, n4)
    m2row = fA[0:1, 224:240]
    p.vchain([lambda e: e.tensor_tensor(m2row, fA[0:1, 192:208], fA[0:1, 208:224], ALU.mult)])
    p.wait("scalar", cV, cV.n)
    mrow2 = fA[0:1, 240:256]
    ns = p.sop(lambda e: e.activation(mrow2, m2row, AF.Sqrt, bias=0.0, scale=1.0 / 128.0))
    p.wait("tensor", cS, ns)
    nt = p.top(lambda e: e.matmul(psBC[:, :], onesr[0:1, :], mrow2, start=True, stop=True))
    p.wait("vector", cT, nt)
    nv = p.vop(lambda e: e.tensor_copy(sm2[:, MBC:MBC + 16], psBC[:, :]))
    p.wait("vector", cV, nv)
    for h in range(H):
        p.vop(lambda e, h=h: e.tensor_scalar(
            sm2[:, BIAS + h * 32:BIAS + (h + 1) * 32], sm2[:, KBC:KBC + 32],
            sm2[:, MBC + h:MBC + h + 1], None, ALU.subtract), inc=False)
    nbias = p.vop(lambda e: e.nop())
    p.wait("scalar", cV, nbias)
    p.wait("tensor", cV, nbias)

    pb = mkprobe("Mbc", (128, 16))
    if pb is not None:
        p.wait("gpsimd", cV, nbias)
        p.dma("gpsimd", pb, sm2[:, MBC:MBC + 16])

    p.wait("sync", dG, dG.n)

    ldm = [0, 0]
    vmem = [0, 0]
    sexp = [0, 0]
    pvm = [0, 0]
    evq = 0
    vplus = lambda hb: bfA[:, VP + hb * 4128:VP + (hb + 1) * 4128].rearrange("p (k c) -> p k c", c=129)
    for h in range(H):
        hb = h % 2
        p.wait("sync", cT, ldm[hb])
        nk = p.dma("sync", bfA[:, KTH + hb * 4096:KTH + (hb + 1) * 4096], t_kT[h * 128:(h + 1) * 128, :])
        nq = p.dma("sync", bfA[:, QTH + hb * 2048:QTH + (hb + 1) * 2048], t_qT[h * 128:(h + 1) * 128, :])
        p.wait("vector", cT, vmem[hb])
        nms = p.vop(lambda e, hb=hb: e.memset(vplus(hb)[:, :, 128:129], 1.0))
        p.wait("sync", cV, nms)
        nvl = p.dma("sync", vplus(hb)[:, :, 0:128],
                    t_v.rearrange("(k p) d -> p k d", p=128)[:, :, h * 128:(h + 1) * 128])
        kT_h = bfA[:, KTH + hb * 4096:KTH + (hb + 1) * 4096]
        qT_h = bfA[:, QTH + hb * 2048:QTH + (hb + 1) * 2048]
        first_pe = True
        for cq in range(NCQ):
            klist = list(range(0, 4 * cq + 4)) + list(range(16, 32))
            nk_t = len(klist)
            for ki, kt in enumerate(klist):
                sb_ = ki % 2
                if first_pe:
                    p.wait("tensor", dS, nvl)
                    first_pe = False
                p.wait("tensor", cS, sexp[sb_])
                nmt = p.top(lambda e, kt=kt, cq=cq, sb_=sb_, kT_h=kT_h, qT_h=qT_h: e.matmul(
                    psS[sb_][:, :], kT_h[:, kt * 128:(kt + 1) * 128],
                    qT_h[:, cq * 512:(cq + 1) * 512], start=True, stop=True))
                diag = 4 * cq <= kt <= 4 * cq + 3
                if diag:
                    z = kt - 4 * cq
                    p.wait("vector", cT, nmt)
                    if z > 0:
                        p.vop(lambda e, z=z, sb_=sb_: e.tensor_scalar_add(
                            psS[sb_][:, 0:z * 128], psS[sb_][:, 0:z * 128], NEG_BIG), inc=False)
                    nmk = p.vop(lambda e, z=z, sb_=sb_: e.tensor_tensor(
                        psS[sb_][:, z * 128:(z + 1) * 128], psS[sb_][:, z * 128:(z + 1) * 128],
                        sb_tri, ALU.add))
                    p.wait("scalar", cV, nmk)
                p.wait("scalar", cT, nmt)
                p.wait("scalar", cT, pvm[sb_])
                nse = p.sop(lambda e, sb_=sb_, h=h, kt=kt: e.activation(
                    bfA[:, PT + sb_ * 512:PT + (sb_ + 1) * 512], psS[sb_][:, :], AF.Exp,
                    bias=sm2[:, BIAS + h * 32 + kt:BIAS + h * 32 + kt + 1], scale=ISQHD))
                sexp[sb_] = nse
                p.wait("tensor", cS, nse)
                if ki == 0:
                    p.wait("tensor", cV, evq)
                for sub in range(4):
                    pvm[sb_] = p.top(lambda e, sub=sub, sb_=sb_, kt=kt, hb=hb, ki=ki, last=(ki == nk_t - 1): e.matmul(
                        psPV[sub][:, :], bfA[:, PT + sb_ * 512 + sub * 128:PT + sb_ * 512 + (sub + 1) * 128],
                        vplus(hb)[:, kt, :], start=(ki == 0), stop=last,
                        skip_group_check=True), inc=(sub == 3))
            ldm[hb] = pvm[0] if pvm[0] > pvm[1] else pvm[1]
            vmem[hb] = ldm[hb]
            p.wait("vector", cT, max(pvm))
            for sub in range(4):
                jj = cq * 4 + sub
                p.vchain([lambda e, sub=sub: e.reciprocal(smc(TPA), psPV[sub][:, 128:129])])
                evq = p.vop(lambda e, sub=sub, jj=jj, h=h: e.tensor_scalar(
                    aout(jj)[:, h * 128:(h + 1) * 128], psPV[sub][:, 0:128],
                    smc(TPA), None, ALU.mult))

    psP3.close()

    pb = mkprobe("aout", (128, 2048))
    if pb is not None:
        nv = p.vop(lambda e: e.tensor_copy(fA[:, 0:2048], aout(0)))
        p.wait("gpsimd", cV, nv)
        p.dma("gpsimd", pb, fA[:, 0:2048])

    if PHASES < 4:
        return _finish_stub(nc, ctx, p, t_out, fA, probes)

    ATT = 0
    attT = lambda dc, jj: bfA[:, ATT + dc * 2048 + jj * 128:ATT + dc * 2048 + (jj + 1) * 128]

    psP4 = ExitStack()
    psTb4 = [psP4.enter_context(nc.psum_tensor(f"psT4{i}", [128, 128], BF16)).ap() for i in range(2)]
    psO = [psP4.enter_context(nc.psum_tensor(f"psO{i}", [128, 512], F32)).ap() for i in range(2)]
    psR4 = psP4.enter_context(nc.psum_tensor("psR4", [128, 512], F32)).ap()

    evm4 = [0, 0]
    for jj in range(NTO):
        for dc in range(DC):
            ti = dc % 2
            p.wait("tensor", cV, evm4[ti])
            nt_ = p.top(lambda e, jj=jj, dc=dc, ti=ti: e.transpose(
                psTb4[ti][:, :], aout(jj)[:, dc * 128:(dc + 1) * 128], sb_id))
            p.wait("vector", cT, nt_)
            evm4[ti] = p.vop(lambda e, jj=jj, dc=dc, ti=ti: e.tensor_copy(
                attT(dc, jj), psTb4[ti][:, :]))
    p.wait("gpsimd", cV, max(evm4))
    WO = AOUT
    wo_sb = bfA[:, WO:WO + 32768].rearrange("p (c o) -> p c o", o=2048)
    nwo = p.dma("gpsimd", wo_sb[:, :, :], t_wo.rearrange("(c p) o -> p c o", p=128))
    bo_rep = fA[:, 4096:6144]
    psr_prev = 0
    for obx in range(4):
        p.wait("sync", cT, psr_prev)
        nbl = p.dma("sync", brow, t_bo[:, obx * 512:(obx + 1) * 512])
        p.wait("tensor", dS, nbl)
        nt_ = p.top(lambda e: e.matmul(psR4[:, :], onesr[0:1, :], brow, start=True, stop=True))
        psr_prev = nt_
        p.wait("vector", cT, nt_)
        nv = p.vop(lambda e, obx=obx: e.tensor_copy(bo_rep[:, obx * 512:(obx + 1) * 512], psR4[:, :]))
        p.wait("tensor", cV, nv)
    p.wait("tensor", dG, nwo)

    proj = fA[:, 0:2048]
    x1t = fA[:, 2048:4096]
    a2t = fA[:, 0:2048]
    qc1 = fA[:, 8192:10240]
    qc2 = fA[:, 10240:12288]

    oev = 0
    oevs = [0, 0]
    qc1_free = 0
    for jj in range(NTO):
        p.wait("tensor", cV, oev)
        for obx in range(4):
            pi = obx % 2
            p.wait("tensor", cV, oevs[pi])
            for dc in range(DC):
                ntm = p.top(lambda e, dc=dc, jj=jj, obx=obx, pi=pi: e.matmul(
                    psO[pi][:, :], attT(dc, jj), wo_sb[:, dc, obx * 512:(obx + 1) * 512],
                    start=(dc == 0), stop=(dc == DC - 1)), inc=(dc == DC - 1))
            p.wait("vector", cT, ntm)
            nv = p.vop(lambda e, obx=obx, pi=pi: e.scalar_tensor_tensor(
                proj[:, obx * 512:(obx + 1) * 512], psO[pi][:, :], 1.0,
                bo_rep[:, obx * 512:(obx + 1) * 512], ALU.mult, ALU.add))
            oevs[pi] = nv
        oev = nv
        ln_stats(proj, D, junk=qc2)
        p.wait("sync", cV, qc1_free)
        nxl = p.dma("sync", qc1, t_x[jj * 128:(jj + 1) * 128, :])
        p.vop(lambda e: e.tensor_scalar(qc2, proj, smc(NEGM), smc(RSTD), ALU.add, ALU.mult), inc=False)
        p.wait("vector", dS, nxl)
        nv = p.vop(lambda e: e.tensor_tensor(x1t, qc1, qc2, ALU.add))
        p.wait("gpsimd", cV, nv)
        nst = p.dma("gpsimd", t_x1[jj * 128:(jj + 1) * 128, :], x1t)
        ln_stats(x1t, D, junk=qc2)
        nv = p.vop(lambda e: e.tensor_scalar(a2t, x1t, smc(NEGM), smc(RSTD), ALU.add, ALU.mult))
        p.wait("vector", cV, nv)
        p.vchain([
            lambda e: e.tensor_reduce(smc(MXC), a2t, AX.X, ALU.max, apply_absolute_value=True),
            lambda e: e.tensor_scalar_max(smc(MXC), smc(MXC), EPS),
            lambda e: e.reciprocal(smc(TPA), smc(MXC)),
            lambda e: e.tensor_scalar_mul(smc(SCL), smc(TPA), 128.0),
            lambda e: e.tensor_scalar_mul(smc(SINV), smc(MXC), 1.0 / 128.0),
            lambda e: e.tensor_scalar(smc(NPS), smc(SCL), smc(NP1), None, ALU.mult),
        ])
        p.vop(lambda e: e.tensor_scalar(qc1, a2t, smc(SCL), C23, ALU.mult, ALU.add), inc=False)
        p.vop(lambda e: e.tensor_scalar(qc2, a2t, smc(SCL), C52, ALU.mult, ALU.add), inc=False)
        p.vop(lambda e: e.scalar_tensor_tensor(qc2, qc2, C52, qc1, ALU.subtract, ALU.add), inc=False)
        nqf = p.vop(lambda e: e.scalar_tensor_tensor(qc1, a2t, smc(NPS), qc2, ALU.mult, ALU.add))
        qc1_free = nqf
        p.vop(lambda e: e.tensor_scalar(qc1, qc1, C23, 127.0, ALU.subtract, ALU.min), inc=False)
        dst = bfA[:, ATT:ATT + 32768].rearrange("p (c t) -> p c t", t=2048)[:, :, jj * 128:(jj + 1) * 128]
        src3 = qc1.rearrange("p (c t) -> p c t", t=128)
        nv = p.vop(lambda e, dst=dst, src3=src3: e.tensor_scalar(dst, src3, smc(SINV), None, ALU.mult))
        p.wait("tensor", cV, nv)
        for dc in range(DC):
            ti = dc % 2
            p.wait("tensor", cV, evm4[ti])
            nt_ = p.top(lambda e, jj=jj, dc=dc, ti=ti: e.transpose(
                psTb4[ti][:, :], attT(dc, jj), sb_id))
            p.wait("vector", cT, nt_)
            evm4[ti] = p.vop(lambda e, jj=jj, dc=dc, ti=ti: e.tensor_copy(
                attT(dc, jj), psTb4[ti][:, :]))
        nv = p.vop(lambda e: e.nop())
        p.wait("sync", cV, nv)
        nst = p.dma("sync",
                    t_a2qT.rearrange("(c p) t -> p c t", p=128)[:, :, jj * 128:(jj + 1) * 128],
                    bfA[:, ATT:ATT + 32768].rearrange("p (c t) -> p c t", t=2048)[:, :, jj * 128:(jj + 1) * 128])
        p.wait("vector", dS, nst)

    psP4.close()

    pb = mkprobe("x1", (128, 2048))
    if pb is not None:
        p.wait("sync", dG, dG.n)
        n_ = p.dma("sync", qc1, t_x1[0:128, :])
        p.wait("gpsimd", dS, n_)
        p.dma("gpsimd", pb, qc1)

    if PHASES < 5:
        return _finish_stub(nc, ctx, p, t_out, fA, probes)

    psP5 = ExitStack()
    ps11 = psP5.enter_context(nc.psum_tensor("ps11", [1, 1], F32)).ap()
    psBC5 = psP5.enter_context(nc.psum_tensor("psBC5", [128, 1], F32)).ap()
    psT5 = [psP5.enter_context(nc.psum_tensor(f"psT5{i}", [128, 128], BF16)).ap() for i in range(2)]

    p.wait("sync", cV, cV.n)
    p.wait("sync", cT, cT.n)
    p.wait("scalar", cV, cV.n)
    p.wait("vector", cT, cT.n)
    onescol = fA[:, 6144:6145]
    nv = p.vop(lambda e: e.memset(onescol, 1.0))
    p.wait("tensor", cV, nv)
    wtiles = [fA[:, 0:2048], fA[:, 2048:4096]]
    absout = fA[:, 4096:6144]
    accall = sm2[:, 0:64]
    acc1 = sm[:, 32:33]
    val1 = fA[0:1, 6148:6149]
    q1b = bfA[:, 0:2048]
    q2b = bfA[:, 2048:4096]
    q3b = bfA[:, 4096:6144]
    accb = [bfA[:, 8192:16384], bfA[:, 16384:24576]]
    mtiles = [fA[:, 6784:8832], fA[:, 8832:10880]]
    tev5 = [0, 0]
    accst = [0, 0]
    scons = [0, 0]
    vcons = [0, 0]
    p.wait("scalar", dS, dS.n)
    p.wait("gpsimd", cV, cV.n)
    p.wait("gpsimd", cT, cT.n)
    p.wait("gpsimd", dS, dS.n)

    for mat, (t_w, t_wqT) in enumerate(((t_wg, t_wgqT), (t_wu, t_wuqT), (t_wd, t_wdqT))):
        nrows = I if mat < 2 else D
        ncols = D if mat < 2 else I
        ncc = ncols // 2048
        if mat < 2:
            slabs = [(rt, cc) for rt in range(nrows // 128) for cc in range(ncc)]
        else:
            slabs = [(rt, cc) for cc in range(ncc) for rt in range(nrows // 128)]
        if mat > 0:
            p.wait("scalar", cV, red_guard)
        for si, (rt, cc) in enumerate(slabs):
            bi = si % 2
            p.wait("gpsimd", cS, scons[bi])
            nld = p.dma("gpsimd", mtiles[bi],
                        t_w[rt * 128:(rt + 1) * 128, cc * 2048:(cc + 1) * 2048])
            p.wait("scalar", dG, nld)
            scons[bi] = p.sop(lambda e, bi=bi, si=si: e.activation(
                absout, mtiles[bi], AF.Abs, bias=0.0, scale=1.0,
                accum_out=sm2[:, si:si + 1]))
        p.wait("vector", cS, max(scons))
        p.vchain([lambda e, n=len(slabs): e.tensor_reduce(acc1, sm2[:, 0:n], AX.X, ALU.add)])
        p.vchain([lambda e, mat=mat: e.tensor_copy(sm[:, 40 + mat:41 + mat], acc1)])
        if mat == 0:
            p.vchain([lambda e: e.tensor_copy(sm2[:, 64:128], sm2[:, 0:64])])
        red_guard = cV.n
        p.wait("tensor", cV, cV.n)
        nt_ = p.top(lambda e: e.matmul(ps11[:, :], acc1, onescol, start=True, stop=True))
        p.wait("vector", cT, nt_)
        p.vchain([
            lambda e: e.tensor_copy(val1, ps11[:, :]),
            lambda e, nr=nrows, nc_=ncols: e.tensor_scalar_mul(val1, val1, 1.0 / (nr * nc_)),
            lambda e: e.tensor_scalar_max(val1, val1, EPS),
        ])
        p.wait("tensor", cV, cV.n)
        nt_ = p.top(lambda e: e.matmul(psBC5[:, :], onesr[0:1, :], val1, start=True, stop=True))
        p.wait("vector", cT, nt_)
        mc = MG if mat == 0 else (MU if mat == 1 else THC)
        p.vchain([
            lambda e, mc=mc: e.tensor_copy(sm[:, mc:mc + 1], psBC5[:, :]),
            lambda e, mc=mc: e.tensor_scalar_mul(smc(TPA), sm[:, mc:mc + 1], 0.5),
            lambda e: e.tensor_scalar_mul(smc(TPB), smc(TPA), -1.0),
        ])
        for si, (rt, cc) in enumerate(slabs):
            bi = si % 2
            gi = (si // 4) % 2
            rtl = si % 4
            p.wait("sync", cV, vcons[bi])
            nld = p.dma("sync", wtiles[bi],
                        t_w[rt * 128:(rt + 1) * 128, cc * 2048:(cc + 1) * 2048])
            p.wait("vector", dS, nld)
            p.vop(lambda e, bi=bi: e.tensor_scalar(
                q2b, wtiles[bi], smc(TPB), None, ALU.is_lt), inc=False)
            nv = p.vop(lambda e, bi=bi: e.scalar_tensor_tensor(
                q3b, wtiles[bi], smc(TPA), q2b, ALU.is_gt, ALU.subtract))
            vcons[bi] = nv
            p.wait("tensor", cV, nv)
            if rtl == 0:
                p.wait("scalar", dG, accst[gi])
            for dc in range(DC):
                ti = dc % 2
                p.wait("tensor", cS, tev5[ti])
                ntt = p.top(lambda e, ti=ti, dc=dc: e.transpose(
                    psT5[ti][:, :], q3b[:, dc * 128:(dc + 1) * 128], sb_id))
                p.wait("scalar", cT, ntt)
                tev5[ti] = p.sop(lambda e, ti=ti, dc=dc, gi=gi, rtl=rtl: e.activation(
                    accb[gi][:, dc * 512 + rtl * 128:dc * 512 + (rtl + 1) * 128],
                    psT5[ti][:, :], AF.Copy, bias=0.0, scale=1.0))
            p.wait("vector", cT, ntt)
            if rtl == 3:
                p.wait("gpsimd", cS, max(tev5))
                rt0 = rt - 3
                if mat < 2:
                    dstp = t_wqT.rearrange("(g p) i -> p g i", p=128)[:, :, rt0 * 128:rt0 * 128 + 512]
                else:
                    dstp = t_wqT.rearrange("(g p) d -> p g d", p=128)[:, cc * 16:(cc + 1) * 16, rt0 * 128:rt0 * 128 + 512]
                accst[gi] = p.dma("gpsimd", dstp,
                                  accb[gi].rearrange("p (g k) -> p g k", k=512))

    psP5.close()

    if PHASES < 6:
        return _finish_stub(nc, ctx, p, t_out, fA, probes)

    A2Q, WGA, WUA, SGA, ITM, TR1, INTER, YB = 0, 32768, 40960, 49152, 50176, 51200, 24576, 57344
    a2q_al = bfA[:, A2Q:A2Q + 32768].rearrange("p (c t) -> p c t", t=2048)

    x1sl = fA[:, 0:2048]
    chn = fA[:, 2048:4096]
    rows = lambda ts_: fA[:, 4096 + ts_ * 2048:4096 + (ts_ + 1) * 2048]

    p.wait("sync", dG, dG.n)
    p.wait("scalar", dG, dG.n)

    psA_ = ExitStack()
    psGs = [psA_.enter_context(nc.psum_tensor(f"psG{i}", [128, 512], F32)).ap() for i in range(2)]
    psUs = [psA_.enter_context(nc.psum_tensor(f"psU{i}", [128, 512], F32)).ap() for i in range(2)]
    psF = [psA_.enter_context(nc.psum_tensor(f"psF{i}", [128, 512], F32)).ap() for i in range(2)]
    nld = p.dma("sync", a2q_al, t_a2qT.rearrange("(c p) t -> p c t", p=128))
    p.wait("tensor", dS, nld)
    guevs = [0, 0]
    nsgs = [0, 0]
    itcons = [0, 0]
    fevac = [0, 0]
    fst = [0, 0, 0, 0]
    cpar = 0
    wcons = 0
    for ib in range(16):
        p.wait("sync", cT, wcons)
        p.dma("sync", bfA[:, WGA:WGA + 8192].rearrange("p (c k) -> p c k", k=512),
              t_wgqT.rearrange("(c p) i -> p c i", p=128)[:, :, ib * 512:(ib + 1) * 512])
        nlu = p.dma("sync", bfA[:, WUA:WUA + 8192].rearrange("p (c k) -> p c k", k=512),
                    t_wuqT.rearrange("(c p) i -> p c i", p=128)[:, :, ib * 512:(ib + 1) * 512])
        p.wait("tensor", dS, nlu)
        for tt in range(NTT):
            for it in range(4):
                c = ib * 4 + it
                pr = cpar % 2
                ri = cpar % 4
                cpar += 1
                psG, psU, psFc = psGs[pr], psUs[pr], psF[pr]
                sgb = bfA[:, SGA + pr * 512:SGA + (pr + 1) * 512]
                itb = bfA[:, ITM + pr * 512:ITM + (pr + 1) * 512]
                p.wait("tensor", cV, guevs[pr])
                p.wait("tensor", cS, nsgs[pr])
                for dc in range(DC):
                    ng = p.top(lambda e, dc=dc, it=it, tt=tt, psG=psG: e.matmul(
                        psG[:, :], bfA[:, WGA + dc * 512 + it * 128:WGA + dc * 512 + (it + 1) * 128],
                        a2q_al[:, dc, tt * 512:(tt + 1) * 512], start=(dc == 0), stop=(dc == DC - 1)), inc=(dc == DC - 1))
                p.wait("scalar", cT, ng)
                nsg = p.sop(lambda e, psG=psG, sgb=sgb: e.activation(
                    sgb, psG[:, :], AF.Silu, bias=0.0, scale=sm[:, MG:MG + 1]))
                nsgs[pr] = nsg
                for dc in range(DC):
                    nu = p.top(lambda e, dc=dc, it=it, tt=tt, psU=psU: e.matmul(
                        psU[:, :], bfA[:, WUA + dc * 512 + it * 128:WUA + dc * 512 + (it + 1) * 128],
                        a2q_al[:, dc, tt * 512:(tt + 1) * 512], start=(dc == 0), stop=(dc == DC - 1)), inc=(dc == DC - 1))
                wcons = nu
                p.wait("vector", cT, max(nu, itcons[pr]))
                p.wait("vector", cS, nsg)
                guevs[pr] = p.vop(lambda e, psU=psU, sgb=sgb, itb=itb: e.scalar_tensor_tensor(
                    itb, psU[:, :], sm[:, MU:MU + 1], sgb, ALU.mult, ALU.mult))
                p.wait("tensor", cV, max(guevs[pr], fevac[pr]))
                ntf = p.top(lambda e, psFc=psFc, itb=itb: e.matmul(
                    psFc[:, :], sb_h128, itb, start=True, stop=True))
                itcons[pr] = ntf
                p.wait("vector", cT, ntf)
                p.wait("vector", dG, fst[ri])
                fevac[pr] = p.vop(lambda e, psFc=psFc, ri=ri: e.tensor_copy(
                    bfA[:, TR1 + ri * 512:TR1 + (ri + 1) * 512], psFc[:, :]))
                p.wait("gpsimd", cV, fevac[pr])
                fst[ri] = p.dma("gpsimd", t_Tsc[c * 128:(c + 1) * 128, tt * 512:(tt + 1) * 512],
                                bfA[:, TR1 + ri * 512:TR1 + (ri + 1) * 512])
    psA_.close()

    trcons = [0, 0]
    yev = 0
    p.wait("sync", dG, dG.n)
    for tt in range(NTT):
        psC_ = ExitStack()
        psY = psC_.enter_context(nc.psum_tensor(f"psY{tt}", [128, 512], F32)).ap()
        p.wait("sync", dG, dG.n)
        yv = bfA[:, YB:YB + 8192]
        y3 = yv.rearrange("p (a b) -> p a b", b=128)
        Tsc_r = t_Tsc.rearrange("(c s) t -> c s t", s=128)
        for tb in range(4):
            for g in range(16):
                gi = g % 2
                trh = sb_Tr[0:IC, gi * 1024:(gi + 1) * 1024]
                p.wait("sync", cT, trcons[gi])
                nld = p.dma("sync", trh.rearrange("c (s t) -> c s t", t=128),
                            Tsc_r[:, g * 8:(g + 1) * 8, tt * 512 + tb * 128:tt * 512 + (tb + 1) * 128])
                p.wait("tensor", dS, nld)
                p.wait("tensor", cV, yev)
                for s_ in range(8):
                    ntm = p.top(lambda e, s_=s_, gi=gi: e.matmul(
                        psY[:, s_ * 64:(s_ + 1) * 64],
                        sb_Tr[0:IC, gi * 1024 + s_ * 128:gi * 1024 + (s_ + 1) * 128],
                        sb_hic, start=True, stop=True), inc=(s_ == 7))
                trcons[gi] = ntm
                p.wait("vector", cT, ntm)
                yev = p.vop(lambda e, g=g: e.tensor_scalar_mul(
                    y3[:, :, g * 8:(g + 1) * 8],
                    psY.rearrange("p (s c) -> p c s", c=64), FWHT_NORM))
            p.wait("vector", cV, yev)
            p.vchain([
                lambda e: e.tensor_reduce(smc(MXC), yv, AX.X, ALU.max, apply_absolute_value=True),
                lambda e: e.tensor_scalar_max(smc(MXC), smc(MXC), EPS),
                lambda e: e.reciprocal(smc(TPA), smc(MXC)),
                lambda e: e.tensor_scalar_mul(smc(SCL), smc(TPA), 128.0),
                lambda e: e.tensor_scalar_mul(smc(SINV), smc(MXC), 1.0 / 128.0),
                lambda e: e.tensor_scalar(smc(NPS), smc(SCL), smc(NP1), None, ALU.mult),
            ])
            for qk in range(4):
                ysl = yv[:, qk * 2048:(qk + 1) * 2048]
                p.vop(lambda e, ysl=ysl: e.tensor_scalar(x1sl, ysl, smc(SCL), C23, ALU.mult, ALU.add), inc=False)
                p.vop(lambda e, ysl=ysl: e.tensor_scalar(chn, ysl, smc(SCL), C52, ALU.mult, ALU.add), inc=False)
                p.vop(lambda e: e.scalar_tensor_tensor(chn, chn, C52, x1sl, ALU.subtract, ALU.add), inc=False)
                p.vop(lambda e, ysl=ysl: e.scalar_tensor_tensor(x1sl, ysl, smc(NPS), chn, ALU.mult, ALU.add), inc=False)
                p.vop(lambda e: e.tensor_scalar(x1sl, x1sl, C23, 127.0, ALU.subtract, ALU.min), inc=False)
                nv = p.vop(lambda e, ysl=ysl: e.tensor_scalar(ysl, x1sl, smc(SINV), None, ALU.mult))
            p.wait("gpsimd", cV, nv)
            nst = p.dma("gpsimd", t_a3[tb * 128:(tb + 1) * 128, :], yv)
            p.wait("vector", dG, nst)
        psC_.close()

        psD_ = ExitStack()
        psDn = [psD_.enter_context(nc.psum_tensor(f"psD{tt}_{i}", [128, 512], F32)).ap() for i in range(4)]
        p.wait("sync", dG, dG.n)
        p.wait("sync", cT, cT.n)
        a3sl = lambda c: bfA[:, INTER + c * 512:INTER + (c + 1) * 512]
        for c in range(IC):
            p.dma("sync", a3sl(c), t_a3[:, c * 128:(c + 1) * 128], transpose=True)
        wdbB = [bfA[:, 12288:16384], bfA[:, 16384:20480]]
        dcons = [0, 0]
        dnev = cV.n
        for ob in range(4):
            p.wait("tensor", cV, dnev)
            for g8 in range(8):
                bi = g8 % 2
                p.wait("sync", cT, dcons[bi])
                nld = p.dma("sync", wdbB[bi].rearrange("p (c k) -> p c k", k=512),
                            t_wdqT.rearrange("(c p) d -> p c d", p=128)[:, g8 * 8:(g8 + 1) * 8, ob * 512:(ob + 1) * 512])
                p.wait("tensor", dS, nld)
                if g8 == 0:
                    p.wait("tensor", cV, dnev)
                for i8 in range(8):
                    ic = g8 * 8 + i8
                    for ts_ in range(4):
                        dcons[bi] = p.top(lambda e, ts_=ts_, bi=bi, ic=ic, i8=i8: e.matmul(
                            psDn[ts_][:, :], a3sl(ic)[:, ts_ * 128:(ts_ + 1) * 128],
                            wdbB[bi][:, i8 * 512:(i8 + 1) * 512],
                            start=(ic == 0), stop=(ic == IC - 1), skip_group_check=True),
                            inc=(ts_ == 3 and i8 == 7))
            p.wait("vector", cT, max(dcons))
            for ts_ in range(4):
                dnev = p.vop(lambda e, ts_=ts_, ob=ob: e.tensor_scalar(
                    rows(ts_)[:, ob * 512:(ob + 1) * 512], psDn[ts_][:, :],
                    sm[:, THC:THC + 1], None, ALU.mult))
        p.wait("tensor", cV, dnev)
        psD_.close()

        for ts_ in range(4):
            jj = tt * 4 + ts_
            ln_stats(rows(ts_), D)
            p.wait("sync", dG, dG.n)
            nxl = p.dma("sync", x1sl, t_x1[jj * 128:(jj + 1) * 128, :])
            nv = p.vop(lambda e, ts_=ts_: e.tensor_scalar(
                chn, rows(ts_), smc(NEGM), smc(RSTD), ALU.add, ALU.mult))
            pbc = mkprobe(f"chn{jj}", (128, 2048))
            if pbc is not None:
                p.wait("gpsimd", cV, nv)
                p.dma("gpsimd", pbc, chn)
            p.wait("vector", dS, nxl)
            nv = p.vop(lambda e: e.tensor_tensor(x1sl, x1sl, chn, ALU.add))
            p.wait("gpsimd", cV, nv)
            p.dma("gpsimd", t_out[jj * 128:(jj + 1) * 128, :], x1sl)
            p.wait("vector", dG, dG.n)

    return _finish(nc, ctx, p, probes)


def _finish_stub(nc, ctx, p, t_out, fA, probes):
    nv = p.vop(lambda e: e.memset(fA[:, 0:2048], 0.0))
    p.wait("gpsimd", p.cV, nv)
    for j in range(NTO):
        p.dma("gpsimd", t_out[j * 128:(j + 1) * 128, :], fA[:, 0:2048])
    return _finish(nc, ctx, p, probes)


def _finish(nc, ctx, p, probes):
    p.wait("gpsimd", p.dG, p.dG.n)
    p.wait("gpsimd", p.dS, p.dS.n)
    blk = ctx.enter_context(nc.Block())
    for eng in ENGS:
        ops = p.ops[eng]
        getattr(blk, eng)(lambda e, ops=ops: [f(e) for f in ops])
    ctx.close()
    return nc, probes


def _stage(inputs):
    x = np.ascontiguousarray(np.asarray(inputs["L0_attn_x"], dtype=np.float32))
    perm = _qkv_perm_cached()
    wqkv = np.ascontiguousarray(np.asarray(inputs["L0_attn_w_qkv"], np.float32)[:, perm])
    bqkv = np.ascontiguousarray(np.asarray(inputs["L0_attn_b_qkv"], np.float32)[perm][None, :])
    wo = np.ascontiguousarray(np.asarray(inputs["L0_attn_w_o"], np.float32))
    bo = np.ascontiguousarray(np.asarray(inputs["L0_attn_b_o"], np.float32)[None, :])
    wg = np.ascontiguousarray(np.asarray(inputs["L0_ffn_w_gate"], np.float32))
    wu = np.ascontiguousarray(np.asarray(inputs["L0_ffn_w_up"], np.float32))
    wd = np.ascontiguousarray(np.asarray(inputs["L0_ffn_w_down"], np.float32))

    ident = np.eye(128, dtype=BFD)
    h128 = _had(128).astype(BFD)
    hic = _had(IC).astype(BFD)
    pp = np.arange(128)[:, None]
    fl = np.arange(128)[None, :]
    tri = np.where(pp > fl, np.float32(NEG_BIG), np.float32(0.0)).astype(BFD)
    invf32 = (1.0 / (10000.0 ** (np.arange(0, 128, 2).astype(np.float32) / 128))).astype(np.float32)

    maps = []
    for c in range(8):
        b, half = divmod(c, 2)
        if half == 0:
            xp = x[b]
            pos = np.arange(S, dtype=np.float32)
            kb = np.concatenate([np.zeros(TO, np.float32), np.full(TA - TO, NEG_BIG, np.float32)])
        else:
            xp = np.concatenate([x[b, TO:], x[b, :TO]], axis=0)
            pos = np.concatenate([np.arange(TO, TA), np.arange(0, TO)]).astype(np.float32)
            kb = np.zeros(TA, np.float32)
        ang = (pos.astype(np.float32)[:, None] * invf32[None, :]).astype(np.float32)
        sint = np.sin(ang).astype(np.float32)
        cost = np.cos(ang).astype(np.float32)
        sint_d = np.ascontiguousarray(sint.reshape(NTA, 128, 64).transpose(1, 0, 2).reshape(128, NTA * 64))
        cost_d = np.ascontiguousarray(cost.reshape(NTA, 128, 64).transpose(1, 0, 2).reshape(128, NTA * 64))
        maps.append({
            "x": np.ascontiguousarray(xp),
            "pos": np.ascontiguousarray(pos.reshape(NTA, 128).T),
            "kbias": np.ascontiguousarray(kb.reshape(NTA, 128).T),
            "wqkv": wqkv, "bqkv": bqkv, "wo": wo, "bo": bo,
            "wg": wg, "wu": wu, "wd": wd,
            "ident": ident, "h128": h128, "hic": hic, "tri": tri,
            "sint": sint_d, "cost": cost_d,
        })
    return maps


_QKV_PERM_C = None


def _qkv_perm_cached():
    global _QKV_PERM_C
    if _QKV_PERM_C is None:
        perm = np.zeros(3 * D, dtype=np.int64)
        j = np.arange(128)
        for h in range(H):
            perm[h * 128 + j] = h * 384 + j
            perm[D + h * 128 + j] = h * 384 + 128 + j
            perm[2 * D + h * 128 + j] = h * 384 + 256 + j
        _QKV_PERM_C = perm
    return _QKV_PERM_C


_BUILT = None


def kernel(**inputs):
    global _BUILT
    if _BUILT is None:
        _BUILT = build_program()
    nc, probes = _BUILT
    maps = _stage(inputs)
    res = run_bass_kernel_spmd(nc, maps, core_ids=list(range(8)))
    if PROBES:
        kernel.last_results = res.results
    out = np.zeros((B, S, D), np.float32)
    for c in range(8):
        b, half = divmod(c, 2)
        out[b, half * TO:(half + 1) * TO] = res.results[c]["out"]
    return out


# version 12
# speedup vs baseline: 1.3123x; 1.0493x over previous
import os
import math
import numpy as np
import ml_dtypes
from contextlib import ExitStack

import concourse.bass as bass
import concourse.mybir as mybir
from concourse.bass_utils import run_bass_kernel_spmd

BFD = ml_dtypes.bfloat16
F32 = mybir.dt.float32
BF16 = mybir.dt.bfloat16
AF = mybir.ActivationFunctionType
ALU = mybir.AluOpType
AX = mybir.AxisListType

B, S, D, H, I = 4, 4096, 2048, 16, 8192
NTO, NTA, DC, IC = 16, 32, 16, 64
TO, TA = NTO * 128, NTA * 128
NCQ, TQW = 4, 512
TT, NTT = 512, 4
EPS = 1e-5

C23 = 12582912.0
C52 = float(1.5 * 2 ** 52)
ISQHD = float(1.0 / math.sqrt(128.0))
FWHT_NORM = float(np.float32(1.0 / np.sqrt(8192.0)))
NEG_BIG = -1.0e30

_LN4 = math.log(10000.0) / 64.0
LN4_HI = float(np.float32(int(_LN4 * 2 ** 14) / 2 ** 14))
LN4_MID = float(np.float32(_LN4 - LN4_HI))
LN4_LO = float(np.float32(_LN4 - LN4_HI - float(np.float32(LN4_MID))))
_TPI = 2.0 * math.pi
TPI_HI = float(np.float32(int(_TPI * 2 ** 11) / 2 ** 11))
TPI_MID = float(np.float32(_TPI - TPI_HI))
TPI_LO = float(np.float32(_TPI - TPI_HI - float(np.float32(TPI_MID))))
INV2PI = float(1.0 / _TPI)
PI32 = float(np.float32(math.pi))
_HPI = math.pi / 2.0
HPI_HI = float(np.float32(int(_HPI * 2 ** 12) / 2 ** 12))
HPI_MID = float(np.float32(_HPI - HPI_HI))
HPI_LO = float(np.float32(_HPI - HPI_HI - float(np.float32(HPI_MID))))

PHASES = int(os.environ.get("MK_PHASES", "6"))
P2LIM = int(os.environ.get("MK_P2LIM", "99"))
SCOUT = os.environ.get("MK_SCOUT", "") == "1"
QD = int(os.environ.get("MK_QD", "0"))
PROBES = [s for s in os.environ.get("MK_PROBE", "").split(",") if s]

ENGS = ("sync", "gpsimd", "tensor", "vector", "scalar")


class Sem:
    __slots__ = ("h", "n")

    def __init__(self, ctx, nc, name):
        self.h = ctx.enter_context(nc.semaphore(name))
        self.n = 0


class Prog:
    def __init__(self, nc, ctx):
        self.nc = nc
        self.ctx = ctx
        self.ops = {e: [] for e in ENGS}
        self.dS = Sem(ctx, nc, "dS")
        self.dG = Sem(ctx, nc, "dG")
        self.cT = Sem(ctx, nc, "cT")
        self.cV = Sem(ctx, nc, "cV")
        self.cS = Sem(ctx, nc, "cS")
        self.cG = Sem(ctx, nc, "cG")
        self.dA = Sem(ctx, nc, "dA")

    def em(self, eng, fn):
        self.ops[eng].append(fn)

    def wait(self, eng, sem, n=None):
        v = sem.n if n is None else n
        if v <= 0:
            return
        self.em(eng, lambda e, s=sem.h, v=v: e.wait_ge(s, v))

    def dma(self, eng, out, in_, **kw):
        sem = {"sync": self.dS, "gpsimd": self.dG, "scalar": self.dA}[eng]
        sem.n += 16
        self.em(eng, lambda e, o=out, i=in_, s=sem.h, kw=kw:
                e.dma_start(out=o, in_=i, **kw).then_inc(s, 16))
        return sem.n

    def _op(self, eng, sem, fn, inc):
        if inc:
            sem.n += 1
            self.em(eng, lambda e, f=fn, s=sem.h: f(e).then_inc(s, 1))
        else:
            self.em(eng, lambda e, f=fn: f(e))
        return sem.n

    def top(self, fn, inc=True):
        return self._op("tensor", self.cT, fn, inc)

    def vop(self, fn, inc=True):
        return self._op("vector", self.cV, fn, inc)

    def sop(self, fn, inc=True):
        return self._op("scalar", self.cS, fn, inc)

    def gop(self, fn, inc=True):
        return self._op("gpsimd", self.cG, fn, inc)

    def vchain(self, fns):
        for fn in fns:
            n = self.vop(fn)
            self.wait("vector", self.cV, n)
        return self.cV.n


def _had(n):
    h = np.array([[1.0]])
    while h.shape[0] < n:
        h = np.block([[h, h], [h, -h]])
    return h


def build_program():
    nc = bass.Bass("TRN2", target_bir_lowering=False, debug=False,
                   detect_race_conditions=False)
    ctx = ExitStack()

    t_x = nc.dram_tensor("x", [TA, D], F32, kind="ExternalInput").ap()
    t_pos = nc.dram_tensor("pos", [128, NTA], F32, kind="ExternalInput").ap()
    t_kb = nc.dram_tensor("kbias", [128, NTA], F32, kind="ExternalInput").ap()
    t_wqkv = nc.dram_tensor("wqkv", [D, 3 * D], F32, kind="ExternalInput").ap()
    t_bqkv = nc.dram_tensor("bqkv", [1, 3 * D], F32, kind="ExternalInput").ap()
    t_wo = nc.dram_tensor("wo", [D, D], F32, kind="ExternalInput").ap()
    t_bo = nc.dram_tensor("bo", [1, D], F32, kind="ExternalInput").ap()
    t_wg = nc.dram_tensor("wg", [I, D], F32, kind="ExternalInput").ap()
    t_wu = nc.dram_tensor("wu", [I, D], F32, kind="ExternalInput").ap()
    t_wd = nc.dram_tensor("wd", [D, I], F32, kind="ExternalInput").ap()
    t_id = nc.dram_tensor("ident", [128, 128], BF16, kind="ExternalInput").ap()
    t_h128 = nc.dram_tensor("h128", [128, 128], BF16, kind="ExternalInput").ap()
    t_hic = nc.dram_tensor("hic", [IC, IC], BF16, kind="ExternalInput").ap()
    t_tri = nc.dram_tensor("tri", [128, 128], BF16, kind="ExternalInput").ap()
    t_sin = nc.dram_tensor("sint", [128, NTA * 64], F32, kind="ExternalInput").ap()
    t_cos = nc.dram_tensor("cost", [128, NTA * 64], F32, kind="ExternalInput").ap()
    t_out = nc.dram_tensor("out", [TO, D], F32, kind="ExternalOutput").ap()

    SK = "ExternalOutput" if SCOUT else "Internal"
    t_kT = nc.dram_tensor("kT_sc", [H * 128, TA], BF16, kind=SK).ap()
    t_qT = nc.dram_tensor("qT_sc", [H * 128, TO], BF16, kind=SK).ap()
    t_v = nc.dram_tensor("v_sc", [TA, D], BF16, kind=SK).ap()
    t_x1 = nc.dram_tensor("x1_sc", [TO, D], F32, kind=SK).ap()
    t_a2qT = nc.dram_tensor("a2qT_sc", [D, TO], BF16, kind=SK).ap()
    t_wgqT = nc.dram_tensor("wgqT_sc", [D, I], BF16, kind="Internal").ap()
    t_wuqT = nc.dram_tensor("wuqT_sc", [D, I], BF16, kind="Internal").ap()
    t_wdqT = nc.dram_tensor("wdqT_sc", [I, D], BF16, kind="Internal").ap()
    t_Tsc = nc.dram_tensor("T_sc", [I, NTT * TT], BF16, kind="Internal").ap()
    t_a3 = nc.dram_tensor("a3_sc", [TT, I], BF16, kind="Internal").ap()
    t_nrm = nc.dram_tensor("nrm_sc", [128, 32], F32, kind="Internal").ap()
    t_nrm2 = nc.dram_tensor("nrm2_sc", [32, 1], F32, kind="Internal").ap()

    probes = {}

    def mkprobe(name, shape, dtype=F32):
        if name in PROBES:
            probes[name] = nc.dram_tensor("pb_" + name, list(shape), dtype,
                                          kind="ExternalOutput").ap()
            return probes[name]
        return None

    NBF, NF32 = 65536, 12288
    bfA = ctx.enter_context(nc.sbuf_tensor("bfA", [128, NBF], BF16)).ap()
    fA = ctx.enter_context(nc.sbuf_tensor("fA", [128, NF32], F32)).ap()
    sb_id = ctx.enter_context(nc.sbuf_tensor("sb_id", [128, 128], BF16)).ap()
    sb_h128 = ctx.enter_context(nc.sbuf_tensor("sb_h128", [128, 128], BF16)).ap()
    sb_hic = ctx.enter_context(nc.sbuf_tensor("sb_hic", [IC, IC], BF16)).ap()
    sb_tri = ctx.enter_context(nc.sbuf_tensor("sb_tri", [128, 128], BF16)).ap()
    sb_S = ctx.enter_context(nc.sbuf_tensor("sb_S", [128, NTA * 64], F32)).ap()
    sb_C = ctx.enter_context(nc.sbuf_tensor("sb_C", [128, NTA * 64], F32)).ap()
    sb_Ar = ctx.enter_context(nc.sbuf_tensor("sb_Ar", [128, 512], F32)).ap()
    sb_Br = ctx.enter_context(nc.sbuf_tensor("sb_Br", [128, 512], F32)).ap()
    sm = ctx.enter_context(nc.sbuf_tensor("sm", [128, 256], F32)).ap()
    sm2 = ctx.enter_context(nc.sbuf_tensor("sm2", [128, 1408], F32)).ap()
    sb_Tr = ctx.enter_context(nc.sbuf_tensor("sb_Tr", [IC, 2048], BF16)).ap()

    PH, NP1 = 0, 1
    S1, SSQ, NEGM, RSTD, TPA, TPB = 8, 9, 10, 11, 12, 13
    MXC, SCL, SINV, NPS = 16, 17, 18, 19
    MG, MU, THC, NTHC = 24, 25, 26, 27
    INVF = 64
    NRMQ, NRMK, BIAS = 0, 256, 768
    HMX, POSC, KBC, MBC = 1280, 1312, 1344, 1376

    p = Prog(nc, ctx)
    dS, dG, cT, cV, cS, cG = p.dS, p.dG, p.cT, p.cV, p.cS, p.cG

    smc = lambda c: sm[:, c:c + 1]

    p.dma("sync", sb_id[:, :], t_id)
    p.dma("sync", sb_h128[:, :], t_h128)
    p.dma("sync", sb_hic[:, :], t_hic)
    p.dma("sync", sb_tri[:, :], t_tri)
    p.dma("sync", sb_S[:, :], t_sin)
    p.dma("sync", sb_C[:, :], t_cos)
    p.dma("sync", sm2[:, POSC:POSC + 32], t_pos)
    n0 = p.dma("sync", sm2[:, KBC:KBC + 32], t_kb)
    p.wait("vector", dS, n0)

    p.vchain([
        lambda e: e.memset(smc(PH), 0.5),
        lambda e: e.tensor_scalar_add(smc(PH), smc(PH), C23),
        lambda e: e.tensor_scalar_sub(smc(PH), smc(PH), C23),
        lambda e: e.tensor_scalar_mul(smc(NP1), smc(PH), -2.0),
    ])
    nv = p.vop(lambda e: e.nop())
    p.wait("vector", cV, nv)

    A1T, WS, QWK, KST, QST, JNK = 0, 32768, 40960, 43008, 51200, 59392
    a1T = lambda dc, tl: bfA[:, A1T + dc * 2048 + tl * 128:A1T + dc * 2048 + (tl + 1) * 128]
    w_sb = bfA[:, WS:WS + 8192].rearrange("p (c j) -> p c j", j=512)
    qkv_bf = bfA[:, QWK:QWK + 512]
    rp2 = bfA[:, QWK + 512:QWK + 1024]
    t1r = bfA[:, QWK + 1024:QWK + 1536]
    a1bf = bfA[:, JNK:JNK + 2048]
    junkb = bfA[:, JNK + 2048:JNK + 4096]

    xbuf = [fA[:, 0:2048], fA[:, 2048:4096]]
    a1f = fA[:, 4096:6144]
    b_rep = fA[:, 6144:6656]
    onesr = fA[:, 6656:6784]
    brow = fA[0:1, 6784:7296]

    psP = ExitStack()
    psQ = [psP.enter_context(nc.psum_tensor(f"psQ{i}", [128, 512], F32)).ap() for i in range(2)]
    psTb = [psP.enter_context(nc.psum_tensor(f"psTb{i}", [128, 128], BF16)).ap() for i in range(2)]
    psR = psP.enter_context(nc.psum_tensor("psR", [128, 512], F32)).ap()

    nv = p.vop(lambda e: e.memset(onesr[0:1, :], 1.0))
    p.wait("tensor", cV, nv)

    def ln_stats(xap, nfeat, junk=None):
        jk = junkb if junk is None else junk
        nv_ = p.vop(lambda e, x_=xap: e.tensor_reduce(smc(S1), x_, AX.X, ALU.add))
        p.wait("scalar", cV, nv_)
        ns_ = p.sop(lambda e, x_=xap, n=nfeat, jk=jk: e.activation(
            jk[:, 0:n], x_, AF.Square, bias=0.0, scale=1.0, accum_out=smc(SSQ)))
        p.wait("vector", cS, ns_)
        p.vchain([
            lambda e: e.tensor_scalar_mul(smc(NEGM), smc(S1), -1.0 / nfeat),
            lambda e: e.tensor_tensor(smc(TPA), smc(NEGM), smc(NEGM), ALU.mult),
            lambda e: e.scalar_tensor_tensor(smc(TPB), smc(SSQ), 1.0 / nfeat, smc(TPA), ALU.mult, ALU.subtract),
            lambda e: e.tensor_scalar_add(smc(TPB), smc(TPB), EPS),
            lambda e: e.reciprocal(smc(TPA), smc(TPB)),
        ])
        p.wait("scalar", cV, cV.n)
        ns_ = p.sop(lambda e: e.activation(smc(RSTD), smc(TPA), AF.Sqrt, bias=0.0, scale=1.0))
        p.wait("vector", cS, ns_)

    n_tables_done = cV.n
    xcons = [n_tables_done, n_tables_done]
    evm = [0, 0]
    qevm = [0, 0]
    tevm = [0, 0]

    for chunk in range(2):
        for tl in range(16):
            j = chunk * 16 + tl
            bi = tl % 2
            p.wait("sync", cV, xcons[bi])
            nld = p.dma("sync", xbuf[bi], t_x[j * 128:(j + 1) * 128, :])
            p.wait("vector", dS, nld)
            ln_stats(xbuf[bi], D)
            p.vop(lambda e, b_=bi: e.tensor_scalar(a1f, xbuf[b_], smc(NEGM), smc(RSTD), ALU.add, ALU.mult), inc=False)
            nv_ = p.vop(lambda e: e.tensor_copy(a1bf, a1f))
            xcons[bi] = nv_
            p.wait("tensor", cV, nv_)
            for dc in range(DC):
                pi = dc % 2
                p.wait("tensor", cV, evm[pi])
                nt_ = p.top(lambda e, dc=dc, pi=pi: e.transpose(
                    psTb[pi][:, :], a1bf[:, dc * 128:(dc + 1) * 128], sb_id))
                p.wait("vector", cT, nt_)
                evm[pi] = p.vop(lambda e, dc=dc, tl=tl, pi=pi: e.tensor_copy(
                    a1T(dc, tl), psTb[pi][:, :]))

        pb = mkprobe(f"a1T_c{chunk}", (128, 2048))
        if pb is not None:
            nv = p.vop(lambda e: e.nop())
            p.wait("gpsimd", cV, nv)
            p.dma("gpsimd", pb, bfA[:, A1T:A1T + 2048])

        wqkv_r = t_wqkv.rearrange("(c p) o -> p c o", p=128)
        oblks = list(range(12)) if chunk == 0 else list(range(4, 12))

        wcons = 0
        psr_prev = 0
        for obi, ob in enumerate(oblks):
            p.wait("gpsimd", cT, wcons)
            nwl = p.dma("gpsimd", w_sb[:, :, :], wqkv_r[:, :, ob * 512:(ob + 1) * 512])
            if True:
                p.wait("sync", cT, psr_prev)
                nbl = p.dma("sync", brow, t_bqkv[:, ob * 512:(ob + 1) * 512])
                p.wait("tensor", dS, nbl)
                nt_ = p.top(lambda e: e.matmul(psR[:, :], onesr[0:1, :], brow, start=True, stop=True))
                psr_prev = nt_
                p.wait("vector", cT, nt_)
                nbr = p.vop(lambda e: e.tensor_copy(b_rep, psR[:, :]))
            else:
                nbr = cV.n

            is_q, is_k = ob < 4, 4 <= ob < 8
            for tl in range(16):
                j = chunk * 16 + tl
                pi = tl % 2
                p.wait("tensor", dG, nwl)
                p.wait("tensor", cV, qevm[pi])
                for dc in range(DC):
                    ntm = p.top(lambda e, dc=dc, pi=pi, tl=tl: e.matmul(
                        psQ[pi][:, :], a1T(dc, tl), w_sb[:, dc, :],
                        start=(dc == 0), stop=(dc == DC - 1)), inc=(dc == DC - 1))
                wcons = ntm
                p.wait("vector", cT, ntm)
                p.wait("vector", cV, nbr)
                nv_ = p.vop(lambda e, pi=pi: e.scalar_tensor_tensor(
                    qkv_bf, psQ[pi][:, :], 1.0, b_rep, ALU.mult, ALU.add))
                qevm[pi] = nv_
                if is_q or is_k:
                    p.wait("scalar", cV, nv_)
                    for hh in range(4):
                        if is_q:
                            col = NRMQ + (ob * 4 + hh) * 16 + tl
                        else:
                            col = NRMK + ((ob - 4) * 4 + hh) * 32 + j
                        nsq = p.sop(lambda e, hh=hh, col=col: e.activation(
                            junkb[:, hh * 128:(hh + 1) * 128],
                            qkv_bf[:, hh * 128:(hh + 1) * 128], AF.Square,
                            bias=0.0, scale=1.0, accum_out=sm2[:, col:col + 1]), inc=(hh == 3))
                    for hh in range(4):
                        p.vop(lambda e, hh=hh, j=j: e.tensor_copy(
                            sb_Ar[:, hh * 128:hh * 128 + 64], sb_S[:, j * 64:(j + 1) * 64]), inc=False)
                        p.vop(lambda e, hh=hh, j=j: e.tensor_copy(
                            sb_Ar[:, hh * 128 + 64:(hh + 1) * 128], sb_S[:, j * 64:(j + 1) * 64]), inc=False)
                        p.vop(lambda e, hh=hh, j=j: e.tensor_scalar_mul(
                            sb_Br[:, hh * 128:hh * 128 + 64], sb_C[:, j * 64:(j + 1) * 64], -1.0), inc=False)
                        p.vop(lambda e, hh=hh, j=j: e.tensor_copy(
                            sb_Br[:, hh * 128 + 64:(hh + 1) * 128], sb_C[:, j * 64:(j + 1) * 64]), inc=False)
                    nab = p.vop(lambda e, j=j: e.tensor_copy(sb_Br[:, 64:128], sb_C[:, j * 64:(j + 1) * 64]))
                    p.wait("vector", cV, nab)
                    p.wait("vector", cS, nsq)
                    qv = qkv_bf.rearrange("p (h q) -> p h q", q=128)
                    rv = rp2.rearrange("p (h q) -> p h q", q=128)
                    bv = sb_Br.rearrange("p (h q) -> p h q", q=128)
                    p.vop(lambda e: e.tensor_tensor(t1r, qkv_bf, sb_Ar, ALU.mult), inc=False)
                    p.vop(lambda e: e.tensor_tensor(rv[:, :, 0:64], qv[:, :, 64:128], bv[:, :, 0:64], ALU.mult), inc=False)
                    p.vop(lambda e: e.tensor_tensor(rv[:, :, 64:128], qv[:, :, 0:64], bv[:, :, 64:128], ALU.mult), inc=False)
                    nv_ = p.vop(lambda e: e.tensor_tensor(qkv_bf, t1r, rp2, ALU.add))
                    qevm[pi] = nv_
                    p.wait("tensor", cV, nv_)
                    stg = QST if is_q else KST
                    for hh in range(4):
                        ti = hh % 2
                        p.wait("tensor", cV, tevm[ti])
                        nt2 = p.top(lambda e, hh=hh, ti=ti: e.transpose(
                            psTb[ti][:, :], qkv_bf[:, hh * 128:(hh + 1) * 128], sb_id))
                        p.wait("vector", cT, nt2)
                        tevm[ti] = p.vop(lambda e, hh=hh, ti=ti, stg=stg, tl=tl: e.tensor_copy(
                            bfA[:, stg + hh * 2048 + tl * 128:stg + hh * 2048 + (tl + 1) * 128],
                            psTb[ti][:, :]))
                else:
                    p.wait("gpsimd", cV, nv_)
                    nst = p.dma("gpsimd", t_v[j * 128:(j + 1) * 128, (ob - 8) * 512:(ob - 7) * 512], qkv_bf)
                    p.wait("vector", dG, nst)
            if is_q or is_k:
                nvf = p.vop(lambda e: e.nop())
                p.wait("gpsimd", cV, nvf)
                stg = QST if is_q else KST
                tsc = t_qT if is_q else t_kT
                hb = (ob - (0 if is_q else 4)) * 4
                cb = chunk * 2048
                width = 2048
                for hh in range(4):
                    hr = (hb + hh) * 128
                    p.dma("gpsimd", tsc[hr:hr + 128, cb:cb + width],
                          bfA[:, stg + hh * 2048:stg + hh * 2048 + width])
                p.wait("vector", dG, dG.n)

    psP.close()

    if PHASES < 3:
        return _finish_stub(nc, ctx, p, t_out, fA, probes)

    KTH, QTH, VP, PT, AOUT = 0, 8192, 12288, 20608, 32768
    aout = lambda jj: bfA[:, AOUT + jj * 2048:AOUT + (jj + 1) * 2048]

    psP3 = ExitStack()
    psS = [psP3.enter_context(nc.psum_tensor(f"psS{i}", [128, 512], F32)).ap() for i in range(2)]
    psPV = [psP3.enter_context(nc.psum_tensor(f"psPV{i}", [128, 129], F32)).ap() for i in range(4)]
    psBC = psP3.enter_context(nc.psum_tensor("psBC", [128, 16], F32)).ap()

    p.wait("vector", cS, cS.n)
    for h in range(H):
        p.vop(lambda e, h=h: e.tensor_reduce(
            sm2[:, HMX + h:HMX + h + 1], sm2[:, NRMQ + h * 16:NRMQ + (h + 1) * 16], AX.X, ALU.max), inc=False)
        p.vop(lambda e, h=h: e.tensor_reduce(
            sm2[:, HMX + 16 + h:HMX + 17 + h], sm2[:, NRMK + h * 32:NRMK + (h + 1) * 32], AX.X, ALU.max), inc=False)
    nv = p.vop(lambda e: e.nop())
    p.wait("gpsimd", cV, nv)
    n1 = p.dma("gpsimd", t_nrm, sm2[:, HMX:HMX + 32])
    p.wait("sync", dG, n1)
    nrm_rows = fA[0:32, 0:128]
    n2 = p.dma("sync", nrm_rows, t_nrm.rearrange("p c -> c p"), allow_slow_non_contiguous=True)
    p.wait("vector", dS, n2)
    mx32 = fA[0:32, 128:129]
    p.vchain([lambda e: e.tensor_reduce(mx32, nrm_rows, AX.X, ALU.max)])
    nv = p.vop(lambda e: e.nop())
    p.wait("gpsimd", cV, nv)
    n3 = p.dma("gpsimd", t_nrm2, mx32)
    p.wait("sync", dG, n3)
    mrow = fA[0:1, 192:224]
    n4 = p.dma("sync", mrow, t_nrm2.rearrange("a b -> b a"), allow_slow_non_contiguous=True)
    p.wait("vector", dS, n4)
    m2row = fA[0:1, 224:240]
    p.vchain([lambda e: e.tensor_tensor(m2row, fA[0:1, 192:208], fA[0:1, 208:224], ALU.mult)])
    p.wait("scalar", cV, cV.n)
    mrow2 = fA[0:1, 240:256]
    ns = p.sop(lambda e: e.activation(mrow2, m2row, AF.Sqrt, bias=0.0, scale=1.0 / 128.0))
    p.wait("tensor", cS, ns)
    nt = p.top(lambda e: e.matmul(psBC[:, :], onesr[0:1, :], mrow2, start=True, stop=True))
    p.wait("vector", cT, nt)
    nv = p.vop(lambda e: e.tensor_copy(sm2[:, MBC:MBC + 16], psBC[:, :]))
    p.wait("vector", cV, nv)
    for h in range(H):
        p.vop(lambda e, h=h: e.tensor_scalar(
            sm2[:, BIAS + h * 32:BIAS + (h + 1) * 32], sm2[:, KBC:KBC + 32],
            sm2[:, MBC + h:MBC + h + 1], None, ALU.subtract), inc=False)
    nbias = p.vop(lambda e: e.nop())
    p.wait("scalar", cV, nbias)
    p.wait("tensor", cV, nbias)

    pb = mkprobe("Mbc", (128, 16))
    if pb is not None:
        p.wait("gpsimd", cV, nbias)
        p.dma("gpsimd", pb, sm2[:, MBC:MBC + 16])

    p.wait("sync", dG, dG.n)

    ldm = [0, 0]
    vmem = [0, 0]
    sexp = [0, 0]
    pvm = [0, 0]
    evq = 0
    vplus = lambda hb: bfA[:, VP + hb * 4128:VP + (hb + 1) * 4128].rearrange("p (k c) -> p k c", c=129)
    for h in range(H):
        hb = h % 2
        p.wait("sync", cT, ldm[hb])
        nk = p.dma("sync", bfA[:, KTH + hb * 4096:KTH + (hb + 1) * 4096], t_kT[h * 128:(h + 1) * 128, :])
        nq = p.dma("sync", bfA[:, QTH + hb * 2048:QTH + (hb + 1) * 2048], t_qT[h * 128:(h + 1) * 128, :])
        p.wait("vector", cT, vmem[hb])
        nms = p.vop(lambda e, hb=hb: e.memset(vplus(hb)[:, :, 128:129], 1.0))
        p.wait("sync", cV, nms)
        nvl = p.dma("sync", vplus(hb)[:, :, 0:128],
                    t_v.rearrange("(k p) d -> p k d", p=128)[:, :, h * 128:(h + 1) * 128])
        kT_h = bfA[:, KTH + hb * 4096:KTH + (hb + 1) * 4096]
        qT_h = bfA[:, QTH + hb * 2048:QTH + (hb + 1) * 2048]
        first_pe = True
        for cq in range(NCQ):
            klist = list(range(0, 4 * cq + 4)) + list(range(16, 32))
            nk_t = len(klist)
            for ki, kt in enumerate(klist):
                sb_ = ki % 2
                if first_pe:
                    p.wait("tensor", dS, nvl)
                    first_pe = False
                p.wait("tensor", cS, sexp[sb_])
                nmt = p.top(lambda e, kt=kt, cq=cq, sb_=sb_, kT_h=kT_h, qT_h=qT_h: e.matmul(
                    psS[sb_][:, :], kT_h[:, kt * 128:(kt + 1) * 128],
                    qT_h[:, cq * 512:(cq + 1) * 512], start=True, stop=True))
                diag = 4 * cq <= kt <= 4 * cq + 3
                if diag:
                    z = kt - 4 * cq
                    p.wait("vector", cT, nmt)
                    if z > 0:
                        p.vop(lambda e, z=z, sb_=sb_: e.tensor_scalar_add(
                            psS[sb_][:, 0:z * 128], psS[sb_][:, 0:z * 128], NEG_BIG), inc=False)
                    nmk = p.vop(lambda e, z=z, sb_=sb_: e.tensor_tensor(
                        psS[sb_][:, z * 128:(z + 1) * 128], psS[sb_][:, z * 128:(z + 1) * 128],
                        sb_tri, ALU.add))
                    p.wait("scalar", cV, nmk)
                p.wait("scalar", cT, nmt)
                p.wait("scalar", cT, pvm[sb_])
                nse = p.sop(lambda e, sb_=sb_, h=h, kt=kt: e.activation(
                    bfA[:, PT + sb_ * 512:PT + (sb_ + 1) * 512], psS[sb_][:, :], AF.Exp,
                    bias=sm2[:, BIAS + h * 32 + kt:BIAS + h * 32 + kt + 1], scale=ISQHD))
                sexp[sb_] = nse
                p.wait("tensor", cS, nse)
                if ki == 0:
                    p.wait("tensor", cV, evq)
                for sub in range(4):
                    pvm[sb_] = p.top(lambda e, sub=sub, sb_=sb_, kt=kt, hb=hb, ki=ki, last=(ki == nk_t - 1): e.matmul(
                        psPV[sub][:, :], bfA[:, PT + sb_ * 512 + sub * 128:PT + sb_ * 512 + (sub + 1) * 128],
                        vplus(hb)[:, kt, :], start=(ki == 0), stop=last,
                        skip_group_check=True), inc=(sub == 3))
            ldm[hb] = pvm[0] if pvm[0] > pvm[1] else pvm[1]
            vmem[hb] = ldm[hb]
            p.wait("vector", cT, max(pvm))
            for sub in range(4):
                jj = cq * 4 + sub
                p.vchain([lambda e, sub=sub: e.reciprocal(smc(TPA), psPV[sub][:, 128:129])])
                evq = p.vop(lambda e, sub=sub, jj=jj, h=h: e.tensor_scalar(
                    aout(jj)[:, h * 128:(h + 1) * 128], psPV[sub][:, 0:128],
                    smc(TPA), None, ALU.mult))

    psP3.close()

    pb = mkprobe("aout", (128, 2048))
    if pb is not None:
        nv = p.vop(lambda e: e.tensor_copy(fA[:, 0:2048], aout(0)))
        p.wait("gpsimd", cV, nv)
        p.dma("gpsimd", pb, fA[:, 0:2048])

    if PHASES < 4:
        return _finish_stub(nc, ctx, p, t_out, fA, probes)

    ATT = 0
    attT = lambda dc, jj: bfA[:, ATT + dc * 2048 + jj * 128:ATT + dc * 2048 + (jj + 1) * 128]

    psP4 = ExitStack()
    psTb4 = [psP4.enter_context(nc.psum_tensor(f"psT4{i}", [128, 128], BF16)).ap() for i in range(2)]
    psO = [psP4.enter_context(nc.psum_tensor(f"psO{i}", [128, 512], F32)).ap() for i in range(2)]
    psR4 = psP4.enter_context(nc.psum_tensor("psR4", [128, 512], F32)).ap()

    evm4 = [0, 0]
    for jj in range(NTO):
        for dc in range(DC):
            ti = dc % 2
            p.wait("tensor", cV, evm4[ti])
            nt_ = p.top(lambda e, jj=jj, dc=dc, ti=ti: e.transpose(
                psTb4[ti][:, :], aout(jj)[:, dc * 128:(dc + 1) * 128], sb_id))
            p.wait("vector", cT, nt_)
            evm4[ti] = p.vop(lambda e, jj=jj, dc=dc, ti=ti: e.tensor_copy(
                attT(dc, jj), psTb4[ti][:, :]))
    p.wait("gpsimd", cV, max(evm4))
    WO = AOUT
    wo_sb = bfA[:, WO:WO + 32768].rearrange("p (c o) -> p c o", o=2048)
    nwo = p.dma("gpsimd", wo_sb[:, :, :], t_wo.rearrange("(c p) o -> p c o", p=128))
    bo_rep = fA[:, 4096:6144]
    psr_prev = 0
    for obx in range(4):
        p.wait("sync", cT, psr_prev)
        nbl = p.dma("sync", brow, t_bo[:, obx * 512:(obx + 1) * 512])
        p.wait("tensor", dS, nbl)
        nt_ = p.top(lambda e: e.matmul(psR4[:, :], onesr[0:1, :], brow, start=True, stop=True))
        psr_prev = nt_
        p.wait("vector", cT, nt_)
        nv = p.vop(lambda e, obx=obx: e.tensor_copy(bo_rep[:, obx * 512:(obx + 1) * 512], psR4[:, :]))
        p.wait("tensor", cV, nv)
    p.wait("tensor", dG, nwo)

    proj = fA[:, 0:2048]
    x1t = fA[:, 2048:4096]
    a2t = fA[:, 0:2048]
    qc1 = fA[:, 8192:10240]
    qc2 = fA[:, 10240:12288]

    oev = 0
    oevs = [0, 0]
    qc1_free = 0
    for jj in range(NTO):
        p.wait("tensor", cV, oev)
        for obx in range(4):
            pi = obx % 2
            p.wait("tensor", cV, oevs[pi])
            for dc in range(DC):
                ntm = p.top(lambda e, dc=dc, jj=jj, obx=obx, pi=pi: e.matmul(
                    psO[pi][:, :], attT(dc, jj), wo_sb[:, dc, obx * 512:(obx + 1) * 512],
                    start=(dc == 0), stop=(dc == DC - 1)), inc=(dc == DC - 1))
            p.wait("vector", cT, ntm)
            nv = p.vop(lambda e, obx=obx, pi=pi: e.scalar_tensor_tensor(
                proj[:, obx * 512:(obx + 1) * 512], psO[pi][:, :], 1.0,
                bo_rep[:, obx * 512:(obx + 1) * 512], ALU.mult, ALU.add))
            oevs[pi] = nv
        oev = nv
        ln_stats(proj, D, junk=qc2)
        p.wait("sync", cV, qc1_free)
        nxl = p.dma("sync", qc1, t_x[jj * 128:(jj + 1) * 128, :])
        p.vop(lambda e: e.tensor_scalar(qc2, proj, smc(NEGM), smc(RSTD), ALU.add, ALU.mult), inc=False)
        p.wait("vector", dS, nxl)
        nv = p.vop(lambda e: e.tensor_tensor(x1t, qc1, qc2, ALU.add))
        p.wait("gpsimd", cV, nv)
        nst = p.dma("gpsimd", t_x1[jj * 128:(jj + 1) * 128, :], x1t)
        ln_stats(x1t, D, junk=qc2)
        nv = p.vop(lambda e: e.tensor_scalar(a2t, x1t, smc(NEGM), smc(RSTD), ALU.add, ALU.mult))
        p.wait("vector", cV, nv)
        p.vchain([
            lambda e: e.tensor_reduce(smc(MXC), a2t, AX.X, ALU.max, apply_absolute_value=True),
            lambda e: e.tensor_scalar_max(smc(MXC), smc(MXC), EPS),
            lambda e: e.reciprocal(smc(TPA), smc(MXC)),
            lambda e: e.tensor_scalar_mul(smc(SCL), smc(TPA), 128.0),
            lambda e: e.tensor_scalar_mul(smc(SINV), smc(MXC), 1.0 / 128.0),
            lambda e: e.tensor_scalar(smc(NPS), smc(SCL), smc(NP1), None, ALU.mult),
        ])
        p.vop(lambda e: e.tensor_scalar(qc1, a2t, smc(SCL), C23, ALU.mult, ALU.add), inc=False)
        p.vop(lambda e: e.tensor_scalar(qc2, a2t, smc(SCL), C52, ALU.mult, ALU.add), inc=False)
        p.vop(lambda e: e.scalar_tensor_tensor(qc2, qc2, C52, qc1, ALU.subtract, ALU.add), inc=False)
        nqf = p.vop(lambda e: e.scalar_tensor_tensor(qc1, a2t, smc(NPS), qc2, ALU.mult, ALU.add))
        qc1_free = nqf
        p.vop(lambda e: e.tensor_scalar(qc1, qc1, C23, 127.0, ALU.subtract, ALU.min), inc=False)
        dst = bfA[:, ATT:ATT + 32768].rearrange("p (c t) -> p c t", t=2048)[:, :, jj * 128:(jj + 1) * 128]
        src3 = qc1.rearrange("p (c t) -> p c t", t=128)
        nv = p.vop(lambda e, dst=dst, src3=src3: e.tensor_scalar(dst, src3, smc(SINV), None, ALU.mult))
        p.wait("tensor", cV, nv)
        for dc in range(DC):
            ti = dc % 2
            p.wait("tensor", cV, evm4[ti])
            nt_ = p.top(lambda e, jj=jj, dc=dc, ti=ti: e.transpose(
                psTb4[ti][:, :], attT(dc, jj), sb_id))
            p.wait("vector", cT, nt_)
            evm4[ti] = p.vop(lambda e, jj=jj, dc=dc, ti=ti: e.tensor_copy(
                attT(dc, jj), psTb4[ti][:, :]))
        nv = p.vop(lambda e: e.nop())
        p.wait("sync", cV, nv)
        nst = p.dma("sync",
                    t_a2qT.rearrange("(c p) t -> p c t", p=128)[:, :, jj * 128:(jj + 1) * 128],
                    bfA[:, ATT:ATT + 32768].rearrange("p (c t) -> p c t", t=2048)[:, :, jj * 128:(jj + 1) * 128])
        p.wait("vector", dS, nst)

    psP4.close()

    pb = mkprobe("x1", (128, 2048))
    if pb is not None:
        p.wait("sync", dG, dG.n)
        n_ = p.dma("sync", qc1, t_x1[0:128, :])
        p.wait("gpsimd", dS, n_)
        p.dma("gpsimd", pb, qc1)

    if PHASES < 5:
        return _finish_stub(nc, ctx, p, t_out, fA, probes)

    psP5 = ExitStack()
    ps11 = psP5.enter_context(nc.psum_tensor("ps11", [1, 1], F32)).ap()
    psBC5 = psP5.enter_context(nc.psum_tensor("psBC5", [128, 1], F32)).ap()
    psT5 = [psP5.enter_context(nc.psum_tensor(f"psT5{i}", [128, 512], BF16)).ap() for i in range(2)]

    p.wait("sync", cV, cV.n)
    p.wait("sync", cT, cT.n)
    p.wait("scalar", cV, cV.n)
    p.wait("vector", cT, cT.n)
    onescol = fA[:, 6144:6145]
    nv = p.vop(lambda e: e.memset(onescol, 1.0))
    p.wait("tensor", cV, nv)
    wtiles = [fA[:, 0:2048], fA[:, 2048:4096]]
    absout = fA[:, 4096:6144]
    accall = sm2[:, 0:64]
    acc1 = sm[:, 32:33]
    val1 = fA[0:1, 6148:6149]
    q1b = bfA[:, 0:2048]
    q2b = bfA[:, 2048:4096]
    q3b = bfA[:, 4096:6144]
    accb = [bfA[:, 8192:16384], bfA[:, 16384:24576]]
    mtiles = [fA[:, 6784:8832], fA[:, 8832:10880]]
    tev5 = [0, 0]
    accst = [0, 0]
    scons = [0, 0]
    vcons = [0, 0]
    p.wait("scalar", dS, dS.n)
    p.wait("gpsimd", cV, cV.n)
    p.wait("gpsimd", cT, cT.n)
    p.wait("gpsimd", dS, dS.n)

    for mat, (t_w, t_wqT) in enumerate(((t_wg, t_wgqT), (t_wu, t_wuqT), (t_wd, t_wdqT))):
        nrows = I if mat < 2 else D
        ncols = D if mat < 2 else I
        ncc = ncols // 2048
        if mat < 2:
            slabs = [(rt, cc) for rt in range(nrows // 128) for cc in range(ncc)]
        else:
            slabs = [(rt, cc) for cc in range(ncc) for rt in range(nrows // 128)]
        if mat > 0:
            p.wait("scalar", cV, red_guard)
        for si, (rt, cc) in enumerate(slabs):
            bi = si % 2
            p.wait("gpsimd", cS, scons[bi])
            nld = p.dma("gpsimd", mtiles[bi],
                        t_w[rt * 128:(rt + 1) * 128, cc * 2048:(cc + 1) * 2048])
            p.wait("scalar", dG, nld)
            scons[bi] = p.sop(lambda e, bi=bi, si=si: e.activation(
                absout, mtiles[bi], AF.Abs, bias=0.0, scale=1.0,
                accum_out=sm2[:, si:si + 1]))
        p.wait("vector", cS, max(scons))
        p.vchain([lambda e, n=len(slabs): e.tensor_reduce(acc1, sm2[:, 0:n], AX.X, ALU.add)])
        p.vchain([lambda e, mat=mat: e.tensor_copy(sm[:, 40 + mat:41 + mat], acc1)])
        if mat == 0:
            p.vchain([lambda e: e.tensor_copy(sm2[:, 64:128], sm2[:, 0:64])])
        red_guard = cV.n
        p.wait("tensor", cV, cV.n)
        nt_ = p.top(lambda e: e.matmul(ps11[:, :], acc1, onescol, start=True, stop=True))
        p.wait("vector", cT, nt_)
        p.vchain([
            lambda e: e.tensor_copy(val1, ps11[:, :]),
            lambda e, nr=nrows, nc_=ncols: e.tensor_scalar_mul(val1, val1, 1.0 / (nr * nc_)),
            lambda e: e.tensor_scalar_max(val1, val1, EPS),
        ])
        p.wait("tensor", cV, cV.n)
        nt_ = p.top(lambda e: e.matmul(psBC5[:, :], onesr[0:1, :], val1, start=True, stop=True))
        p.wait("vector", cT, nt_)
        mc = MG if mat == 0 else (MU if mat == 1 else THC)
        p.vchain([
            lambda e, mc=mc: e.tensor_copy(sm[:, mc:mc + 1], psBC5[:, :]),
            lambda e, mc=mc: e.tensor_scalar_mul(smc(TPA), sm[:, mc:mc + 1], 0.5),
            lambda e: e.tensor_scalar_mul(smc(TPB), smc(TPA), -1.0),
        ])
        for si, (rt, cc) in enumerate(slabs):
            bi = si % 2
            gi = (si // 4) % 2
            rtl = si % 4
            p.wait("sync", cV, vcons[bi])
            nld = p.dma("sync", wtiles[bi],
                        t_w[rt * 128:(rt + 1) * 128, cc * 2048:(cc + 1) * 2048])
            p.wait("vector", dS, nld)
            p.vop(lambda e, bi=bi: e.tensor_scalar(
                q2b, wtiles[bi], smc(TPB), None, ALU.is_lt), inc=False)
            nv = p.vop(lambda e, bi=bi: e.scalar_tensor_tensor(
                q3b, wtiles[bi], smc(TPA), q2b, ALU.is_gt, ALU.subtract))
            vcons[bi] = nv
            p.wait("tensor", cV, nv)
            if rtl == 0:
                p.wait("scalar", dG, accst[gi])
            for dq in range(4):
                ti = dq % 2
                p.wait("tensor", cS, tev5[ti])
                for k4 in range(4):
                    dc = dq * 4 + k4
                    ntt = p.top(lambda e, ti=ti, dc=dc, k4=k4: e.transpose(
                        psT5[ti][:, k4 * 128:(k4 + 1) * 128], q3b[:, dc * 128:(dc + 1) * 128],
                        sb_id), inc=(k4 == 3))
                p.wait("scalar", cT, ntt)
                tev5[ti] = p.sop(lambda e, ti=ti, dq=dq, gi=gi, rtl=rtl: e.activation(
                    accb[gi].rearrange("p (g k) -> p g k", k=512)[:, dq * 4:(dq + 1) * 4, rtl * 128:(rtl + 1) * 128],
                    psT5[ti][:, :], AF.Copy, bias=0.0, scale=1.0))
            p.wait("vector", cT, ntt)
            if rtl == 3:
                p.wait("gpsimd", cS, max(tev5))
                rt0 = rt - 3
                if mat < 2:
                    dstp = t_wqT.rearrange("(g p) i -> p g i", p=128)[:, :, rt0 * 128:rt0 * 128 + 512]
                else:
                    dstp = t_wqT.rearrange("(g p) d -> p g d", p=128)[:, cc * 16:(cc + 1) * 16, rt0 * 128:rt0 * 128 + 512]
                accst[gi] = p.dma("gpsimd", dstp,
                                  accb[gi].rearrange("p (g k) -> p g k", k=512))

    psP5.close()

    if PHASES < 6:
        return _finish_stub(nc, ctx, p, t_out, fA, probes)

    A2Q, WGA, WUA, SGA, ITM, TR1, INTER, YB = 0, 32768, 40960, 49152, 50176, 51200, 24576, 57344
    a2q_al = bfA[:, A2Q:A2Q + 32768].rearrange("p (c t) -> p c t", t=2048)

    x1sl = fA[:, 0:2048]
    chn = fA[:, 2048:4096]
    rows = lambda ts_: fA[:, 4096 + ts_ * 2048:4096 + (ts_ + 1) * 2048]

    p.wait("sync", dG, dG.n)
    p.wait("scalar", dG, dG.n)

    psA_ = ExitStack()
    psGs = [psA_.enter_context(nc.psum_tensor(f"psG{i}", [128, 512], F32)).ap() for i in range(2)]
    psUs = [psA_.enter_context(nc.psum_tensor(f"psU{i}", [128, 512], F32)).ap() for i in range(2)]
    psF = [psA_.enter_context(nc.psum_tensor(f"psF{i}", [128, 512], F32)).ap() for i in range(2)]
    nld = p.dma("sync", a2q_al, t_a2qT.rearrange("(c p) t -> p c t", p=128))
    p.wait("tensor", dS, nld)
    guevs = [0, 0]
    nsgs = [0, 0]
    itcons = [0, 0]
    fevac = [0, 0]
    fst = [0, 0, 0, 0]
    cpar = 0
    wcons = 0
    for ib in range(16):
        p.wait("sync", cT, wcons)
        p.dma("sync", bfA[:, WGA:WGA + 8192].rearrange("p (c k) -> p c k", k=512),
              t_wgqT.rearrange("(c p) i -> p c i", p=128)[:, :, ib * 512:(ib + 1) * 512])
        nlu = p.dma("sync", bfA[:, WUA:WUA + 8192].rearrange("p (c k) -> p c k", k=512),
                    t_wuqT.rearrange("(c p) i -> p c i", p=128)[:, :, ib * 512:(ib + 1) * 512])
        p.wait("tensor", dS, nlu)
        for tt in range(NTT):
            for it in range(4):
                c = ib * 4 + it
                pr = cpar % 2
                ri = cpar % 4
                cpar += 1
                psG, psU, psFc = psGs[pr], psUs[pr], psF[pr]
                sgb = bfA[:, SGA + pr * 512:SGA + (pr + 1) * 512]
                itb = bfA[:, ITM + pr * 512:ITM + (pr + 1) * 512]
                p.wait("tensor", cV, guevs[pr])
                p.wait("tensor", cS, nsgs[pr])
                for dc in range(DC):
                    ng = p.top(lambda e, dc=dc, it=it, tt=tt, psG=psG: e.matmul(
                        psG[:, :], bfA[:, WGA + dc * 512 + it * 128:WGA + dc * 512 + (it + 1) * 128],
                        a2q_al[:, dc, tt * 512:(tt + 1) * 512], start=(dc == 0), stop=(dc == DC - 1)), inc=(dc == DC - 1))
                p.wait("scalar", cT, ng)
                nsg = p.sop(lambda e, psG=psG, sgb=sgb: e.activation(
                    sgb, psG[:, :], AF.Silu, bias=0.0, scale=sm[:, MG:MG + 1]))
                nsgs[pr] = nsg
                for dc in range(DC):
                    nu = p.top(lambda e, dc=dc, it=it, tt=tt, psU=psU: e.matmul(
                        psU[:, :], bfA[:, WUA + dc * 512 + it * 128:WUA + dc * 512 + (it + 1) * 128],
                        a2q_al[:, dc, tt * 512:(tt + 1) * 512], start=(dc == 0), stop=(dc == DC - 1)), inc=(dc == DC - 1))
                wcons = nu
                p.wait("vector", cT, max(nu, itcons[pr]))
                p.wait("vector", cS, nsg)
                guevs[pr] = p.vop(lambda e, psU=psU, sgb=sgb, itb=itb: e.scalar_tensor_tensor(
                    itb, psU[:, :], sm[:, MU:MU + 1], sgb, ALU.mult, ALU.mult))
                p.wait("tensor", cV, max(guevs[pr], fevac[pr]))
                ntf = p.top(lambda e, psFc=psFc, itb=itb: e.matmul(
                    psFc[:, :], sb_h128, itb, start=True, stop=True))
                itcons[pr] = ntf
                p.wait("vector", cT, ntf)
                p.wait("vector", dG, fst[ri])
                fevac[pr] = p.vop(lambda e, psFc=psFc, ri=ri: e.tensor_copy(
                    bfA[:, TR1 + ri * 512:TR1 + (ri + 1) * 512], psFc[:, :]))
                p.wait("gpsimd", cV, fevac[pr])
                fst[ri] = p.dma("gpsimd", t_Tsc[c * 128:(c + 1) * 128, tt * 512:(tt + 1) * 512],
                                bfA[:, TR1 + ri * 512:TR1 + (ri + 1) * 512])
    psA_.close()

    trcons = [0, 0]
    yev = 0
    p.wait("sync", dG, dG.n)
    for tt in range(NTT):
        psC_ = ExitStack()
        psY = psC_.enter_context(nc.psum_tensor(f"psY{tt}", [128, 512], F32)).ap()
        p.wait("sync", dG, dG.n)
        yv = bfA[:, YB:YB + 8192]
        y3 = yv.rearrange("p (a b) -> p a b", b=128)
        Tsc_r = t_Tsc.rearrange("(c s) t -> c s t", s=128)
        for tb in range(4):
            for g in range(16):
                gi = g % 2
                trh = sb_Tr[0:IC, gi * 1024:(gi + 1) * 1024]
                p.wait("sync", cT, trcons[gi])
                nld = p.dma("sync", trh.rearrange("c (s t) -> c s t", t=128),
                            Tsc_r[:, g * 8:(g + 1) * 8, tt * 512 + tb * 128:tt * 512 + (tb + 1) * 128])
                p.wait("tensor", dS, nld)
                p.wait("tensor", cV, yev)
                for s_ in range(8):
                    ntm = p.top(lambda e, s_=s_, gi=gi: e.matmul(
                        psY[:, s_ * 64:(s_ + 1) * 64],
                        sb_Tr[0:IC, gi * 1024 + s_ * 128:gi * 1024 + (s_ + 1) * 128],
                        sb_hic, start=True, stop=True), inc=(s_ == 7))
                trcons[gi] = ntm
                p.wait("vector", cT, ntm)
                yev = p.vop(lambda e, g=g: e.tensor_scalar_mul(
                    y3[:, :, g * 8:(g + 1) * 8],
                    psY.rearrange("p (s c) -> p c s", c=64), FWHT_NORM))
            p.wait("vector", cV, yev)
            p.vchain([
                lambda e: e.tensor_reduce(smc(MXC), yv, AX.X, ALU.max, apply_absolute_value=True),
                lambda e: e.tensor_scalar_max(smc(MXC), smc(MXC), EPS),
                lambda e: e.reciprocal(smc(TPA), smc(MXC)),
                lambda e: e.tensor_scalar_mul(smc(SCL), smc(TPA), 128.0),
                lambda e: e.tensor_scalar_mul(smc(SINV), smc(MXC), 1.0 / 128.0),
                lambda e: e.tensor_scalar(smc(NPS), smc(SCL), smc(NP1), None, ALU.mult),
            ])
            for qk in range(4):
                ysl = yv[:, qk * 2048:(qk + 1) * 2048]
                p.vop(lambda e, ysl=ysl: e.tensor_scalar(x1sl, ysl, smc(SCL), C23, ALU.mult, ALU.add), inc=False)
                p.vop(lambda e, ysl=ysl: e.tensor_scalar(chn, ysl, smc(SCL), C52, ALU.mult, ALU.add), inc=False)
                p.vop(lambda e: e.scalar_tensor_tensor(chn, chn, C52, x1sl, ALU.subtract, ALU.add), inc=False)
                p.vop(lambda e, ysl=ysl: e.scalar_tensor_tensor(x1sl, ysl, smc(NPS), chn, ALU.mult, ALU.add), inc=False)
                p.vop(lambda e: e.tensor_scalar(x1sl, x1sl, C23, 127.0, ALU.subtract, ALU.min), inc=False)
                nv = p.vop(lambda e, ysl=ysl: e.tensor_scalar(ysl, x1sl, smc(SINV), None, ALU.mult))
            p.wait("gpsimd", cV, nv)
            nst = p.dma("gpsimd", t_a3[tb * 128:(tb + 1) * 128, :], yv)
            p.wait("vector", dG, nst)
        psC_.close()

        psD_ = ExitStack()
        psDn = [psD_.enter_context(nc.psum_tensor(f"psD{tt}_{i}", [128, 512], F32)).ap() for i in range(4)]
        p.wait("sync", dG, dG.n)
        p.wait("sync", cT, cT.n)
        a3sl = lambda c: bfA[:, INTER + c * 512:INTER + (c + 1) * 512]
        for c in range(IC):
            p.dma("sync", a3sl(c), t_a3[:, c * 128:(c + 1) * 128], transpose=True)
        wdbB = [bfA[:, 12288:16384], bfA[:, 16384:20480]]
        dcons = [0, 0]
        dnev = cV.n
        for ob in range(4):
            p.wait("tensor", cV, dnev)
            for g8 in range(8):
                bi = g8 % 2
                p.wait("sync", cT, dcons[bi])
                nld = p.dma("sync", wdbB[bi].rearrange("p (c k) -> p c k", k=512),
                            t_wdqT.rearrange("(c p) d -> p c d", p=128)[:, g8 * 8:(g8 + 1) * 8, ob * 512:(ob + 1) * 512])
                p.wait("tensor", dS, nld)
                if g8 == 0:
                    p.wait("tensor", cV, dnev)
                for i8 in range(8):
                    ic = g8 * 8 + i8
                    for ts_ in range(4):
                        dcons[bi] = p.top(lambda e, ts_=ts_, bi=bi, ic=ic, i8=i8: e.matmul(
                            psDn[ts_][:, :], a3sl(ic)[:, ts_ * 128:(ts_ + 1) * 128],
                            wdbB[bi][:, i8 * 512:(i8 + 1) * 512],
                            start=(ic == 0), stop=(ic == IC - 1), skip_group_check=True),
                            inc=(ts_ == 3 and i8 == 7))
            p.wait("vector", cT, max(dcons))
            for ts_ in range(4):
                dnev = p.vop(lambda e, ts_=ts_, ob=ob: e.tensor_scalar(
                    rows(ts_)[:, ob * 512:(ob + 1) * 512], psDn[ts_][:, :],
                    sm[:, THC:THC + 1], None, ALU.mult))
        p.wait("tensor", cV, dnev)
        psD_.close()

        for ts_ in range(4):
            jj = tt * 4 + ts_
            ln_stats(rows(ts_), D)
            p.wait("sync", dG, dG.n)
            nxl = p.dma("sync", x1sl, t_x1[jj * 128:(jj + 1) * 128, :])
            nv = p.vop(lambda e, ts_=ts_: e.tensor_scalar(
                chn, rows(ts_), smc(NEGM), smc(RSTD), ALU.add, ALU.mult))
            pbc = mkprobe(f"chn{jj}", (128, 2048))
            if pbc is not None:
                p.wait("gpsimd", cV, nv)
                p.dma("gpsimd", pbc, chn)
            p.wait("vector", dS, nxl)
            nv = p.vop(lambda e: e.tensor_tensor(x1sl, x1sl, chn, ALU.add))
            p.wait("gpsimd", cV, nv)
            p.dma("gpsimd", t_out[jj * 128:(jj + 1) * 128, :], x1sl)
            p.wait("vector", dG, dG.n)

    return _finish(nc, ctx, p, probes)


def _finish_stub(nc, ctx, p, t_out, fA, probes):
    nv = p.vop(lambda e: e.memset(fA[:, 0:2048], 0.0))
    p.wait("gpsimd", p.cV, nv)
    for j in range(NTO):
        p.dma("gpsimd", t_out[j * 128:(j + 1) * 128, :], fA[:, 0:2048])
    return _finish(nc, ctx, p, probes)


def _finish(nc, ctx, p, probes):
    p.wait("gpsimd", p.dG, p.dG.n)
    p.wait("gpsimd", p.dS, p.dS.n)
    blk = ctx.enter_context(nc.Block())
    for eng in ENGS:
        ops = p.ops[eng]
        getattr(blk, eng)(lambda e, ops=ops: [f(e) for f in ops])
    ctx.close()
    return nc, probes


def _stage(inputs):
    x = np.ascontiguousarray(np.asarray(inputs["L0_attn_x"], dtype=np.float32))
    perm = _qkv_perm_cached()
    wqkv = np.ascontiguousarray(np.asarray(inputs["L0_attn_w_qkv"], np.float32)[:, perm])
    bqkv = np.ascontiguousarray(np.asarray(inputs["L0_attn_b_qkv"], np.float32)[perm][None, :])
    wo = np.ascontiguousarray(np.asarray(inputs["L0_attn_w_o"], np.float32))
    bo = np.ascontiguousarray(np.asarray(inputs["L0_attn_b_o"], np.float32)[None, :])
    wg = np.ascontiguousarray(np.asarray(inputs["L0_ffn_w_gate"], np.float32))
    wu = np.ascontiguousarray(np.asarray(inputs["L0_ffn_w_up"], np.float32))
    wd = np.ascontiguousarray(np.asarray(inputs["L0_ffn_w_down"], np.float32))

    ident = np.eye(128, dtype=BFD)
    h128 = _had(128).astype(BFD)
    hic = _had(IC).astype(BFD)
    pp = np.arange(128)[:, None]
    fl = np.arange(128)[None, :]
    tri = np.where(pp > fl, np.float32(NEG_BIG), np.float32(0.0)).astype(BFD)
    invf32 = (1.0 / (10000.0 ** (np.arange(0, 128, 2).astype(np.float32) / 128))).astype(np.float32)

    maps = []
    for c in range(8):
        b, half = divmod(c, 2)
        if half == 0:
            xp = x[b]
            pos = np.arange(S, dtype=np.float32)
            kb = np.concatenate([np.zeros(TO, np.float32), np.full(TA - TO, NEG_BIG, np.float32)])
        else:
            xp = np.concatenate([x[b, TO:], x[b, :TO]], axis=0)
            pos = np.concatenate([np.arange(TO, TA), np.arange(0, TO)]).astype(np.float32)
            kb = np.zeros(TA, np.float32)
        ang = (pos.astype(np.float32)[:, None] * invf32[None, :]).astype(np.float32)
        sint = np.sin(ang).astype(np.float32)
        cost = np.cos(ang).astype(np.float32)
        sint_d = np.ascontiguousarray(sint.reshape(NTA, 128, 64).transpose(1, 0, 2).reshape(128, NTA * 64))
        cost_d = np.ascontiguousarray(cost.reshape(NTA, 128, 64).transpose(1, 0, 2).reshape(128, NTA * 64))
        maps.append({
            "x": np.ascontiguousarray(xp),
            "pos": np.ascontiguousarray(pos.reshape(NTA, 128).T),
            "kbias": np.ascontiguousarray(kb.reshape(NTA, 128).T),
            "wqkv": wqkv, "bqkv": bqkv, "wo": wo, "bo": bo,
            "wg": wg, "wu": wu, "wd": wd,
            "ident": ident, "h128": h128, "hic": hic, "tri": tri,
            "sint": sint_d, "cost": cost_d,
        })
    return maps


_QKV_PERM_C = None


def _qkv_perm_cached():
    global _QKV_PERM_C
    if _QKV_PERM_C is None:
        perm = np.zeros(3 * D, dtype=np.int64)
        j = np.arange(128)
        for h in range(H):
            perm[h * 128 + j] = h * 384 + j
            perm[D + h * 128 + j] = h * 384 + 128 + j
            perm[2 * D + h * 128 + j] = h * 384 + 256 + j
        _QKV_PERM_C = perm
    return _QKV_PERM_C


_BUILT = None


def kernel(**inputs):
    global _BUILT
    if _BUILT is None:
        _BUILT = build_program()
    nc, probes = _BUILT
    maps = _stage(inputs)
    res = run_bass_kernel_spmd(nc, maps, core_ids=list(range(8)))
    if PROBES:
        kernel.last_results = res.results
    out = np.zeros((B, S, D), np.float32)
    for c in range(8):
        b, half = divmod(c, 2)
        out[b, half * TO:(half + 1) * TO] = res.results[c]["out"]
    return out
